# version 07
# baseline (speedup 1.0000x reference)
import numpy as np
import ml_dtypes
from contextlib import ExitStack

import concourse.bass as bass
import concourse.mybir as mybir
from concourse.bass_utils import run_bass_kernel_spmd

F32 = mybir.dt.float32
BF16 = mybir.dt.bfloat16
I32 = mybir.dt.int32
AF = mybir.ActivationFunctionType
ALU = mybir.AluOpType
AX = mybir.AxisListType.X

NCORE = 8
TCORE = 2048
ROWS = 2176
DM = 4096
ND = DM // 128
WQK = 1024
NW = 8
H3 = 3072
N3 = 24
HID = 11008
NH = 86
NHALF = 1024
NQ = 512

HG = [4] * 21 + [2]
HGOFF = [0]
for _s in HG:
    HGOFF.append(HGOFF[-1] + _s)
W2CH = [(0, 16), (16, 16), (32, 16), (48, 16), (64, 16), (80, 6)]

DBG = False
PH_LN = True
PH_DQ = True


class Ctr:

    def __init__(self, nc, name):
        self.h = nc.alloc_semaphore(name)
        self.n = 0

    def inc(self, k=1):
        self.n += k
        return self.h


class Slots:

    def __init__(self, nc, name, n):
        self.sems = [Ctr(nc, f"{name}{i}") for i in range(n)]
        self.n = n
        self.tgt = []

    def load(self, q, out_ap, in_ap, **kw):
        i = len(self.tgt)
        s = self.sems[i % self.n]
        q.dma_start(out_ap, in_ap, **kw).then_inc(s.inc(16), 16)
        self.tgt.append((s.h, s.n))
        return i

    def wait(self, eng, i):
        h, v = self.tgt[i]
        eng.wait_ge(h, v)


def build(nc, upto="all"):
    x = nc.dram_tensor("x", [ROWS, DM], F32, kind="ExternalInput")
    convw = nc.dram_tensor("convw", [3, DM, H3], F32, kind="ExternalInput")
    convb = nc.dram_tensor("convb", [1, H3], F32, kind="ExternalInput")
    wq = nc.dram_tensor("wq", [WQK, WQK], F32, kind="ExternalInput")
    wk = nc.dram_tensor("wk", [WQK, WQK], F32, kind="ExternalInput")
    wv = nc.dram_tensor("wv", [WQK, WQK], F32, kind="ExternalInput")
    bq = nc.dram_tensor("bq", [1, WQK], F32, kind="ExternalInput")
    bk = nc.dram_tensor("bk", [1, WQK], F32, kind="ExternalInput")
    bv = nc.dram_tensor("bv", [1, WQK], F32, kind="ExternalInput")
    wo = nc.dram_tensor("wo", [WQK, WQK], F32, kind="ExternalInput")
    bo = nc.dram_tensor("bo", [1, WQK], F32, kind="ExternalInput")
    pw = nc.dram_tensor("pw", [WQK, DM], F32, kind="ExternalInput")
    pb = nc.dram_tensor("pb", [1, DM], F32, kind="ExternalInput")
    w1 = nc.dram_tensor("w1", [352256, 64], I32, kind="ExternalInput")
    w2 = nc.dram_tensor("w2", [352256, 64], I32, kind="ExternalInput")
    w3 = nc.dram_tensor("w3", [352256, 64], I32, kind="ExternalInput")
    s1 = nc.dram_tensor("s1", [1, 704512], F32, kind="ExternalInput")
    s2 = nc.dram_tensor("s2", [1, 704512], F32, kind="ExternalInput")
    s3 = nc.dram_tensor("s3", [1, 704512], F32, kind="ExternalInput")
    identin = nc.dram_tensor("identin", [128, 128], BF16, kind="ExternalInput")
    maskin = nc.dram_tensor("maskin", [128, 512], F32, kind="ExternalInput")
    out = nc.dram_tensor("out", [TCORE, DM], F32, kind="ExternalOutput")

    scratch_kind = "ExternalOutput" if DBG else "Internal"
    xnsc = nc.dram_tensor("xnsc", [ROWS, DM], BF16, kind=scratch_kind)
    w1d = nc.dram_tensor("w1d", [HID, DM], BF16, kind=scratch_kind)
    w3d = nc.dram_tensor("w3d", [HID, DM], BF16, kind=scratch_kind)
    w2d = nc.dram_tensor("w2d", [DM, HID], BF16, kind=scratch_kind)
    accs = nc.dram_tensor("accs", [DM, TCORE], F32, kind=scratch_kind)
    asms = nc.dram_tensor("asms", [DM, TCORE], BF16, kind=scratch_kind)
    hsc = nc.dram_tensor("hsc", [HID, 512], BF16, kind=scratch_kind)

    sy = nc.sync
    sc = nc.scalar
    gp = nc.gpsimd
    dve = nc.vector
    act = nc.scalar
    pool = nc.gpsimd
    te = nc.tensor

    _uid = [0]

    def sbt(name, shape, dtype):
        _uid[0] += 1
        return nc.sbuf_tensor(f"{name}_{_uid[0]}", shape, dtype)

    def pst(name, shape, dtype):
        _uid[0] += 1
        return nc.psum_tensor(f"{name}_{_uid[0]}", shape, dtype)

    allsems = []

    def C(name):
        c = Ctr(nc, name)
        allsems.append(c)
        return c

    def SL(name, n):
        s = Slots(nc, name, n)
        allsems.extend(s.sems)
        return s

    ini = SL("ini", 1)
    xls = SL("xls", 2)
    lnv = C("lnv")
    lna = C("lna")
    lnq = C("lnq")
    lnw = SL("lnw", 2)
    pk_v = SL("pkv", 4)
    pk_p = SL("pkp", 4)
    dqv = C("dqv")
    dqp = C("dqp")
    dwr_v = SL("dwrv", 2)
    dwr_p = SL("dwrp", 2)
    xnb = SL("xnb", 1)
    cwsl = SL("cws", 2)
    cbsl = SL("cbs", 1)
    ccp = C("ccp")
    cce = C("cce")
    qwsl = SL("qws", 2)
    qkp = C("qkp")
    qke = C("qke")
    sSf = C("sSf")
    sA = C("sA"); sB = C("sB"); sCx = C("sCx"); sD = C("sD")
    sE = C("sE"); sF = C("sF"); sG = C("sG"); sH = C("sH")
    sR = C("sR")
    wop = C("wop"); woe = C("woe")
    pjp = C("pjp"); pje = C("pje")
    aws = SL("aws", 4)
    fws = SL("fws", 8)
    f1c = C("f1c")
    f2c = C("f2c")
    sSil = C("sSil"); hmul = C("hmul")
    sF2d = C("sF2d")
    accl = SL("accl", 4)
    asws = SL("asws", 4)
    hsw = SL("hsw", 2)
    hrl = SL("hrl", 8)
    abls = SL("abls", 2)
    axls = SL("axls", 2)
    stws = SL("stws", 2)

    est = ExitStack()
    ident = est.enter_context(sbt("ident", [128, 128], BF16))
    mask4 = est.enter_context(sbt("mask4", [128, 512], F32))
    ones = est.enter_context(sbt("ones", [1, 1024], BF16))

    for c in allsems:
        pool.sem_clear(c.h)
    nc.all_engine_barrier()
    nc.all_engine_barrier()

    ini.load(sy, ident[:, :], identin.ap())
    ini.load(sy, mask4[:, :], maskin.ap())
    dve.memset(ones[:, :], 1.0)

    with ExitStack() as ph:
        xbuf = [ph.enter_context(sbt(f"xbuf{i}", [128, DM], F32)) for i in range(2)]
        xsq = ph.enter_context(sbt("xsq", [128, DM], F32))
        xnt = [ph.enter_context(sbt(f"xnt{i}", [128, DM], BF16)) for i in range(2)]
        st8 = ph.enter_context(sbt("st8", [128, 8], F32))

        lna_copy = []
        lnv_sq = []
        for g in range(17):
            s = g % 2
            if g >= 2:
                sc.wait_ge(lna.h, lna_copy[g - 2])
            li = xls.load(sc, xbuf[s][:, :], x.ap()[128 * g:128 * (g + 1), :])
            xls.wait(dve, li)
            dve.reduce_sum(st8[:, 0:1], xbuf[s][:, :], axis=AX).then_inc(lnv.inc(), 1)
            dve.wait_ge(lnv.h, lnv.n)
            xls.wait(act, li)
            if g >= 1:
                act.wait_ge(lnv.h, lnv_sq[g - 1])
            act.activation(xsq[:, :], xbuf[s][:, :], AF.Square).then_inc(lnq.inc(), 1)
            dve.wait_ge(lnq.h, lnq.n)
            dve.reduce_sum(st8[:, 1:2], xsq[:, :], axis=AX).then_inc(lnv.inc(), 1)
            lnv_sq.append(lnv.n)
            dve.wait_ge(lnv.h, lnv.n)
            dve.tensor_single_scalar(out=st8[:, 2:3], in_=st8[:, 0:1],
                                     scalar=1.0 / DM, op=ALU.mult).then_inc(lnv.inc(), 1)
            dve.tensor_single_scalar(out=st8[:, 3:4], in_=st8[:, 1:2],
                                     scalar=1.0 / DM, op=ALU.mult).then_inc(lnv.inc(), 1)
            dve.wait_ge(lnv.h, lnv.n)
            dve.tensor_tensor(out=st8[:, 4:5], in0=st8[:, 2:3], in1=st8[:, 2:3],
                              op=ALU.mult).then_inc(lnv.inc(), 1)
            dve.wait_ge(lnv.h, lnv.n)
            dve.scalar_tensor_tensor(out=st8[:, 5:6], in0=st8[:, 4:5], scalar=-1.0,
                                     in1=st8[:, 3:4], op0=ALU.mult, op1=ALU.add
                                     ).then_inc(lnv.inc(), 1)
            dve.wait_ge(lnv.h, lnv.n)
            dve.tensor_single_scalar(out=st8[:, 5:6], in_=st8[:, 5:6],
                                     scalar=1e-5, op=ALU.add).then_inc(lnv.inc(), 1)
            act.wait_ge(lnv.h, lnv.n)
            act.activation(st8[:, 6:7], st8[:, 5:6], AF.Ln).then_inc(lna.inc(), 1)
            act.wait_ge(lna.h, lna.n)
            act.activation(st8[:, 6:7], st8[:, 6:7], AF.Exp, scale=-0.5
                           ).then_inc(lna.inc(), 1)
            dve.wait_ge(lna.h, lna.n)
            dve.scalar_tensor_tensor(out=st8[:, 7:8], in0=st8[:, 2:3], scalar=-1.0,
                                     in1=st8[:, 6:7], op0=ALU.mult, op1=ALU.mult
                                     ).then_inc(lnv.inc(), 1)
            act.wait_ge(lnv.h, lnv.n)
            if g >= 2:
                _, v = lnw.tgt[g - 2]
                act.wait_ge(lnw.tgt[g - 2][0], v)
            act.activation(xnt[s][:, :], xbuf[s][:, :], AF.Identity,
                           bias=st8[:, 7:8], scale=st8[:, 6:7]).then_inc(lna.inc(), 1)
            lna_copy.append(lna.n)
            gp.wait_ge(lna.h, lna.n)
            lnw.load(gp, xnsc.ap()[128 * g:128 * (g + 1), :], xnt[s][:, :])
    if upto == "ln":
        lnw.wait(gp, 15)
        lnw.wait(gp, 16)
        ini.wait(sy, 1)
        nc.all_engine_barrier()
        return nc

    units = []
    for kt in range(NH):
        for which in (0, 1):
            for dh in (0, 1):
                units.append(("w13", (kt, which, dh)))
    for dt in range(ND):
        for ci in range(len(W2CH)):
            units.append(("w2", (dt, ci)))
    NU = len(units)

    def uid_w(kt, which, dh):
        return kt * 4 + which * 2 + dh

    def uid_2(dt, ci):
        return NH * 4 + dt * len(W2CH) + ci

    def uparams(u):
        kind, idx = u
        if kind == "w13":
            kt, which, dh = idx
            return dict(src=(w1 if which == 0 else w3),
                        ssrc=(s1 if which == 0 else s3),
                        dst=(w1d if which == 0 else w3d),
                        pkoff=kt * 128 * 2048 + dh * 1024, pkw=1024,
                        ppitch=2048, mcnt=16,
                        soff=kt * 128 * 64 + 32 * dh, spp=64, scnt=32,
                        dstoff=kt * 128 * DM + dh * 2048, dpitch=DM, dw=2048)
        dt, ci = idx
        m0, mc = W2CH[ci]
        return dict(src=w2, ssrc=s2, dst=w2d,
                    pkoff=(dt * 128 * 86 + m0) * 64, pkw=mc * 64, ppitch=86 * 64,
                    mcnt=mc, soff=dt * 128 * 172 + 2 * m0, spp=172, scnt=2 * mc,
                    dstoff=dt * 128 * HID + m0 * 128, dpitch=HID, dw=mc * 128)

    def wtgt(u):
        return (dwr_v.sems[u % 2].h, 16 * (u // 2 + 1))

    pkb = [est.enter_context(sbt(f"pkb{i}", [128, 1024], I32)) for i in range(2)]
    scb = [est.enter_context(sbt(f"scqb{i}", [128, 32], F32)) for i in range(2)]
    lsbt = est.enter_context(sbt("lsbt", [128, 1024], I32))
    outb = [est.enter_context(sbt(f"dqo{i}", [128, 2048], BF16)) for i in range(2)]

    dqst = {"load": 0, "comp": 0}

    def dq_load1():
        j = dqst["load"]
        if j >= NU:
            return
        if j == 0:
            for gi_ in (15, 16):
                gp.wait_ge(lnw.tgt[gi_][0], lnw.tgt[gi_][1])
        u = uparams(units[j])
        s = j % 2
        if j >= 2:
            gp.wait_ge(dqv.h, 4 * (j - 1))
        pk_v.load(gp, bass.AP(pkb[s], 0, [[1024, 128], [1, u["pkw"]]]),
                  bass.AP(u["src"], u["pkoff"], [[u["ppitch"], 128], [1, u["pkw"]]]))
        pk_v.load(gp, bass.AP(scb[s], 0, [[32, 128], [1, u["scnt"]]]),
                  bass.AP(u["ssrc"], u["soff"], [[u["spp"], 128], [1, u["scnt"]]]))
        dqst["load"] = j + 1

    def dqfeed(n):
        for _ in range(n):
            j = dqst["comp"]
            if j >= NU:
                return
            while dqst["load"] < min(j + 2, NU):
                dq_load1()
            u = uparams(units[j])
            s = j % 2
            ob = outb[s]
            mc = u["mcnt"]
            if j == 0:
                for gi_ in (15, 16):
                    dve.wait_ge(lnw.tgt[gi_][0], lnw.tgt[gi_][1])
            pk_v.wait(dve, 2 * j)
            pk_v.wait(dve, 2 * j + 1)
            if j >= 1:
                dve.wait_ge(dqv.h, dqv.n)
            if j >= 2:
                dwr_v.wait(dve, j - 2)
            dve.tensor_single_scalar(out=lsbt[:, 0:u["pkw"]], in_=pkb[s][:, 0:u["pkw"]],
                                     scalar=4, op=ALU.arith_shift_right
                                     ).then_inc(dqv.inc(), 1)
            dve.wait_ge(dqv.h, dqv.n)
            dve.tensor_tensor(
                out=bass.AP(ob, 0, [[2048, 128], [128, mc], [1, 64]]),
                in0=bass.AP(lsbt, 0, [[1024, 128], [64, mc], [1, 64]]),
                in1=bass.AP(scb[s], 0, [[32, 128], [2, mc], [0, 64]]),
                op=ALU.mult).then_inc(dqv.inc(), 1)
            dve.wait_ge(dqv.h, dqv.n)
            dve.tensor_scalar(out=lsbt[:, 0:u["pkw"]], in0=pkb[s][:, 0:u["pkw"]],
                              scalar1=15, scalar2=8, op0=ALU.bitwise_and,
                              op1=ALU.bitwise_xor).then_inc(dqv.inc(), 1)
            dve.wait_ge(dqv.h, dqv.n)
            dve.scalar_tensor_tensor(
                out=bass.AP(ob, 64, [[2048, 128], [128, mc], [1, 64]]),
                in0=bass.AP(lsbt, 0, [[1024, 128], [64, mc], [1, 64]]),
                scalar=8.0,
                in1=bass.AP(scb[s], 1, [[32, 128], [2, mc], [0, 64]]),
                op0=ALU.subtract, op1=ALU.mult).then_inc(dqv.inc(), 1)
            sc.wait_ge(dqv.h, dqv.n)
            wl = dwr_v.load(sc, bass.AP(u["dst"], u["dstoff"],
                                        [[u["dpitch"], 128], [1, u["dw"]]]),
                            bass.AP(ob, 0, [[2048, 128], [1, u["dw"]]]))
            assert dwr_v.tgt[wl] == wtgt(j), (dwr_v.tgt[wl], wtgt(j))
            dqst["comp"] = j + 1
            while dqst["load"] < min(j + 3, NU):
                dq_load1()

    if upto == "dq":
        dqfeed(NU)
        dwr_v.wait(sc, len(dwr_v.tgt) - 1)
        dwr_v.wait(sc, len(dwr_v.tgt) - 2)
        nc.all_engine_barrier()
        return nc

    for half in range(2):
        t0 = half * NHALF
        with ExitStack() as hs:
            if half == 0:
                for gi_ in (15, 16):
                    act.wait_ge(lnw.tgt[gi_][0], lnw.tgt[gi_][1])
            with ExitStack() as atts:
                qhT = atts.enter_context(sbt("qhT", [128, NW * 1024], BF16))
                khT = atts.enter_context(sbt("khT", [128, NW * 1024], BF16))
                vh = atts.enter_context(sbt("vh", [128, 8 * 1024], BF16))

                with ExitStack() as cs:
                    qkvT = cs.enter_context(sbt("qkvT", [128, N3 * 1024], BF16))
                    cv = cs.enter_context(ExitStack())
                    xnT = cv.enter_context(sbt("xnTc", [128, ND * 1040], BF16))
                    cwb = [cv.enter_context(sbt(f"cwb{i}", [128, 32 * 128], BF16))
                           for i in range(2)]
                    cbsb = cv.enter_context(sbt("cbsb", [1, H3], BF16))
                    cps = cv.enter_context(pst("cps", [128, 4 * 512], F32))
                    glast = (t0 + 1040 - 1) // 128
                    for gi_ in (glast - 1, glast):
                        sy.wait_ge(lnw.tgt[gi_][0], lnw.tgt[gi_][1])
                    ini.wait(sy, 1)
                    for k in range(ND):
                        xnb.load(sy, bass.AP(xnT, k * 1040, [[ND * 1040, 128], [1, 1040]]),
                                 xnsc.ap()[t0:t0 + 1040, 128 * k:128 * (k + 1)],
                                 transpose=True)
                    xnb_done = xnb.tgt[-1]
                    cbsl.load(gp, cbsb[:, :], convb.ap())
                    ccp_b = ccp.n
                    cce_b = cce.n

                    def cload(c):
                        b, k3 = divmod(c, 3)
                        if c >= 2:
                            gp.wait_ge(ccp.h, ccp_b + c - 1)
                        return cwsl.load(
                            gp,
                            bass.AP(cwb[(len(cwsl.tgt)) % 2], 0,
                                    [[32 * 128, 128], [128, 32], [1, 128]]),
                            bass.AP(convw, k3 * DM * H3 + 128 * b,
                                    [[H3, 128], [128 * H3, 32], [1, 128]]))

                    cw_base = len(cwsl.tgt)
                    cload(0)
                    cload(1)
                    te.wait_ge(xnb_done[0], xnb_done[1])
                    cbsl.wait(te, len(cbsl.tgt) - 1)
                    for b in range(N3):
                        bp = b % 2
                        if b >= 2:
                            te.wait_ge(cce.h, cce_b + b - 1)
                        for k3 in range(3):
                            c = b * 3 + k3
                            if c + 2 < N3 * 3:
                                cload(c + 2)
                            cwsl.wait(te, cw_base + c)
                            sbuf_w = cwb[(cw_base + c) % 2]
                            if k3 == 0:
                                for nh in range(2):
                                    te.matmul(cps[:, (2 * bp + nh) * 512:(2 * bp + nh + 1) * 512],
                                              cbsb[:, 128 * b:128 * (b + 1)],
                                              ones[:, 0:512], start=True, stop=False)
                            for a in range(ND):
                                for nh in range(2):
                                    mm = te.matmul(
                                        cps[:, (2 * bp + nh) * 512:(2 * bp + nh + 1) * 512],
                                        sbuf_w[:, a * 128:(a + 1) * 128],
                                        xnT[:, a * 1040 + k3 + nh * 512:
                                            a * 1040 + k3 + nh * 512 + 512],
                                        start=False, stop=(k3 == 2 and a == ND - 1))
                            mm.then_inc(ccp.inc(), 1)
                        act.wait_ge(ccp.h, ccp_b + 3 * (b + 1))
                        act.activation(qkvT[:, b * 1024:(b + 1) * 1024],
                                       cps[:, 2 * bp * 512:(2 * bp + 2) * 512],
                                       AF.Copy).then_inc(cce.inc(), 1)
                        dqfeed(2)
                    nc.all_engine_barrier()
                    cv.close()

                    with ExitStack() as qs:
                        qwb = [qs.enter_context(sbt(f"qwb{i}", [128, 8 * 1024], BF16))
                               for i in range(2)]
                        bql = qs.enter_context(sbt("bql", [1, 3 * WQK], BF16))
                        qps = qs.enter_context(pst("qps", [128, 4 * 512], F32))
                        cbsl.load(gp, bql[:, 0:WQK], bq.ap())
                        cbsl.load(gp, bql[:, WQK:2 * WQK], bk.ap())
                        cbsl.load(gp, bql[:, 2 * WQK:3 * WQK], bv.ap())
                        qkp_b = qkp.n
                        qke_b = qke.n
                        qw_base = len(qwsl.tgt)

                        def qwload(i, wt):
                            if i >= 2:
                                gp.wait_ge(qkp.h, qkp_b + 8 * (i - 1))
                            qwsl.load(gp,
                                      bass.AP(qwb[(qw_base + i) % 2], 0,
                                              [[8 * 1024, 128], [1024, 8], [1, 1024]]),
                                      bass.AP(wt, 0, [[WQK, 128], [128 * WQK, 8], [1, WQK]]))

                        qwload(0, wq)
                        qwload(1, wk)
                        cbsl.wait(te, len(cbsl.tgt) - 1)
                        for i, (srcoff, dstt, scl) in enumerate(
                                [(0, qhT, 0.125), (8, khT, 1.0), (16, vh, 1.0)]):
                            if i == 2:
                                qwload(2, wv)
                            qwsl.wait(te, qw_base + i)
                            wb = qwb[(qw_base + i) % 2]
                            for m in range(8):
                                bank = (m % 2) * 2
                                if i * 8 + m >= 2:
                                    te.wait_ge(qke.h, qke_b + i * 8 + m - 1)
                                for nh in range(2):
                                    pslice = qps[:, (bank + nh) * 512:(bank + nh + 1) * 512]
                                    if i < 2:
                                        te.matmul(pslice,
                                                  bql[:, i * WQK + 128 * m:i * WQK + 128 * (m + 1)],
                                                  ones[:, 0:512], start=True, stop=False)
                                        for a in range(8):
                                            mm = te.matmul(
                                                pslice,
                                                wb[:, a * 1024 + 128 * m:a * 1024 + 128 * (m + 1)],
                                                qkvT[:, (srcoff + a) * 1024 + nh * 512:
                                                     (srcoff + a) * 1024 + nh * 512 + 512],
                                                start=False, stop=(a == 7))
                                    else:
                                        te.matmul(pslice, ones[:, 0:128],
                                                  bql[:, 2 * WQK + nh * 512:2 * WQK + (nh + 1) * 512],
                                                  start=True, stop=False)
                                        for a in range(8):
                                            mm = te.matmul(
                                                pslice,
                                                qkvT[:, (16 + a) * 1024 + m * 128:
                                                     (16 + a) * 1024 + (m + 1) * 128],
                                                wb[:, a * 1024 + nh * 512:a * 1024 + (nh + 1) * 512],
                                                start=False, stop=(a == 7))
                                mm.then_inc(qkp.inc(), 1)
                                act.wait_ge(qkp.h, qkp.n)
                                act.activation(dstt[:, m * 1024:(m + 1) * 1024],
                                               qps[:, bank * 512:(bank + 2) * 512],
                                               AF.Copy, scale=scl).then_inc(qke.inc(), 1)
                                dqfeed(2)
                        nc.all_engine_barrier()

                with ExitStack() as bs:
                    oT = bs.enter_context(sbt("oT", [128, 8 * 1024], BF16))
                    aoT = bs.enter_context(sbt("aoT", [128, 8 * 1024], BF16))
                    smb = [bs.enter_context(sbt(f"smb{i}", [128, 512], F32)) for i in range(2)]
                    pbf = [bs.enter_context(sbt(f"pbf{i}", [128, 512], BF16)) for i in range(2)]
                    pnb = [bs.enter_context(sbt(f"pnb{i}", [128, 512], BF16)) for i in range(2)]
                    pTb = [bs.enter_context(sbt(f"pTb{i}", [128, 512], BF16)) for i in range(2)]
                    nmb = bs.enter_context(sbt("nmb", [128, 16], F32))
                    Sps = [bs.enter_context(pst(f"Sps{i}", [128, 512], F32)) for i in range(2)]
                    Tps = [bs.enter_context(pst(f"Tps{i}", [128, 512], BF16)) for i in range(2)]
                    Ops = [bs.enter_context(pst(f"Ops{i}", [128, 1024], F32)) for i in range(2)]
                    sA_b = sA.n; sB_b = sB.n; sC_b = sCx.n; sD_b = sD.n
                    sE_b = sE.n; sF_b = sF.n; sG_b = sG.n; sH_b = sH.n; sSf_b = sSf.n
                    for b in range(8):
                        for g in range(4):
                            bg = b * 4 + g
                            par = bg % 2
                            if bg >= 2:
                                te.wait_ge(sSf.h, sSf_b + bg - 1)
                            sa_tgt = [0, 0]
                            for idx, hh in enumerate((0, 2, 1, 3)):
                                h = 4 * g + hh
                                i = h // 2
                                pb_ = 64 * (h % 2)
                                mm = te.matmul(
                                    Sps[par][:, hh * 128:(hh + 1) * 128],
                                    qhT[pb_:pb_ + 64, i * 1024 + 128 * b:i * 1024 + 128 * (b + 1)],
                                    khT[pb_:pb_ + 64, i * 1024 + 128 * b:i * 1024 + 128 * (b + 1)],
                                    start=True, stop=True)
                                if idx == 1:
                                    mm.then_inc(sA.inc(), 1)
                                    te.wait_ge(sA.h, sA.n)
                            mm.then_inc(sA.inc(), 1)
                            sa_grp = sA.n
                            dve.wait_ge(sA.h, sa_grp)
                            if bg >= 2:
                                dve.wait_ge(sCx.h, sC_b + bg - 1)
                            dve.tensor_tensor(out=smb[par][:, :], in0=Sps[par][:, :],
                                              in1=mask4[:, :], op=ALU.add
                                              ).then_inc(sSf.inc(), 1)
                            dve.wait_ge(sSf.h, sSf.n)
                            dve.reduce_max(bass.AP(nmb, 4 * par, [[16, 128], [1, 4]]),
                                           bass.AP(smb[par], 0, [[512, 128], [128, 4], [1, 128]]),
                                           axis=AX, negate=True).then_inc(sB.inc(), 1)
                            dve.wait_ge(sB.h, sB.n)
                            dve.tensor_tensor(
                                out=smb[par][:, :], in0=smb[par][:, :],
                                in1=bass.AP(nmb, 4 * par, [[16, 128], [1, 4], [0, 128]]),
                                op=ALU.add).then_inc(sB.inc(), 1)
                            act.wait_ge(sB.h, sB_b + 2 * bg + 2)
                            if bg >= 2:
                                act.wait_ge(sD.h, sD_b + 2 * (bg - 2) + 2)
                            act.activation(pbf[par][:, :], smb[par][:, :], AF.Exp
                                           ).then_inc(sCx.inc(), 1)
                            dve.wait_ge(sCx.h, sC_b + bg + 1)
                            dve.reduce_sum(bass.AP(nmb, 8 + 4 * par, [[16, 128], [1, 4]]),
                                           bass.AP(pbf[par], 0, [[512, 128], [128, 4], [1, 128]]),
                                           axis=AX).then_inc(sD.inc(), 1)
                            dve.wait_ge(sD.h, sD.n)
                            dve.reciprocal(bass.AP(nmb, 8 + 4 * par, [[16, 128], [1, 4]]),
                                           bass.AP(nmb, 8 + 4 * par, [[16, 128], [1, 4]])
                                           ).then_inc(sR.inc(), 1)
                            dve.wait_ge(sR.h, sR.n)
                            if bg >= 2:
                                dve.wait_ge(sE.h, sE_b + bg - 1)
                            dve.tensor_tensor(
                                out=pnb[par][:, :], in0=pbf[par][:, :],
                                in1=bass.AP(nmb, 8 + 4 * par, [[16, 128], [1, 4], [0, 128]]),
                                op=ALU.mult).then_inc(sD.inc(), 1)
                            te.wait_ge(sD.h, sD_b + 2 * bg + 2)
                            if bg >= 2:
                                te.wait_ge(sF.h, sF_b + bg - 1)
                            for hh in range(4):
                                mm = te.transpose(Tps[par][:, hh * 128:(hh + 1) * 128],
                                                  pnb[par][:, hh * 128:(hh + 1) * 128],
                                                  ident[:, :])
                            mm.then_inc(sE.inc(), 1)
                            dve.wait_ge(sE.h, sE_b + bg + 1)
                            if bg >= 2:
                                dve.wait_ge(sG.h, sG_b + 2 * (bg - 2) + 2)
                            dve.tensor_copy(pTb[par][:, :], Tps[par][:, :]).then_inc(sF.inc(), 1)
                            te.wait_ge(sF.h, sF_b + bg + 1)
                            if g == 0 and b >= 2:
                                te.wait_ge(sH.h, sH_b + b - 1)
                            for idx, hh in enumerate((0, 2, 1, 3)):
                                h = 4 * g + hh
                                i = h // 2
                                pb_ = 64 * (h % 2)
                                mm = te.matmul(
                                    Ops[b % 2][pb_:pb_ + 64, i * 128:(i + 1) * 128],
                                    vh[:, b * 1024 + h * 64:b * 1024 + (h + 1) * 64],
                                    pTb[par][:, hh * 128:(hh + 1) * 128],
                                    start=True, stop=True)
                                if idx == 1:
                                    mm.then_inc(sG.inc(), 1)
                                    te.wait_ge(sG.h, sG.n)
                            mm.then_inc(sG.inc(), 1)
                            dqfeed(1)
                        dve.wait_ge(sG.h, sG_b + 8 * (b + 1))
                        dve.tensor_copy(
                            bass.AP(oT, b * 128, [[8 * 1024, 128], [1024, 8], [1, 128]]),
                            bass.AP(Ops[b % 2], 0, [[1024, 128], [128, 8], [1, 128]]),
                        ).then_inc(sH.inc(), 1)
                    nc.all_engine_barrier()

                    def wob(i):
                        return Ops[i // 2][:, (i % 2) * 512:(i % 2 + 1) * 512]

                    with ExitStack() as ws:
                        wwb = ws.enter_context(sbt("wwb", [128, 8 * 1024], BF16))
                        bolb = ws.enter_context(sbt("bolb", [1, WQK], BF16))
                        cbsl.load(gp, bolb[:, :], bo.ap())
                        wo_li = qwsl.load(
                            gp, bass.AP(wwb, 0, [[8 * 1024, 128], [1024, 8], [1, 1024]]),
                            bass.AP(wo, 0, [[WQK, 128], [128 * WQK, 8], [1, WQK]]))
                        qwsl.wait(te, wo_li)
                        cbsl.wait(te, len(cbsl.tgt) - 1)
                        wop_b = wop.n
                        woe_b = woe.n
                        for m in range(8):
                            bank = (m % 2) * 2
                            if m >= 2:
                                te.wait_ge(woe.h, woe_b + m - 1)
                            for nh in range(2):
                                pslice = wob(bank + nh)
                                te.matmul(pslice, bolb[:, 128 * m:128 * (m + 1)],
                                          ones[:, 0:512], start=True, stop=False)
                                for a in range(8):
                                    mm = te.matmul(
                                        pslice,
                                        wwb[:, a * 1024 + 128 * m:a * 1024 + 128 * (m + 1)],
                                        oT[:, a * 1024 + nh * 512:a * 1024 + (nh + 1) * 512],
                                        start=False, stop=(a == 7))
                            mm.then_inc(wop.inc(), 1)
                            act.wait_ge(wop.h, wop.n)
                            act.activation(aoT[:, m * 1024:(m + 1) * 1024],
                                           Ops[bank // 2][:, :],
                                           AF.Copy).then_inc(woe.inc(), 1)
                            dqfeed(3)
                        nc.all_engine_barrier()

                    with ExitStack() as ps_:
                        pwb = [ps_.enter_context(sbt(f"pwb{i}", [128, 8 * 128], BF16))
                               for i in range(2)]
                        pblb = ps_.enter_context(sbt("pblb", [1, DM], BF16))
                        fbx = [ps_.enter_context(sbt(f"fbx{i}", [128, 512], F32))
                               for i in range(4)]
                        cbsl.load(gp, pblb[:, :], pb.ap())
                        pjp_b = pjp.n
                        pje_b = pje.n
                        pw_base = len(qwsl.tgt)

                        def pwload(m):
                            if m >= 2:
                                gp.wait_ge(pjp.h, pjp_b + m - 1)
                            qwsl.load(gp,
                                      bass.AP(pwb[(pw_base + m) % 2], 0,
                                              [[8 * 128, 128], [128, 8], [1, 128]]),
                                      bass.AP(pw, 128 * m, [[DM, 128], [128 * DM, 8], [1, 128]]))

                        pwload(0)
                        pwload(1)
                        cbsl.wait(te, len(cbsl.tgt) - 1)
                        aw_base = len(aws.tgt)
                        for m in range(32):
                            bank = (m % 2) * 2
                            if m + 2 < 32:
                                pwload(m + 2)
                            qwsl.wait(te, pw_base + m)
                            if m >= 2:
                                te.wait_ge(pje.h, pje_b + 2 * (m - 2) + 2)
                            wbm = pwb[(pw_base + m) % 2]
                            for nh in range(2):
                                pslice = wob(bank + nh)
                                te.matmul(pslice, pblb[:, 128 * m:128 * (m + 1)],
                                          ones[:, 0:512], start=True, stop=False)
                                for a in range(8):
                                    mm = te.matmul(
                                        pslice, wbm[:, a * 128:(a + 1) * 128],
                                        aoT[:, a * 1024 + nh * 512:a * 1024 + (nh + 1) * 512],
                                        start=False, stop=(a == 7))
                            mm.then_inc(pjp.inc(), 1)
                            act.wait_ge(pjp.h, pjp.n)
                            for nh in range(2):
                                fi = 2 * (m % 2) + nh
                                if m >= 2:
                                    aws.wait(act, aw_base + 2 * (m - 2) + nh)
                                act.activation(fbx[fi][:, :], wob(bank + nh),
                                               AF.Copy).then_inc(pje.inc(), 1)
                                gp.wait_ge(pje.h, pje.n)
                                aws.load(gp,
                                         bass.AP(accs, 128 * m * TCORE + t0 + nh * 512,
                                                 [[TCORE, 128], [1, 512]]),
                                         fbx[fi][:, :])
                            dqfeed(1)
                        for i_ in range(4):
                            aws.wait(gp, len(aws.tgt) - 1 - i_)
                        nc.all_engine_barrier()
            if upto == "attn" and half == 0:
                return nc

            with ExitStack() as fhs:
                hT = fhs.enter_context(sbt("hT", [128, NH * 512], BF16))
                wsl = [fhs.enter_context(sbt(f"wsl{i}", [128, 512], BF16))
                       for i in range(8)]
                fps = fhs.enter_context(pst("fps", [128, 8 * 512], F32))

                with ExitStack() as f1s:
                    xnTq2 = [f1s.enter_context(sbt(f"xnTq{qq}", [128, ND * 528], BF16))
                             for qq in range(2)]
                    hstmp = [f1s.enter_context(sbt(f"hst{i}", [128, 4 * 512], BF16))
                             for i in range(2)]
                    hs1 = [f1s.enter_context(sbt(f"hs1_{i}", [128, 2 * 512], BF16))
                           for i in range(2)]

                    if half == 0:
                        for gi_ in (15, 16):
                            sy.wait_ge(lnw.tgt[gi_][0], lnw.tgt[gi_][1])
                            sc.wait_ge(lnw.tgt[gi_][0], lnw.tgt[gi_][1])
                    for lq in range(2):
                        q = half * 2 + lq
                        glast = (q * 512 + 528 - 1) // 128
                        for gi_ in (glast - 1, glast):
                            sy.wait_ge(lnw.tgt[gi_][0], lnw.tgt[gi_][1])
                        for k in range(ND):
                            xnb.load(sy,
                                     bass.AP(xnTq2[lq], k * 528, [[ND * 528, 128], [1, 528]]),
                                     xnsc.ap()[q * 512:q * 512 + 528, 128 * k:128 * (k + 1)],
                                     transpose=True)
                    xnbq_done = xnb.tgt[-1]

                    f1c_b = f1c.n
                    sSil_b = sSil.n
                    hmul_b = hmul.n
                    fw_base = len(fws.tgt)
                    hsw_base = len(hsw.tgt)
                    NMG = NH // 2
                    loads = [(mg, which, k) for mg in range(NMG)
                             for which in (0, 1) for k in range(ND)]

                    def f1load(li):
                        mg, which, k = loads[li]
                        src = w1d if which == 0 else w3d
                        rows0 = 256 * mg
                        if li >= 8:
                            sy.wait_ge(f1c.h, f1c_b + li - 7)
                        if k % 16 == 0:
                            for kt in (2 * mg, 2 * mg + 1):
                                h_, v_ = wtgt(uid_w(kt, which, k // 16))
                                sy.wait_ge(h_, v_)
                        fws.load(sy,
                                 bass.AP(wsl[(fw_base + li) % 8], 0, [[512, 128], [1, 256]]),
                                 src.ap()[rows0:rows0 + 256, 128 * k:128 * (k + 1)],
                                 transpose=True)

                    for li in range(8):
                        f1load(li)
                    te.wait_ge(xnbq_done[0], xnbq_done[1])
                    li = 0
                    for mg in range(NMG):
                        for which in (0, 1):
                            if mg >= 1:
                                if which == 0:
                                    te.wait_ge(sSil.h, sSil_b + mg)
                                else:
                                    te.wait_ge(hmul.h, hmul_b + mg)
                            for k in range(ND):
                                if li + 8 < len(loads):
                                    f1load(li + 8)
                                fws.wait(te, fw_base + li)
                                wtile = wsl[(fw_base + li) % 8]
                                for j in range(2):
                                    for qq in range(2):
                                        bank = 4 * which + 2 * qq + j
                                        mm = te.matmul(
                                            fps[:, bank * 512:(bank + 1) * 512],
                                            wtile[:, j * 128:(j + 1) * 128],
                                            xnTq2[qq][:, k * 528 + 1:k * 528 + 513],
                                            start=(k == 0), stop=(k == ND - 1))
                                mm.then_inc(f1c.inc(), 1)
                                li += 1
                            dqfeed(6)
                        act.wait_ge(f1c.h, f1c_b + (2 * mg + 1) * ND)
                        if mg >= 2:
                            act.wait_ge(hmul.h, hmul_b + mg - 1)
                        for b_ in range(4):
                            aa = act.activation(hstmp[mg % 2][:, b_ * 512:(b_ + 1) * 512],
                                                fps[:, b_ * 512:(b_ + 1) * 512], AF.Silu)
                        aa.then_inc(sSil.inc(), 1)
                        dve.wait_ge(f1c.h, f1c_b + (2 * mg + 2) * ND)
                        dve.wait_ge(sSil.h, sSil_b + mg + 1)
                        if mg >= 2:
                            hsw.wait(dve, hsw_base + mg - 2)
                        for j in range(2):
                            dve.tensor_tensor(
                                out=hT[:, (2 * mg + j) * 512:(2 * mg + j + 1) * 512],
                                in0=hstmp[mg % 2][:, j * 512:(j + 1) * 512],
                                in1=fps[:, (4 + j) * 512:(4 + j + 1) * 512],
                                op=ALU.mult)
                        for j in range(2):
                            vv = dve.tensor_tensor(
                                out=hs1[mg % 2][:, j * 512:(j + 1) * 512],
                                in0=hstmp[mg % 2][:, (2 + j) * 512:(3 + j) * 512],
                                in1=fps[:, (6 + j) * 512:(6 + j + 1) * 512],
                                op=ALU.mult)
                        vv.then_inc(hmul.inc(), 1)
                        gp.wait_ge(hmul.h, hmul_b + mg + 1)
                        hsw.load(gp,
                                 bass.AP(hsc, 2 * mg * 128 * 512,
                                         [[512, 128], [128 * 512, 2], [1, 512]]),
                                 bass.AP(hs1[mg % 2], 0, [[1024, 128], [512, 2], [1, 512]]))
                    hsw.wait(gp, len(hsw.tgt) - 1)
                    hsw.wait(gp, len(hsw.tgt) - 2)
                    nc.all_engine_barrier()

                for lq in range(2):
                    q = half * 2 + lq
                    hrl_base = len(hrl.tgt)
                    if lq == 1:
                        for c_ in range(8):
                            k0 = c_ * 11
                            nk = min(11, NH - k0)
                            hrl.load(gp,
                                     bass.AP(hT, k0 * 512,
                                             [[NH * 512, 128], [512, nk], [1, 512]]),
                                     bass.AP(hsc, k0 * 128 * 512,
                                             [[512, 128], [128 * 512, nk], [1, 512]]))
                    with ExitStack() as f2s:
                        accb = [f2s.enter_context(sbt(f"accb{i}", [128, 512], F32))
                                for i in range(4)]
                        asmb = [f2s.enter_context(sbt(f"asmb{i}", [128, 512], BF16))
                                for i in range(4)]

                        f2c_b = f2c.n
                        sF2_b = sF2d.n
                        f2_base = len(fws.tgt)
                        acc_base = len(accl.tgt)
                        asw_base = len(asws.tgt)

                        def f2load(li2):
                            dg, k = divmod(li2, NH)
                            if li2 >= 8:
                                sy.wait_ge(f2c.h, f2c_b + li2 - 7)
                            ci = min(k // 16, len(W2CH) - 1)
                            if k == W2CH[ci][0]:
                                for dt in range(4 * dg, 4 * dg + 4):
                                    h_, v_ = wtgt(uid_2(dt, ci))
                                    sy.wait_ge(h_, v_)
                            fws.load(sy,
                                     bass.AP(wsl[(f2_base + li2) % 8], 0,
                                             [[512, 128], [1, 512]]),
                                     w2d.ap()[512 * dg:512 * (dg + 1),
                                              128 * k:128 * (k + 1)],
                                     transpose=True)

                        def accload(m):
                            if m >= 4:
                                gp.wait_ge(sF2d.h, sF2_b + m - 3)
                            accl.load(gp,
                                      bass.AP(accb[(acc_base + m) % 4], 0,
                                              [[512, 128], [1, 512]]),
                                      bass.AP(accs, 128 * m * TCORE + q * 512,
                                              [[TCORE, 128], [1, 512]]))

                        for li2 in range(8):
                            f2load(li2)
                        for m in range(4):
                            accload(m)
                        li2 = 0
                        for dg in range(8):
                            bank0 = 4 * (dg % 2)
                            if dg >= 2:
                                te.wait_ge(sF2d.h, sF2_b + 4 * (dg - 1))
                            for k in range(NH):
                                if lq == 1 and dg == 0 and k % 11 == 0:
                                    hrl.wait(te, hrl_base + k // 11)
                                if k % 4 == 0:
                                    dqfeed(1)
                                if li2 + 8 < 8 * NH:
                                    f2load(li2 + 8)
                                fws.wait(te, f2_base + li2)
                                wtile = wsl[(f2_base + li2) % 8]
                                for j in range(4):
                                    mm = te.matmul(
                                        fps[:, (bank0 + j) * 512:(bank0 + j + 1) * 512],
                                        wtile[:, j * 128:(j + 1) * 128],
                                        hT[:, k * 512:(k + 1) * 512],
                                        start=(k == 0), stop=(k == NH - 1))
                                mm.then_inc(f2c.inc(), 1)
                                li2 += 1
                            for j in range(4):
                                m = 4 * dg + j
                                if m + 4 < 32:
                                    accload(m + 4)
                                dve.wait_ge(f2c.h, f2c_b + (dg + 1) * NH)
                                accl.wait(dve, acc_base + m)
                                if m >= 4:
                                    asws.wait(dve, asw_base + m - 4)
                                dve.tensor_tensor(out=asmb[m % 4][:, :],
                                                  in0=fps[:, (bank0 + j) * 512:
                                                          (bank0 + j + 1) * 512],
                                                  in1=accb[(acc_base + m) % 4][:, :],
                                                  op=ALU.add).then_inc(sF2d.inc(), 1)
                                gp.wait_ge(sF2d.h, sF2d.n)
                                asws.load(gp,
                                          bass.AP(asms, 128 * m * TCORE + q * 512,
                                                  [[TCORE, 128], [1, 512]]),
                                          asmb[m % 4][:, :])
                        for i_ in range(4):
                            asws.wait(gp, len(asws.tgt) - 1 - i_)
                        nc.all_engine_barrier()

                    with ExitStack() as asx:
                        abuf = [asx.enter_context(sbt(f"abuf{i}", [128, DM], BF16))
                                for i in range(2)]
                        xb2 = [asx.enter_context(sbt(f"xb2{i}", [128, DM], F32))
                               for i in range(2)]
                        ob = [asx.enter_context(sbt(f"ob{i}", [128, DM], F32))
                              for i in range(2)]
                        ab_base = len(abls.tgt)
                        ax_base = len(axls.tgt)
                        st_base = len(stws.tgt)
                        f3add = []
                        for j in range(4):
                            s = j % 2
                            if j >= 2:
                                sy.wait_ge(sF2d.h, f3add[j - 2])
                                sc.wait_ge(sF2d.h, f3add[j - 2])
                            abls.load(sy, abuf[s][:, :],
                                      asms.ap()[0:DM, q * 512 + 128 * j:
                                                q * 512 + 128 * (j + 1)],
                                      transpose=True)
                            axls.load(sc, xb2[s][:, :],
                                      x.ap()[1 + q * 512 + 128 * j:
                                             1 + q * 512 + 128 * (j + 1), :])
                            abls.wait(dve, ab_base + j)
                            axls.wait(dve, ax_base + j)
                            if j >= 2:
                                stws.wait(dve, st_base + j - 2)
                            dve.tensor_tensor(out=ob[s][:, :], in0=abuf[s][:, :],
                                              in1=xb2[s][:, :], op=ALU.add
                                              ).then_inc(sF2d.inc(), 1)
                            f3add.append(sF2d.n)
                            gp.wait_ge(sF2d.h, sF2d.n)
                            stws.load(gp, out.ap()[q * 512 + 128 * j:
                                                   q * 512 + 128 * (j + 1), :],
                                      ob[s][:, :])
                        stws.wait(gp, len(stws.tgt) - 1)
                        stws.wait(gp, len(stws.tgt) - 2)
                        nc.all_engine_barrier()
                    if upto == "ffn0" and q == 0:
                        return nc

    for i_ in range(2):
        stws.wait(gp, len(stws.tgt) - 1 - i_)
    est.close()
    return nc


_cache = {}


def kernel(L0_attn_inputs, L0_attn_conv_w, L0_attn_conv_b, L0_attn_wq, L0_attn_bq,
           L0_attn_wk, L0_attn_bk, L0_attn_wv, L0_attn_bv, L0_attn_wo, L0_attn_bo,
           L0_attn_proj_w, L0_attn_proj_b, L0_ffn_w1, L0_ffn_s1, L0_ffn_w2,
           L0_ffn_s2, L0_ffn_w3, L0_ffn_s3):
    xf = np.asarray(L0_attn_inputs, dtype=np.float32).reshape(16384, DM)
    zero = np.zeros((1, DM), np.float32)
    pad = np.zeros((ROWS - TCORE - 2, DM), np.float32)

    common = {
        "convw": np.ascontiguousarray(np.asarray(L0_attn_conv_w, np.float32)),
        "convb": np.asarray(L0_attn_conv_b, np.float32).reshape(1, H3),
        "wq": np.ascontiguousarray(np.asarray(L0_attn_wq, np.float32).reshape(WQK, WQK)),
        "bq": np.asarray(L0_attn_bq, np.float32).reshape(1, WQK),
        "wk": np.ascontiguousarray(np.asarray(L0_attn_wk, np.float32).reshape(WQK, WQK)),
        "bk": np.asarray(L0_attn_bk, np.float32).reshape(1, WQK),
        "wv": np.ascontiguousarray(np.asarray(L0_attn_wv, np.float32).reshape(WQK, WQK)),
        "bv": np.asarray(L0_attn_bv, np.float32).reshape(1, WQK),
        "wo": np.ascontiguousarray(np.asarray(L0_attn_wo, np.float32).reshape(WQK, WQK)),
        "bo": np.asarray(L0_attn_bo, np.float32).reshape(1, WQK),
        "pw": np.ascontiguousarray(np.asarray(L0_attn_proj_w, np.float32)),
        "pb": np.asarray(L0_attn_proj_b, np.float32).reshape(1, DM),
        "w1": np.ascontiguousarray(np.asarray(L0_ffn_w1, np.int32)),
        "s1": np.asarray(L0_ffn_s1, np.float32).reshape(1, 704512),
        "w2": np.ascontiguousarray(np.asarray(L0_ffn_w2, np.int32)),
        "s2": np.asarray(L0_ffn_s2, np.float32).reshape(1, 704512),
        "w3": np.ascontiguousarray(np.asarray(L0_ffn_w3, np.int32)),
        "s3": np.asarray(L0_ffn_s3, np.float32).reshape(1, 704512),
        "identin": np.eye(128, dtype=ml_dtypes.bfloat16),
        "maskin": np.tile(np.where(np.tril(np.ones((128, 128), bool)), 0.0,
                                   np.float32(-1e9)).astype(np.float32), (1, 4)),
    }

    in_maps = []
    for c in range(NCORE):
        lo = c * TCORE
        left = zero if lo % 8192 == 0 else xf[lo - 1:lo]
        hi = lo + TCORE
        right = zero if hi % 8192 == 0 else xf[hi:hi + 1]
        shard = np.ascontiguousarray(
            np.concatenate([left, xf[lo:hi], right, pad], axis=0))
        m = dict(common)
        m["x"] = shard
        in_maps.append(m)

    key = "prog"
    if key not in _cache:
        nc = bass.Bass("TRN2", target_bir_lowering=False)
        build(nc)
        _cache[key] = nc
    nc = _cache[key]

    res = run_bass_kernel_spmd(nc, in_maps, core_ids=list(range(NCORE)))
    outs = [res.results[c]["out"] for c in range(NCORE)]
    return np.concatenate(outs, axis=0).reshape(2, 8192, DM)




# version 08
# speedup vs baseline: 1.1312x; 1.1312x over previous
import numpy as np
import ml_dtypes
from contextlib import ExitStack

import concourse.bass as bass
import concourse.mybir as mybir
from concourse.bass_utils import run_bass_kernel_spmd

F32 = mybir.dt.float32
BF16 = mybir.dt.bfloat16
I32 = mybir.dt.int32
AF = mybir.ActivationFunctionType
ALU = mybir.AluOpType
AX = mybir.AxisListType.X

NCORE = 8
TCORE = 2048
ROWS = 2176
DM = 4096
ND = DM // 128
WQK = 1024
NW = 8
H3 = 3072
N3 = 24
HID = 11008
NH = 86
NHALF = 1024
NQ = 512

HG = [4] * 21 + [2]
HGOFF = [0]
for _s in HG:
    HGOFF.append(HGOFF[-1] + _s)
W2CH = [(0, 16), (16, 16), (32, 16), (48, 16), (64, 16), (80, 6)]

DBG = False
PH_LN = True
PH_DQ = True


class Ctr:

    def __init__(self, nc, name):
        self.h = nc.alloc_semaphore(name)
        self.n = 0

    def inc(self, k=1):
        self.n += k
        return self.h


class Slots:

    def __init__(self, nc, name, n):
        self.sems = [Ctr(nc, f"{name}{i}") for i in range(n)]
        self.n = n
        self.tgt = []

    def load(self, q, out_ap, in_ap, **kw):
        i = len(self.tgt)
        s = self.sems[i % self.n]
        q.dma_start(out_ap, in_ap, **kw).then_inc(s.inc(16), 16)
        self.tgt.append((s.h, s.n))
        return i

    def wait(self, eng, i):
        h, v = self.tgt[i]
        eng.wait_ge(h, v)


def build(nc, upto="all"):
    x = nc.dram_tensor("x", [ROWS, DM], F32, kind="ExternalInput")
    convw = nc.dram_tensor("convw", [3, DM, H3], F32, kind="ExternalInput")
    convb = nc.dram_tensor("convb", [1, H3], F32, kind="ExternalInput")
    wq = nc.dram_tensor("wq", [WQK, WQK], F32, kind="ExternalInput")
    wk = nc.dram_tensor("wk", [WQK, WQK], F32, kind="ExternalInput")
    wv = nc.dram_tensor("wv", [WQK, WQK], F32, kind="ExternalInput")
    bq = nc.dram_tensor("bq", [1, WQK], F32, kind="ExternalInput")
    bk = nc.dram_tensor("bk", [1, WQK], F32, kind="ExternalInput")
    bv = nc.dram_tensor("bv", [1, WQK], F32, kind="ExternalInput")
    wo = nc.dram_tensor("wo", [WQK, WQK], F32, kind="ExternalInput")
    bo = nc.dram_tensor("bo", [1, WQK], F32, kind="ExternalInput")
    pw = nc.dram_tensor("pw", [WQK, DM], F32, kind="ExternalInput")
    pb = nc.dram_tensor("pb", [1, DM], F32, kind="ExternalInput")
    w1 = nc.dram_tensor("w1", [352256, 64], I32, kind="ExternalInput")
    w2 = nc.dram_tensor("w2", [352256, 64], I32, kind="ExternalInput")
    w3 = nc.dram_tensor("w3", [352256, 64], I32, kind="ExternalInput")
    s1 = nc.dram_tensor("s1", [1, 704512], F32, kind="ExternalInput")
    s2 = nc.dram_tensor("s2", [1, 704512], F32, kind="ExternalInput")
    s3 = nc.dram_tensor("s3", [1, 704512], F32, kind="ExternalInput")
    identin = nc.dram_tensor("identin", [128, 128], BF16, kind="ExternalInput")
    maskin = nc.dram_tensor("maskin", [128, 512], F32, kind="ExternalInput")
    out = nc.dram_tensor("out", [TCORE, DM], F32, kind="ExternalOutput")

    scratch_kind = "ExternalOutput" if DBG else "Internal"
    xnsc = nc.dram_tensor("xnsc", [ROWS, DM], BF16, kind=scratch_kind)
    w1d = nc.dram_tensor("w1d", [HID, DM], BF16, kind=scratch_kind)
    w3d = nc.dram_tensor("w3d", [HID, DM], BF16, kind=scratch_kind)
    w2d = nc.dram_tensor("w2d", [DM, HID], BF16, kind=scratch_kind)
    accs = nc.dram_tensor("accs", [DM, TCORE], F32, kind=scratch_kind)
    asms = nc.dram_tensor("asms", [DM, TCORE], BF16, kind=scratch_kind)
    hsc = nc.dram_tensor("hsc", [HID, 512], BF16, kind=scratch_kind)

    sy = nc.sync
    sc = nc.scalar
    gp = nc.gpsimd
    dve = nc.vector
    act = nc.scalar
    pool = nc.gpsimd
    te = nc.tensor

    _uid = [0]

    def sbt(name, shape, dtype):
        _uid[0] += 1
        return nc.sbuf_tensor(f"{name}_{_uid[0]}", shape, dtype)

    def pst(name, shape, dtype):
        _uid[0] += 1
        return nc.psum_tensor(f"{name}_{_uid[0]}", shape, dtype)

    allsems = []

    def C(name):
        c = Ctr(nc, name)
        allsems.append(c)
        return c

    def SL(name, n):
        s = Slots(nc, name, n)
        allsems.extend(s.sems)
        return s

    ini = SL("ini", 1)
    xls = SL("xls", 2)
    lnv = C("lnv")
    lna = C("lna")
    lnq = C("lnq")
    lnw = SL("lnw", 2)
    pk_v = SL("pkv", 4)
    pk_p = SL("pkp", 4)
    dqv = C("dqv")
    dqp = C("dqp")
    dwr_v = SL("dwrv", 2)
    dwr_p = SL("dwrp", 2)
    xnb = SL("xnb", 1)
    cwsl = SL("cws", 2)
    cbsl = SL("cbs", 1)
    ccp = C("ccp")
    cce = C("cce")
    qwsl = SL("qws", 2)
    qkp = C("qkp")
    qke = C("qke")
    sSf = C("sSf")
    sA = C("sA"); sB = C("sB"); sCx = C("sCx"); sD = C("sD")
    sE = C("sE"); sF = C("sF"); sG = C("sG"); sH = C("sH")
    sR = C("sR")
    wop = C("wop"); woe = C("woe")
    pjp = C("pjp"); pje = C("pje")
    aws = SL("aws", 4)
    fws = SL("fws", 8)
    f1c = C("f1c")
    f2c = C("f2c")
    sSil = C("sSil"); hmul = C("hmul")
    sF2d = C("sF2d")
    accl = SL("accl", 4)
    asws = SL("asws", 4)
    hsw = SL("hsw", 2)
    hrl = SL("hrl", 8)
    abls = SL("abls", 2)
    axls = SL("axls", 2)
    stws = SL("stws", 2)

    est = ExitStack()
    ident = est.enter_context(sbt("ident", [128, 128], BF16))
    mask4 = est.enter_context(sbt("mask4", [128, 512], F32))
    ones = est.enter_context(sbt("ones", [1, 1024], BF16))

    for c in allsems:
        pool.sem_clear(c.h)
    nc.all_engine_barrier()
    nc.all_engine_barrier()

    ini.load(sy, ident[:, :], identin.ap())
    ini.load(sy, mask4[:, :], maskin.ap())
    dve.memset(ones[:, :], 1.0)

    with ExitStack() as ph:
        xbuf = [ph.enter_context(sbt(f"xbuf{i}", [128, DM], F32)) for i in range(2)]
        xsq = ph.enter_context(sbt("xsq", [128, DM], F32))
        xnt = [ph.enter_context(sbt(f"xnt{i}", [128, DM], BF16)) for i in range(2)]
        st8 = ph.enter_context(sbt("st8", [128, 8], F32))

        lna_copy = []
        lnv_sq = []
        for g in range(17):
            s = g % 2
            if g >= 2:
                sc.wait_ge(lna.h, lna_copy[g - 2])
            li = xls.load(sc, xbuf[s][:, :], x.ap()[128 * g:128 * (g + 1), :])
            xls.wait(dve, li)
            dve.reduce_sum(st8[:, 0:1], xbuf[s][:, :], axis=AX).then_inc(lnv.inc(), 1)
            dve.wait_ge(lnv.h, lnv.n)
            xls.wait(act, li)
            if g >= 1:
                act.wait_ge(lnv.h, lnv_sq[g - 1])
            act.activation(xsq[:, :], xbuf[s][:, :], AF.Square).then_inc(lnq.inc(), 1)
            dve.wait_ge(lnq.h, lnq.n)
            dve.reduce_sum(st8[:, 1:2], xsq[:, :], axis=AX).then_inc(lnv.inc(), 1)
            lnv_sq.append(lnv.n)
            dve.wait_ge(lnv.h, lnv.n)
            dve.tensor_single_scalar(out=st8[:, 2:3], in_=st8[:, 0:1],
                                     scalar=1.0 / DM, op=ALU.mult).then_inc(lnv.inc(), 1)
            dve.tensor_single_scalar(out=st8[:, 3:4], in_=st8[:, 1:2],
                                     scalar=1.0 / DM, op=ALU.mult).then_inc(lnv.inc(), 1)
            dve.wait_ge(lnv.h, lnv.n)
            dve.tensor_tensor(out=st8[:, 4:5], in0=st8[:, 2:3], in1=st8[:, 2:3],
                              op=ALU.mult).then_inc(lnv.inc(), 1)
            dve.wait_ge(lnv.h, lnv.n)
            dve.scalar_tensor_tensor(out=st8[:, 5:6], in0=st8[:, 4:5], scalar=-1.0,
                                     in1=st8[:, 3:4], op0=ALU.mult, op1=ALU.add
                                     ).then_inc(lnv.inc(), 1)
            dve.wait_ge(lnv.h, lnv.n)
            dve.tensor_single_scalar(out=st8[:, 5:6], in_=st8[:, 5:6],
                                     scalar=1e-5, op=ALU.add).then_inc(lnv.inc(), 1)
            act.wait_ge(lnv.h, lnv.n)
            act.activation(st8[:, 6:7], st8[:, 5:6], AF.Ln).then_inc(lna.inc(), 1)
            act.wait_ge(lna.h, lna.n)
            act.activation(st8[:, 6:7], st8[:, 6:7], AF.Exp, scale=-0.5
                           ).then_inc(lna.inc(), 1)
            dve.wait_ge(lna.h, lna.n)
            dve.scalar_tensor_tensor(out=st8[:, 7:8], in0=st8[:, 2:3], scalar=-1.0,
                                     in1=st8[:, 6:7], op0=ALU.mult, op1=ALU.mult
                                     ).then_inc(lnv.inc(), 1)
            act.wait_ge(lnv.h, lnv.n)
            if g >= 2:
                _, v = lnw.tgt[g - 2]
                act.wait_ge(lnw.tgt[g - 2][0], v)
            act.activation(xnt[s][:, :], xbuf[s][:, :], AF.Identity,
                           bias=st8[:, 7:8], scale=st8[:, 6:7]).then_inc(lna.inc(), 1)
            lna_copy.append(lna.n)
            gp.wait_ge(lna.h, lna.n)
            lnw.load(gp, xnsc.ap()[128 * g:128 * (g + 1), :], xnt[s][:, :])
    if upto == "ln":
        lnw.wait(gp, 15)
        lnw.wait(gp, 16)
        ini.wait(sy, 1)
        nc.all_engine_barrier()
        return nc

    units = []
    for kt in range(NH):
        for which in (0, 1):
            for dh in (0, 1):
                units.append(("w13", (kt, which, dh)))
    for dt in range(ND):
        for ci in range(len(W2CH)):
            units.append(("w2", (dt, ci)))
    NU = len(units)

    def uid_w(kt, which, dh):
        return kt * 4 + which * 2 + dh

    def uid_2(dt, ci):
        return NH * 4 + dt * len(W2CH) + ci

    def uparams(u):
        kind, idx = u
        if kind == "w13":
            kt, which, dh = idx
            return dict(src=(w1 if which == 0 else w3),
                        ssrc=(s1 if which == 0 else s3),
                        dst=(w1d if which == 0 else w3d),
                        pkoff=kt * 128 * 2048 + dh * 1024, pkw=1024,
                        ppitch=2048, mcnt=16,
                        soff=kt * 128 * 64 + 32 * dh, spp=64, scnt=32,
                        dstoff=kt * 128 * DM + dh * 2048, dpitch=DM, dw=2048)
        dt, ci = idx
        m0, mc = W2CH[ci]
        return dict(src=w2, ssrc=s2, dst=w2d,
                    pkoff=(dt * 128 * 86 + m0) * 64, pkw=mc * 64, ppitch=86 * 64,
                    mcnt=mc, soff=dt * 128 * 172 + 2 * m0, spp=172, scnt=2 * mc,
                    dstoff=dt * 128 * HID + m0 * 128, dpitch=HID, dw=mc * 128)

    def wtgt(u):
        return (dwr_v.sems[u % 2].h, 16 * (u // 2 + 1))

    pkb = [est.enter_context(sbt(f"pkb{i}", [128, 1024], I32)) for i in range(2)]
    scb = [est.enter_context(sbt(f"scqb{i}", [128, 32], F32)) for i in range(2)]
    lsbt = est.enter_context(sbt("lsbt", [128, 1024], I32))
    outb = [est.enter_context(sbt(f"dqo{i}", [128, 2048], BF16)) for i in range(2)]

    dqst = {"load": 0, "comp": 0}

    def dq_load1():
        j = dqst["load"]
        if j >= NU:
            return
        if j == 0:
            for gi_ in (15, 16):
                gp.wait_ge(lnw.tgt[gi_][0], lnw.tgt[gi_][1])
        u = uparams(units[j])
        s = j % 2
        if j >= 2:
            gp.wait_ge(dqv.h, 4 * (j - 1))
        pk_v.load(gp, bass.AP(pkb[s], 0, [[1024, 128], [1, u["pkw"]]]),
                  bass.AP(u["src"], u["pkoff"], [[u["ppitch"], 128], [1, u["pkw"]]]))
        pk_v.load(gp, bass.AP(scb[s], 0, [[32, 128], [1, u["scnt"]]]),
                  bass.AP(u["ssrc"], u["soff"], [[u["spp"], 128], [1, u["scnt"]]]))
        dqst["load"] = j + 1

    def dqfeed(n):
        for _ in range(n):
            j = dqst["comp"]
            if j >= NU:
                return
            while dqst["load"] < min(j + 2, NU):
                dq_load1()
            u = uparams(units[j])
            s = j % 2
            ob = outb[s]
            mc = u["mcnt"]
            if j == 0:
                for gi_ in (15, 16):
                    dve.wait_ge(lnw.tgt[gi_][0], lnw.tgt[gi_][1])
            pk_v.wait(dve, 2 * j)
            pk_v.wait(dve, 2 * j + 1)
            if j >= 1:
                dve.wait_ge(dqv.h, dqv.n)
            if j >= 2:
                dwr_v.wait(dve, j - 2)
            dve.tensor_single_scalar(out=lsbt[:, 0:u["pkw"]], in_=pkb[s][:, 0:u["pkw"]],
                                     scalar=4, op=ALU.arith_shift_right
                                     ).then_inc(dqv.inc(), 1)
            dve.wait_ge(dqv.h, dqv.n)
            dve.tensor_tensor(
                out=bass.AP(ob, 0, [[2048, 128], [128, mc], [1, 64]]),
                in0=bass.AP(lsbt, 0, [[1024, 128], [64, mc], [1, 64]]),
                in1=bass.AP(scb[s], 0, [[32, 128], [2, mc], [0, 64]]),
                op=ALU.mult).then_inc(dqv.inc(), 1)
            dve.wait_ge(dqv.h, dqv.n)
            dve.tensor_scalar(out=lsbt[:, 0:u["pkw"]], in0=pkb[s][:, 0:u["pkw"]],
                              scalar1=15, scalar2=8, op0=ALU.bitwise_and,
                              op1=ALU.bitwise_xor).then_inc(dqv.inc(), 1)
            dve.wait_ge(dqv.h, dqv.n)
            dve.scalar_tensor_tensor(
                out=bass.AP(ob, 64, [[2048, 128], [128, mc], [1, 64]]),
                in0=bass.AP(lsbt, 0, [[1024, 128], [64, mc], [1, 64]]),
                scalar=8.0,
                in1=bass.AP(scb[s], 1, [[32, 128], [2, mc], [0, 64]]),
                op0=ALU.subtract, op1=ALU.mult).then_inc(dqv.inc(), 1)
            sc.wait_ge(dqv.h, dqv.n)
            wl = dwr_v.load(sc, bass.AP(u["dst"], u["dstoff"],
                                        [[u["dpitch"], 128], [1, u["dw"]]]),
                            bass.AP(ob, 0, [[2048, 128], [1, u["dw"]]]))
            assert dwr_v.tgt[wl] == wtgt(j), (dwr_v.tgt[wl], wtgt(j))
            dqst["comp"] = j + 1
            while dqst["load"] < min(j + 3, NU):
                dq_load1()

    if upto == "dq":
        dqfeed(NU)
        dwr_v.wait(sc, len(dwr_v.tgt) - 1)
        dwr_v.wait(sc, len(dwr_v.tgt) - 2)
        nc.all_engine_barrier()
        return nc

    for half in range(2):
        t0 = half * NHALF
        with ExitStack() as hs:
            if half == 0:
                for gi_ in (15, 16):
                    act.wait_ge(lnw.tgt[gi_][0], lnw.tgt[gi_][1])
            with ExitStack() as atts:
                qhT = atts.enter_context(sbt("qhT", [128, NW * 1024], BF16))
                khT = atts.enter_context(sbt("khT", [128, NW * 1024], BF16))
                vh = atts.enter_context(sbt("vh", [128, 8 * 1024], BF16))

                with ExitStack() as cs:
                    qkvT = cs.enter_context(sbt("qkvT", [128, N3 * 1024], BF16))
                    cv = cs.enter_context(ExitStack())
                    xnT = cv.enter_context(sbt("xnTc", [128, ND * 1040], BF16))
                    cwb = [cv.enter_context(sbt(f"cwb{i}", [128, 32 * 128], BF16))
                           for i in range(2)]
                    cbsb = cv.enter_context(sbt("cbsb", [1, H3], BF16))
                    cps = cv.enter_context(pst("cps", [128, 4 * 512], F32))
                    glast = (t0 + 1040 - 1) // 128
                    for gi_ in (glast - 1, glast):
                        sy.wait_ge(lnw.tgt[gi_][0], lnw.tgt[gi_][1])
                    ini.wait(sy, 1)
                    for k in range(ND):
                        xnb.load(sy, bass.AP(xnT, k * 1040, [[ND * 1040, 128], [1, 1040]]),
                                 xnsc.ap()[t0:t0 + 1040, 128 * k:128 * (k + 1)],
                                 transpose=True)
                    xnb_done = xnb.tgt[-1]
                    cbsl.load(gp, cbsb[:, :], convb.ap())
                    ccp_b = ccp.n
                    cce_b = cce.n

                    def cload(c):
                        b, k3 = divmod(c, 3)
                        if c >= 2:
                            gp.wait_ge(ccp.h, ccp_b + c - 1)
                        return cwsl.load(
                            gp,
                            bass.AP(cwb[(len(cwsl.tgt)) % 2], 0,
                                    [[32 * 128, 128], [128, 32], [1, 128]]),
                            bass.AP(convw, k3 * DM * H3 + 128 * b,
                                    [[H3, 128], [128 * H3, 32], [1, 128]]))

                    cw_base = len(cwsl.tgt)
                    cload(0)
                    cload(1)
                    te.wait_ge(xnb_done[0], xnb_done[1])
                    cbsl.wait(te, len(cbsl.tgt) - 1)
                    for b in range(N3):
                        bp = b % 2
                        if b >= 2:
                            te.wait_ge(cce.h, cce_b + b - 1)
                        for k3 in range(3):
                            c = b * 3 + k3
                            if c + 2 < N3 * 3:
                                cload(c + 2)
                            cwsl.wait(te, cw_base + c)
                            sbuf_w = cwb[(cw_base + c) % 2]
                            if k3 == 0:
                                for nh in range(2):
                                    te.matmul(cps[:, (2 * bp + nh) * 512:(2 * bp + nh + 1) * 512],
                                              cbsb[:, 128 * b:128 * (b + 1)],
                                              ones[:, 0:512], start=True, stop=False)
                            for a in range(ND):
                                for nh in range(2):
                                    mm = te.matmul(
                                        cps[:, (2 * bp + nh) * 512:(2 * bp + nh + 1) * 512],
                                        sbuf_w[:, a * 128:(a + 1) * 128],
                                        xnT[:, a * 1040 + k3 + nh * 512:
                                            a * 1040 + k3 + nh * 512 + 512],
                                        start=False, stop=(k3 == 2 and a == ND - 1))
                            mm.then_inc(ccp.inc(), 1)
                        act.wait_ge(ccp.h, ccp_b + 3 * (b + 1))
                        act.activation(qkvT[:, b * 1024:(b + 1) * 1024],
                                       cps[:, 2 * bp * 512:(2 * bp + 2) * 512],
                                       AF.Copy).then_inc(cce.inc(), 1)
                        dqfeed(2)
                    nc.all_engine_barrier()
                    cv.close()

                    with ExitStack() as qs:
                        qwb = [qs.enter_context(sbt(f"qwb{i}", [128, 8 * 1024], BF16))
                               for i in range(2)]
                        bql = qs.enter_context(sbt("bql", [1, 3 * WQK], BF16))
                        qps = qs.enter_context(pst("qps", [128, 4 * 512], F32))
                        cbsl.load(gp, bql[:, 0:WQK], bq.ap())
                        cbsl.load(gp, bql[:, WQK:2 * WQK], bk.ap())
                        cbsl.load(gp, bql[:, 2 * WQK:3 * WQK], bv.ap())
                        qkp_b = qkp.n
                        qke_b = qke.n
                        qw_base = len(qwsl.tgt)

                        def qwload(i, wt):
                            if i >= 2:
                                gp.wait_ge(qkp.h, qkp_b + 8 * (i - 1))
                            qwsl.load(gp,
                                      bass.AP(qwb[(qw_base + i) % 2], 0,
                                              [[8 * 1024, 128], [1024, 8], [1, 1024]]),
                                      bass.AP(wt, 0, [[WQK, 128], [128 * WQK, 8], [1, WQK]]))

                        qwload(0, wq)
                        qwload(1, wk)
                        cbsl.wait(te, len(cbsl.tgt) - 1)
                        for i, (srcoff, dstt, scl) in enumerate(
                                [(0, qhT, 0.125), (8, khT, 1.0), (16, vh, 1.0)]):
                            if i == 2:
                                qwload(2, wv)
                            qwsl.wait(te, qw_base + i)
                            wb = qwb[(qw_base + i) % 2]
                            for m in range(8):
                                bank = (m % 2) * 2
                                if i * 8 + m >= 2:
                                    te.wait_ge(qke.h, qke_b + i * 8 + m - 1)
                                for nh in range(2):
                                    pslice = qps[:, (bank + nh) * 512:(bank + nh + 1) * 512]
                                    if i < 2:
                                        te.matmul(pslice,
                                                  bql[:, i * WQK + 128 * m:i * WQK + 128 * (m + 1)],
                                                  ones[:, 0:512], start=True, stop=False)
                                        for a in range(8):
                                            mm = te.matmul(
                                                pslice,
                                                wb[:, a * 1024 + 128 * m:a * 1024 + 128 * (m + 1)],
                                                qkvT[:, (srcoff + a) * 1024 + nh * 512:
                                                     (srcoff + a) * 1024 + nh * 512 + 512],
                                                start=False, stop=(a == 7))
                                    else:
                                        te.matmul(pslice, ones[:, 0:128],
                                                  bql[:, 2 * WQK + nh * 512:2 * WQK + (nh + 1) * 512],
                                                  start=True, stop=False)
                                        for a in range(8):
                                            mm = te.matmul(
                                                pslice,
                                                qkvT[:, (16 + a) * 1024 + m * 128:
                                                     (16 + a) * 1024 + (m + 1) * 128],
                                                wb[:, a * 1024 + nh * 512:a * 1024 + (nh + 1) * 512],
                                                start=False, stop=(a == 7))
                                mm.then_inc(qkp.inc(), 1)
                                act.wait_ge(qkp.h, qkp.n)
                                act.activation(dstt[:, m * 1024:(m + 1) * 1024],
                                               qps[:, bank * 512:(bank + 2) * 512],
                                               AF.Copy, scale=scl).then_inc(qke.inc(), 1)
                                dqfeed(2)
                        nc.all_engine_barrier()

                with ExitStack() as bs:
                    oT = bs.enter_context(sbt("oT", [128, 8 * 1024], BF16))
                    aoT = bs.enter_context(sbt("aoT", [128, 8 * 1024], BF16))
                    smb = [bs.enter_context(sbt(f"smb{i}", [128, 512], F32)) for i in range(2)]
                    pbf = [bs.enter_context(sbt(f"pbf{i}", [128, 512], BF16)) for i in range(2)]
                    pnb = [bs.enter_context(sbt(f"pnb{i}", [128, 512], BF16)) for i in range(2)]
                    pTb = [bs.enter_context(sbt(f"pTb{i}", [128, 512], BF16)) for i in range(2)]
                    nmb = bs.enter_context(sbt("nmb", [128, 16], F32))
                    Sps = [bs.enter_context(pst(f"Sps{i}", [128, 512], F32)) for i in range(2)]
                    Tps = [bs.enter_context(pst(f"Tps{i}", [128, 512], BF16)) for i in range(2)]
                    Ops = [bs.enter_context(pst(f"Ops{i}", [128, 1024], F32)) for i in range(2)]
                    sA_b = sA.n; sB_b = sB.n; sC_b = sCx.n; sD_b = sD.n
                    sE_b = sE.n; sF_b = sF.n; sG_b = sG.n; sH_b = sH.n; sSf_b = sSf.n
                    for b in range(8):
                        for g in range(4):
                            bg = b * 4 + g
                            par = bg % 2
                            if bg >= 2:
                                te.wait_ge(sSf.h, sSf_b + bg - 1)
                            sa_tgt = [0, 0]
                            for idx, hh in enumerate((0, 2, 1, 3)):
                                h = 4 * g + hh
                                i = h // 2
                                pb_ = 64 * (h % 2)
                                mm = te.matmul(
                                    Sps[par][:, hh * 128:(hh + 1) * 128],
                                    qhT[pb_:pb_ + 64, i * 1024 + 128 * b:i * 1024 + 128 * (b + 1)],
                                    khT[pb_:pb_ + 64, i * 1024 + 128 * b:i * 1024 + 128 * (b + 1)],
                                    start=True, stop=True)
                                if idx == 1:
                                    mm.then_inc(sA.inc(), 1)
                                    te.wait_ge(sA.h, sA.n)
                            mm.then_inc(sA.inc(), 1)
                            sa_grp = sA.n
                            dve.wait_ge(sA.h, sa_grp)
                            if bg >= 2:
                                dve.wait_ge(sCx.h, sC_b + bg - 1)
                            dve.tensor_tensor(out=smb[par][:, :], in0=Sps[par][:, :],
                                              in1=mask4[:, :], op=ALU.add
                                              ).then_inc(sSf.inc(), 1)
                            dve.wait_ge(sSf.h, sSf.n)
                            dve.reduce_max(bass.AP(nmb, 4 * par, [[16, 128], [1, 4]]),
                                           bass.AP(smb[par], 0, [[512, 128], [128, 4], [1, 128]]),
                                           axis=AX, negate=True).then_inc(sB.inc(), 1)
                            dve.wait_ge(sB.h, sB.n)
                            dve.tensor_tensor(
                                out=smb[par][:, :], in0=smb[par][:, :],
                                in1=bass.AP(nmb, 4 * par, [[16, 128], [1, 4], [0, 128]]),
                                op=ALU.add).then_inc(sB.inc(), 1)
                            act.wait_ge(sB.h, sB_b + 2 * bg + 2)
                            if bg >= 2:
                                act.wait_ge(sD.h, sD_b + 2 * (bg - 2) + 2)
                            act.activation(pbf[par][:, :], smb[par][:, :], AF.Exp
                                           ).then_inc(sCx.inc(), 1)
                            dve.wait_ge(sCx.h, sC_b + bg + 1)
                            dve.reduce_sum(bass.AP(nmb, 8 + 4 * par, [[16, 128], [1, 4]]),
                                           bass.AP(pbf[par], 0, [[512, 128], [128, 4], [1, 128]]),
                                           axis=AX).then_inc(sD.inc(), 1)
                            dve.wait_ge(sD.h, sD.n)
                            dve.reciprocal(bass.AP(nmb, 8 + 4 * par, [[16, 128], [1, 4]]),
                                           bass.AP(nmb, 8 + 4 * par, [[16, 128], [1, 4]])
                                           ).then_inc(sR.inc(), 1)
                            dve.wait_ge(sR.h, sR.n)
                            if bg >= 2:
                                dve.wait_ge(sE.h, sE_b + bg - 1)
                            dve.tensor_tensor(
                                out=pnb[par][:, :], in0=pbf[par][:, :],
                                in1=bass.AP(nmb, 8 + 4 * par, [[16, 128], [1, 4], [0, 128]]),
                                op=ALU.mult).then_inc(sD.inc(), 1)
                            te.wait_ge(sD.h, sD_b + 2 * bg + 2)
                            if bg >= 2:
                                te.wait_ge(sF.h, sF_b + bg - 1)
                            for hh in range(4):
                                mm = te.transpose(Tps[par][:, hh * 128:(hh + 1) * 128],
                                                  pnb[par][:, hh * 128:(hh + 1) * 128],
                                                  ident[:, :])
                            mm.then_inc(sE.inc(), 1)
                            dve.wait_ge(sE.h, sE_b + bg + 1)
                            if bg >= 2:
                                dve.wait_ge(sG.h, sG_b + 2 * (bg - 2) + 2)
                            dve.tensor_copy(pTb[par][:, :], Tps[par][:, :]).then_inc(sF.inc(), 1)
                            te.wait_ge(sF.h, sF_b + bg + 1)
                            if g == 0 and b >= 2:
                                te.wait_ge(sH.h, sH_b + b - 1)
                            for idx, hh in enumerate((0, 2, 1, 3)):
                                h = 4 * g + hh
                                i = h // 2
                                pb_ = 64 * (h % 2)
                                mm = te.matmul(
                                    Ops[b % 2][pb_:pb_ + 64, i * 128:(i + 1) * 128],
                                    vh[:, b * 1024 + h * 64:b * 1024 + (h + 1) * 64],
                                    pTb[par][:, hh * 128:(hh + 1) * 128],
                                    start=True, stop=True)
                                if idx == 1:
                                    mm.then_inc(sG.inc(), 1)
                                    te.wait_ge(sG.h, sG.n)
                            mm.then_inc(sG.inc(), 1)
                            dqfeed(1)
                        dve.wait_ge(sG.h, sG_b + 8 * (b + 1))
                        dve.tensor_copy(
                            bass.AP(oT, b * 128, [[8 * 1024, 128], [1024, 8], [1, 128]]),
                            bass.AP(Ops[b % 2], 0, [[1024, 128], [128, 8], [1, 128]]),
                        ).then_inc(sH.inc(), 1)
                    nc.all_engine_barrier()

                    def wob(i):
                        return Ops[i // 2][:, (i % 2) * 512:(i % 2 + 1) * 512]

                    with ExitStack() as ws:
                        wwb = ws.enter_context(sbt("wwb", [128, 8 * 1024], BF16))
                        bolb = ws.enter_context(sbt("bolb", [1, WQK], BF16))
                        cbsl.load(gp, bolb[:, :], bo.ap())
                        wo_li = qwsl.load(
                            gp, bass.AP(wwb, 0, [[8 * 1024, 128], [1024, 8], [1, 1024]]),
                            bass.AP(wo, 0, [[WQK, 128], [128 * WQK, 8], [1, WQK]]))
                        qwsl.wait(te, wo_li)
                        cbsl.wait(te, len(cbsl.tgt) - 1)
                        wop_b = wop.n
                        woe_b = woe.n
                        for m in range(8):
                            bank = (m % 2) * 2
                            if m >= 2:
                                te.wait_ge(woe.h, woe_b + m - 1)
                            for nh in range(2):
                                pslice = wob(bank + nh)
                                te.matmul(pslice, bolb[:, 128 * m:128 * (m + 1)],
                                          ones[:, 0:512], start=True, stop=False)
                                for a in range(8):
                                    mm = te.matmul(
                                        pslice,
                                        wwb[:, a * 1024 + 128 * m:a * 1024 + 128 * (m + 1)],
                                        oT[:, a * 1024 + nh * 512:a * 1024 + (nh + 1) * 512],
                                        start=False, stop=(a == 7))
                            mm.then_inc(wop.inc(), 1)
                            act.wait_ge(wop.h, wop.n)
                            act.activation(aoT[:, m * 1024:(m + 1) * 1024],
                                           Ops[bank // 2][:, :],
                                           AF.Copy).then_inc(woe.inc(), 1)
                            dqfeed(3)
                        nc.all_engine_barrier()

                    with ExitStack() as ps_:
                        pwb = [ps_.enter_context(sbt(f"pwb{i}", [128, 8 * 128], BF16))
                               for i in range(2)]
                        pblb = ps_.enter_context(sbt("pblb", [1, DM], BF16))
                        fbx = [ps_.enter_context(sbt(f"fbx{i}", [128, 512], F32))
                               for i in range(4)]
                        cbsl.load(gp, pblb[:, :], pb.ap())
                        pjp_b = pjp.n
                        pje_b = pje.n
                        pw_base = len(qwsl.tgt)

                        def pwload(m):
                            if m >= 2:
                                gp.wait_ge(pjp.h, pjp_b + m - 1)
                            qwsl.load(gp,
                                      bass.AP(pwb[(pw_base + m) % 2], 0,
                                              [[8 * 128, 128], [128, 8], [1, 128]]),
                                      bass.AP(pw, 128 * m, [[DM, 128], [128 * DM, 8], [1, 128]]))

                        pwload(0)
                        pwload(1)
                        cbsl.wait(te, len(cbsl.tgt) - 1)
                        aw_base = len(aws.tgt)
                        for m in range(32):
                            bank = (m % 2) * 2
                            if m + 2 < 32:
                                pwload(m + 2)
                            qwsl.wait(te, pw_base + m)
                            if m >= 2:
                                te.wait_ge(pje.h, pje_b + 2 * (m - 2) + 2)
                            wbm = pwb[(pw_base + m) % 2]
                            for nh in range(2):
                                pslice = wob(bank + nh)
                                te.matmul(pslice, pblb[:, 128 * m:128 * (m + 1)],
                                          ones[:, 0:512], start=True, stop=False)
                                for a in range(8):
                                    mm = te.matmul(
                                        pslice, wbm[:, a * 128:(a + 1) * 128],
                                        aoT[:, a * 1024 + nh * 512:a * 1024 + (nh + 1) * 512],
                                        start=False, stop=(a == 7))
                            mm.then_inc(pjp.inc(), 1)
                            act.wait_ge(pjp.h, pjp.n)
                            for nh in range(2):
                                fi = 2 * (m % 2) + nh
                                if m >= 2:
                                    aws.wait(act, aw_base + 2 * (m - 2) + nh)
                                act.activation(fbx[fi][:, :], wob(bank + nh),
                                               AF.Copy).then_inc(pje.inc(), 1)
                                gp.wait_ge(pje.h, pje.n)
                                aws.load(gp,
                                         bass.AP(accs, 128 * m * TCORE + t0 + nh * 512,
                                                 [[TCORE, 128], [1, 512]]),
                                         fbx[fi][:, :])
                            dqfeed(1)
                        for i_ in range(4):
                            aws.wait(gp, len(aws.tgt) - 1 - i_)
                        nc.all_engine_barrier()
            if upto == "attn" and half == 0:
                return nc

            with ExitStack() as fhs:
                hT = fhs.enter_context(sbt("hT", [128, NH * 512], BF16))
                fps = fhs.enter_context(pst("fps", [128, 8 * 512], F32))

                with ExitStack() as f1s:
                    xnTq2 = [f1s.enter_context(sbt(f"xnTq{qq}", [128, ND * 528], BF16))
                             for qq in range(2)]
                    wsl = [f1s.enter_context(sbt(f"wsl{i}", [128, 512], BF16))
                           for i in range(8)]
                    hstmp = f1s.enter_context(sbt("hst", [128, 8 * 512], BF16))
                    hs1 = [f1s.enter_context(sbt(f"hs1_{i}", [128, 4 * 512], BF16))
                           for i in range(2)]

                    if half == 0:
                        for gi_ in (15, 16):
                            sy.wait_ge(lnw.tgt[gi_][0], lnw.tgt[gi_][1])
                            sc.wait_ge(lnw.tgt[gi_][0], lnw.tgt[gi_][1])
                    for lq in range(2):
                        q = half * 2 + lq
                        glast = (q * 512 + 528 - 1) // 128
                        for gi_ in (glast - 1, glast):
                            sy.wait_ge(lnw.tgt[gi_][0], lnw.tgt[gi_][1])
                        for k in range(ND):
                            xnb.load(sy,
                                     bass.AP(xnTq2[lq], k * 528, [[ND * 528, 128], [1, 528]]),
                                     xnsc.ap()[q * 512:q * 512 + 528, 128 * k:128 * (k + 1)],
                                     transpose=True)
                    xnbq_done = xnb.tgt[-1]

                    f1c_b = f1c.n
                    fw_base = len(fws.tgt)
                    hsw_base = len(hsw.tgt)
                    loads = [(mg, which, k) for mg in range(len(HG))
                             for which in (0, 1) for k in range(ND)]

                    def f1load(li):
                        mg, which, k = loads[li]
                        src = w1d if which == 0 else w3d
                        rows0 = 128 * HGOFF[mg]
                        nrows = 128 * HG[mg]
                        if li >= 8:
                            sy.wait_ge(f1c.h, f1c_b + li - 7)
                        if k % 16 == 0:
                            for kt in range(HGOFF[mg], HGOFF[mg] + HG[mg]):
                                h_, v_ = wtgt(uid_w(kt, which, k // 16))
                                sy.wait_ge(h_, v_)
                        fws.load(sy,
                                 bass.AP(wsl[(fw_base + li) % 8], 0, [[512, 128], [1, nrows]]),
                                 src.ap()[rows0:rows0 + nrows, 128 * k:128 * (k + 1)],
                                 transpose=True)

                    for li in range(8):
                        f1load(li)
                    te.wait_ge(xnbq_done[0], xnbq_done[1])
                    li = 0
                    silu_op = []
                    hmul_op = []
                    for mg in range(len(HG)):
                        ntile = HG[mg]
                        nb = 2 * ntile
                        for which in (0, 1):
                            for k in range(ND):
                                if li + 8 < len(loads):
                                    f1load(li + 8)
                                fws.wait(te, fw_base + li)
                                wtile = wsl[(fw_base + li) % 8]
                                for h_ in range(ntile):
                                    for qq in range(2):
                                        b_ = 2 * h_ + qq
                                        if k == 0:
                                            if which == 0 and mg >= 1:
                                                te.wait_ge(hmul.h,
                                                           hmul_op[8 * (mg - 1) + b_])
                                            if which == 1:
                                                te.wait_ge(sSil.h,
                                                           silu_op[8 * mg + b_])
                                        mm = te.matmul(
                                            fps[:, b_ * 512:(b_ + 1) * 512],
                                            wtile[:, h_ * 128:(h_ + 1) * 128],
                                            xnTq2[qq][:, k * 528 + 1:k * 528 + 513],
                                            start=(k == 0), stop=(k == ND - 1))
                                mm.then_inc(f1c.inc(), 1)
                                li += 1
                            dqfeed(6)
                            if which == 0:
                                act.wait_ge(f1c.h, f1c.n)
                                if mg >= 1:
                                    act.wait_ge(hmul.h, hmul_op[8 * (mg - 1) + 7])
                                for b_ in range(nb):
                                    act.activation(
                                        hstmp[:, b_ * 512:(b_ + 1) * 512],
                                        fps[:, b_ * 512:(b_ + 1) * 512],
                                        AF.Silu).then_inc(sSil.inc(), 1)
                                    silu_op.append(sSil.n)
                                for _pad in range(nb, 8):
                                    silu_op.append(sSil.n)
                        dve.wait_ge(f1c.h, f1c.n)
                        dve.wait_ge(sSil.h, silu_op[8 * mg + nb - 1])
                        if mg >= 2:
                            hsw.wait(dve, hsw_base + mg - 2)
                        for h_ in range(ntile):
                            for qq in range(2):
                                b_ = 2 * h_ + qq
                                if qq == 0:
                                    o_ = hT[:, (HGOFF[mg] + h_) * 512:
                                            (HGOFF[mg] + h_ + 1) * 512]
                                else:
                                    o_ = hs1[mg % 2][:, h_ * 512:(h_ + 1) * 512]
                                dve.tensor_tensor(
                                    out=o_,
                                    in0=hstmp[:, b_ * 512:(b_ + 1) * 512],
                                    in1=fps[:, b_ * 512:(b_ + 1) * 512],
                                    op=ALU.mult).then_inc(hmul.inc(), 1)
                                hmul_op.append(hmul.n)
                        for _pad in range(nb, 8):
                            hmul_op.append(hmul.n)
                        gp.wait_ge(hmul.h, hmul_op[8 * mg + nb - 1])
                        hsw.load(gp,
                                 bass.AP(hsc, HGOFF[mg] * 128 * 512,
                                         [[512, 128], [128 * 512, ntile], [1, 512]]),
                                 bass.AP(hs1[mg % 2], 0,
                                         [[2048, 128], [512, ntile], [1, 512]]))
                    hsw.wait(gp, len(hsw.tgt) - 1)
                    hsw.wait(gp, len(hsw.tgt) - 2)
                    nc.all_engine_barrier()

                for lq in range(2):
                    q = half * 2 + lq
                    hrl_base = len(hrl.tgt)
                    if lq == 1:
                        for c_ in range(8):
                            k0 = c_ * 11
                            nk = min(11, NH - k0)
                            hrl.load(gp,
                                     bass.AP(hT, k0 * 512,
                                             [[NH * 512, 128], [512, nk], [1, 512]]),
                                     bass.AP(hsc, k0 * 128 * 512,
                                             [[512, 128], [128 * 512, nk], [1, 512]]))
                    with ExitStack() as f2s:
                        wsl2 = [f2s.enter_context(sbt(f"wsl2_{i}", [128, 1024], BF16))
                                for i in range(8)]
                        accb = [f2s.enter_context(sbt(f"accb{i}", [128, 512], F32))
                                for i in range(4)]
                        asmb = [f2s.enter_context(sbt(f"asmb{i}", [128, 512], BF16))
                                for i in range(4)]

                        f2c_b = f2c.n
                        sF2_b = sF2d.n
                        f2_base = len(fws.tgt)
                        acc_base = len(accl.tgt)
                        asw_base = len(asws.tgt)

                        def f2load(li2):
                            dgp, k = divmod(li2, NH)
                            if li2 >= 8:
                                sy.wait_ge(f2c.h, f2c_b + li2 - 7)
                            ci = min(k // 16, len(W2CH) - 1)
                            if k == W2CH[ci][0]:
                                for dt in range(8 * dgp, 8 * dgp + 8):
                                    h_, v_ = wtgt(uid_2(dt, ci))
                                    sy.wait_ge(h_, v_)
                            fws.load(sy,
                                     bass.AP(wsl2[(f2_base + li2) % 8], 0,
                                             [[1024, 128], [1, 1024]]),
                                     w2d.ap()[1024 * dgp:1024 * (dgp + 1),
                                              128 * k:128 * (k + 1)],
                                     transpose=True)

                        def accload(m):
                            if m >= 4:
                                gp.wait_ge(sF2d.h, sF2_b + m - 3)
                            accl.load(gp,
                                      bass.AP(accb[(acc_base + m) % 4], 0,
                                              [[512, 128], [1, 512]]),
                                      bass.AP(accs, 128 * m * TCORE + q * 512,
                                              [[TCORE, 128], [1, 512]]))

                        for li2 in range(8):
                            f2load(li2)
                        for m in range(4):
                            accload(m)
                        li2 = 0
                        for dgp in range(4):
                            for k in range(NH):
                                if lq == 1 and dgp == 0 and k % 11 == 0:
                                    hrl.wait(te, hrl_base + k // 11)
                                if k % 4 == 0:
                                    dqfeed(1)
                                if li2 + 8 < 4 * NH:
                                    f2load(li2 + 8)
                                fws.wait(te, f2_base + li2)
                                wtile = wsl2[(f2_base + li2) % 8]
                                for j8 in range(8):
                                    if k == 0 and dgp >= 1:
                                        te.wait_ge(sF2d.h, sF2_b + 8 * (dgp - 1) + j8 + 1)
                                    mm = te.matmul(
                                        fps[:, j8 * 512:(j8 + 1) * 512],
                                        wtile[:, j8 * 128:(j8 + 1) * 128],
                                        hT[:, k * 512:(k + 1) * 512],
                                        start=(k == 0), stop=(k == NH - 1))
                                mm.then_inc(f2c.inc(), 1)
                                li2 += 1
                            for j8 in range(8):
                                m = 8 * dgp + j8
                                if m + 4 < 32:
                                    accload(m + 4)
                                dve.wait_ge(f2c.h, f2c_b + (dgp + 1) * NH)
                                accl.wait(dve, acc_base + m)
                                if m >= 4:
                                    asws.wait(dve, asw_base + m - 4)
                                dve.tensor_tensor(out=asmb[m % 4][:, :],
                                                  in0=fps[:, j8 * 512:(j8 + 1) * 512],
                                                  in1=accb[(acc_base + m) % 4][:, :],
                                                  op=ALU.add).then_inc(sF2d.inc(), 1)
                                gp.wait_ge(sF2d.h, sF2d.n)
                                asws.load(gp,
                                          bass.AP(asms, 128 * m * TCORE + q * 512,
                                                  [[TCORE, 128], [1, 512]]),
                                          asmb[m % 4][:, :])
                        for i_ in range(4):
                            asws.wait(gp, len(asws.tgt) - 1 - i_)
                        nc.all_engine_barrier()

                    with ExitStack() as asx:
                        abuf = [asx.enter_context(sbt(f"abuf{i}", [128, DM], BF16))
                                for i in range(2)]
                        xb2 = [asx.enter_context(sbt(f"xb2{i}", [128, DM], F32))
                               for i in range(2)]
                        ob = [asx.enter_context(sbt(f"ob{i}", [128, DM], F32))
                              for i in range(2)]
                        ab_base = len(abls.tgt)
                        ax_base = len(axls.tgt)
                        st_base = len(stws.tgt)
                        f3add = []
                        for j in range(4):
                            s = j % 2
                            if j >= 2:
                                sy.wait_ge(sF2d.h, f3add[j - 2])
                                sc.wait_ge(sF2d.h, f3add[j - 2])
                            abls.load(sy, abuf[s][:, :],
                                      asms.ap()[0:DM, q * 512 + 128 * j:
                                                q * 512 + 128 * (j + 1)],
                                      transpose=True)
                            axls.load(sc, xb2[s][:, :],
                                      x.ap()[1 + q * 512 + 128 * j:
                                             1 + q * 512 + 128 * (j + 1), :])
                            abls.wait(dve, ab_base + j)
                            axls.wait(dve, ax_base + j)
                            if j >= 2:
                                stws.wait(dve, st_base + j - 2)
                            dve.tensor_tensor(out=ob[s][:, :], in0=abuf[s][:, :],
                                              in1=xb2[s][:, :], op=ALU.add
                                              ).then_inc(sF2d.inc(), 1)
                            f3add.append(sF2d.n)
                            gp.wait_ge(sF2d.h, sF2d.n)
                            stws.load(gp, out.ap()[q * 512 + 128 * j:
                                                   q * 512 + 128 * (j + 1), :],
                                      ob[s][:, :])
                        stws.wait(gp, len(stws.tgt) - 1)
                        stws.wait(gp, len(stws.tgt) - 2)
                        nc.all_engine_barrier()
                    if upto == "ffn0" and q == 0:
                        return nc

    for i_ in range(2):
        stws.wait(gp, len(stws.tgt) - 1 - i_)
    est.close()
    return nc


_cache = {}


def kernel(L0_attn_inputs, L0_attn_conv_w, L0_attn_conv_b, L0_attn_wq, L0_attn_bq,
           L0_attn_wk, L0_attn_bk, L0_attn_wv, L0_attn_bv, L0_attn_wo, L0_attn_bo,
           L0_attn_proj_w, L0_attn_proj_b, L0_ffn_w1, L0_ffn_s1, L0_ffn_w2,
           L0_ffn_s2, L0_ffn_w3, L0_ffn_s3):
    xf = np.asarray(L0_attn_inputs, dtype=np.float32).reshape(16384, DM)
    zero = np.zeros((1, DM), np.float32)
    pad = np.zeros((ROWS - TCORE - 2, DM), np.float32)

    common = {
        "convw": np.ascontiguousarray(np.asarray(L0_attn_conv_w, np.float32)),
        "convb": np.asarray(L0_attn_conv_b, np.float32).reshape(1, H3),
        "wq": np.ascontiguousarray(np.asarray(L0_attn_wq, np.float32).reshape(WQK, WQK)),
        "bq": np.asarray(L0_attn_bq, np.float32).reshape(1, WQK),
        "wk": np.ascontiguousarray(np.asarray(L0_attn_wk, np.float32).reshape(WQK, WQK)),
        "bk": np.asarray(L0_attn_bk, np.float32).reshape(1, WQK),
        "wv": np.ascontiguousarray(np.asarray(L0_attn_wv, np.float32).reshape(WQK, WQK)),
        "bv": np.asarray(L0_attn_bv, np.float32).reshape(1, WQK),
        "wo": np.ascontiguousarray(np.asarray(L0_attn_wo, np.float32).reshape(WQK, WQK)),
        "bo": np.asarray(L0_attn_bo, np.float32).reshape(1, WQK),
        "pw": np.ascontiguousarray(np.asarray(L0_attn_proj_w, np.float32)),
        "pb": np.asarray(L0_attn_proj_b, np.float32).reshape(1, DM),
        "w1": np.ascontiguousarray(np.asarray(L0_ffn_w1, np.int32)),
        "s1": np.asarray(L0_ffn_s1, np.float32).reshape(1, 704512),
        "w2": np.ascontiguousarray(np.asarray(L0_ffn_w2, np.int32)),
        "s2": np.asarray(L0_ffn_s2, np.float32).reshape(1, 704512),
        "w3": np.ascontiguousarray(np.asarray(L0_ffn_w3, np.int32)),
        "s3": np.asarray(L0_ffn_s3, np.float32).reshape(1, 704512),
        "identin": np.eye(128, dtype=ml_dtypes.bfloat16),
        "maskin": np.tile(np.where(np.tril(np.ones((128, 128), bool)), 0.0,
                                   np.float32(-1e9)).astype(np.float32), (1, 4)),
    }

    in_maps = []
    for c in range(NCORE):
        lo = c * TCORE
        left = zero if lo % 8192 == 0 else xf[lo - 1:lo]
        hi = lo + TCORE
        right = zero if hi % 8192 == 0 else xf[hi:hi + 1]
        shard = np.ascontiguousarray(
            np.concatenate([left, xf[lo:hi], right, pad], axis=0))
        m = dict(common)
        m["x"] = shard
        in_maps.append(m)

    key = "prog"
    if key not in _cache:
        nc = bass.Bass("TRN2", target_bir_lowering=False)
        build(nc)
        _cache[key] = nc
    nc = _cache[key]

    res = run_bass_kernel_spmd(nc, in_maps, core_ids=list(range(NCORE)))
    outs = [res.results[c]["out"] for c in range(NCORE)]
    return np.concatenate(outs, axis=0).reshape(2, 8192, DM)




# version 09
# speedup vs baseline: 1.1786x; 1.0419x over previous
import numpy as np
import ml_dtypes
from contextlib import ExitStack

import concourse.bass as bass
import concourse.mybir as mybir
from concourse.bass_utils import run_bass_kernel_spmd

F32 = mybir.dt.float32
BF16 = mybir.dt.bfloat16
I32 = mybir.dt.int32
AF = mybir.ActivationFunctionType
ALU = mybir.AluOpType
AX = mybir.AxisListType.X

NCORE = 8
TCORE = 2048
ROWS = 2176
DM = 4096
ND = DM // 128
WQK = 1024
NW = 8
H3 = 3072
N3 = 24
HID = 11008
NH = 86
NHALF = 1024
NQ = 512

HG = [4] * 21 + [2]
HGOFF = [0]
for _s in HG:
    HGOFF.append(HGOFF[-1] + _s)
W2CH = [(0, 16), (16, 16), (32, 16), (48, 16), (64, 16), (80, 6)]

DBG = False
PH_LN = True
PH_DQ = True


class Ctr:

    def __init__(self, nc, name):
        self.h = nc.alloc_semaphore(name)
        self.n = 0

    def inc(self, k=1):
        self.n += k
        return self.h


class Slots:

    def __init__(self, nc, name, n):
        self.sems = [Ctr(nc, f"{name}{i}") for i in range(n)]
        self.n = n
        self.tgt = []

    def load(self, q, out_ap, in_ap, **kw):
        i = len(self.tgt)
        s = self.sems[i % self.n]
        q.dma_start(out_ap, in_ap, **kw).then_inc(s.inc(16), 16)
        self.tgt.append((s.h, s.n))
        return i

    def wait(self, eng, i):
        h, v = self.tgt[i]
        eng.wait_ge(h, v)


def build(nc, upto="all"):
    x = nc.dram_tensor("x", [ROWS, DM], F32, kind="ExternalInput")
    convw = nc.dram_tensor("convw", [3, DM, H3], F32, kind="ExternalInput")
    convb = nc.dram_tensor("convb", [1, H3], F32, kind="ExternalInput")
    wq = nc.dram_tensor("wq", [WQK, WQK], F32, kind="ExternalInput")
    wk = nc.dram_tensor("wk", [WQK, WQK], F32, kind="ExternalInput")
    wv = nc.dram_tensor("wv", [WQK, WQK], F32, kind="ExternalInput")
    bq = nc.dram_tensor("bq", [1, WQK], F32, kind="ExternalInput")
    bk = nc.dram_tensor("bk", [1, WQK], F32, kind="ExternalInput")
    bv = nc.dram_tensor("bv", [1, WQK], F32, kind="ExternalInput")
    wo = nc.dram_tensor("wo", [WQK, WQK], F32, kind="ExternalInput")
    bo = nc.dram_tensor("bo", [1, WQK], F32, kind="ExternalInput")
    pw = nc.dram_tensor("pw", [WQK, DM], F32, kind="ExternalInput")
    pb = nc.dram_tensor("pb", [1, DM], F32, kind="ExternalInput")
    w1 = nc.dram_tensor("w1", [352256, 64], I32, kind="ExternalInput")
    w2 = nc.dram_tensor("w2", [352256, 64], I32, kind="ExternalInput")
    w3 = nc.dram_tensor("w3", [352256, 64], I32, kind="ExternalInput")
    s1 = nc.dram_tensor("s1", [1, 704512], F32, kind="ExternalInput")
    s2 = nc.dram_tensor("s2", [1, 704512], F32, kind="ExternalInput")
    s3 = nc.dram_tensor("s3", [1, 704512], F32, kind="ExternalInput")
    identin = nc.dram_tensor("identin", [128, 128], BF16, kind="ExternalInput")
    maskin = nc.dram_tensor("maskin", [128, 512], F32, kind="ExternalInput")
    out = nc.dram_tensor("out", [TCORE, DM], F32, kind="ExternalOutput")

    scratch_kind = "ExternalOutput" if DBG else "Internal"
    xnsc = nc.dram_tensor("xnsc", [ROWS, DM], BF16, kind=scratch_kind)
    w1d = nc.dram_tensor("w1d", [HID, DM], BF16, kind=scratch_kind)
    w3d = nc.dram_tensor("w3d", [HID, DM], BF16, kind=scratch_kind)
    w2d = nc.dram_tensor("w2d", [DM, HID], BF16, kind=scratch_kind)
    accs = nc.dram_tensor("accs", [DM, TCORE], F32, kind=scratch_kind)
    asms = nc.dram_tensor("asms", [DM, TCORE], BF16, kind=scratch_kind)
    hsc = nc.dram_tensor("hsc", [HID, 512], BF16, kind=scratch_kind)

    sy = nc.sync
    sc = nc.scalar
    gp = nc.gpsimd
    dve = nc.vector
    act = nc.scalar
    pool = nc.gpsimd
    te = nc.tensor

    _uid = [0]

    def sbt(name, shape, dtype):
        _uid[0] += 1
        return nc.sbuf_tensor(f"{name}_{_uid[0]}", shape, dtype)

    def pst(name, shape, dtype):
        _uid[0] += 1
        return nc.psum_tensor(f"{name}_{_uid[0]}", shape, dtype)

    allsems = []

    def C(name):
        c = Ctr(nc, name)
        allsems.append(c)
        return c

    def SL(name, n):
        s = Slots(nc, name, n)
        allsems.extend(s.sems)
        return s

    ini = SL("ini", 1)
    inim = SL("inim", 1)
    xls = SL("xls", 2)
    lnv = C("lnv")
    lna = C("lna")
    lnq = C("lnq")
    lnw = SL("lnw", 2)
    pk_v = SL("pkv", 4)
    pk_p = SL("pkp", 4)
    dqv = C("dqv")
    dqp = C("dqp")
    dwr_v = SL("dwrv", 2)
    dwr_p = SL("dwrp", 2)
    xnb = SL("xnb", 1)
    cwsl = SL("cws", 2)
    cbsl = SL("cbs", 1)
    ccp = C("ccp")
    cce = C("cce")
    qwsl = SL("qws", 2)
    qkp = C("qkp")
    qke = C("qke")
    sSf = C("sSf")
    sA = C("sA"); sB = C("sB"); sCx = C("sCx"); sD = C("sD")
    sE = C("sE"); sF = C("sF"); sG = C("sG"); sH = C("sH")
    sR = C("sR")
    wop = C("wop"); woe = C("woe")
    pjp = C("pjp"); pje = C("pje")
    aws = SL("aws", 4)
    fws = SL("fws", 8)
    f1c = C("f1c")
    f2c = C("f2c")
    sSil = C("sSil"); hmul = C("hmul")
    sF2d = C("sF2d")
    accl = SL("accl", 4)
    asws = SL("asws", 4)
    hsw = SL("hsw", 2)
    hrl = SL("hrl", 8)
    abls = SL("abls", 2)
    axls = SL("axls", 2)
    stws = SL("stws", 2)

    est = ExitStack()
    ident = est.enter_context(sbt("ident", [128, 128], BF16))
    mask4 = est.enter_context(sbt("mask4", [128, 512], BF16))
    ones = est.enter_context(sbt("ones", [1, 1024], BF16))

    for c in allsems:
        pool.sem_clear(c.h)
    nc.all_engine_barrier()
    nc.all_engine_barrier()

    ini.load(sy, ident[:, :], identin.ap())
    inim.load(gp, mask4[:, :], maskin.ap())
    dve.memset(ones[:, :], 1.0)

    with ExitStack() as ph:
        xbuf = [ph.enter_context(sbt(f"xbuf{i}", [128, DM], F32)) for i in range(2)]
        xsq = ph.enter_context(sbt("xsq", [128, DM], F32))
        xnt = [ph.enter_context(sbt(f"xnt{i}", [128, DM], BF16)) for i in range(2)]
        st8 = ph.enter_context(sbt("st8", [128, 8], F32))

        lna_copy = []
        lnv_sq = []
        for g in range(17):
            s = g % 2
            if g >= 2:
                sc.wait_ge(lna.h, lna_copy[g - 2])
            li = xls.load(sc, xbuf[s][:, :], x.ap()[128 * g:128 * (g + 1), :])
            xls.wait(dve, li)
            dve.reduce_sum(st8[:, 0:1], xbuf[s][:, :], axis=AX).then_inc(lnv.inc(), 1)
            dve.wait_ge(lnv.h, lnv.n)
            xls.wait(act, li)
            if g >= 1:
                act.wait_ge(lnv.h, lnv_sq[g - 1])
            act.activation(xsq[:, :], xbuf[s][:, :], AF.Square).then_inc(lnq.inc(), 1)
            dve.wait_ge(lnq.h, lnq.n)
            dve.reduce_sum(st8[:, 1:2], xsq[:, :], axis=AX).then_inc(lnv.inc(), 1)
            lnv_sq.append(lnv.n)
            dve.wait_ge(lnv.h, lnv.n)
            dve.tensor_single_scalar(out=st8[:, 2:3], in_=st8[:, 0:1],
                                     scalar=1.0 / DM, op=ALU.mult).then_inc(lnv.inc(), 1)
            dve.tensor_single_scalar(out=st8[:, 3:4], in_=st8[:, 1:2],
                                     scalar=1.0 / DM, op=ALU.mult).then_inc(lnv.inc(), 1)
            dve.wait_ge(lnv.h, lnv.n)
            dve.tensor_tensor(out=st8[:, 4:5], in0=st8[:, 2:3], in1=st8[:, 2:3],
                              op=ALU.mult).then_inc(lnv.inc(), 1)
            dve.wait_ge(lnv.h, lnv.n)
            dve.scalar_tensor_tensor(out=st8[:, 5:6], in0=st8[:, 4:5], scalar=-1.0,
                                     in1=st8[:, 3:4], op0=ALU.mult, op1=ALU.add
                                     ).then_inc(lnv.inc(), 1)
            dve.wait_ge(lnv.h, lnv.n)
            dve.tensor_single_scalar(out=st8[:, 5:6], in_=st8[:, 5:6],
                                     scalar=1e-5, op=ALU.add).then_inc(lnv.inc(), 1)
            act.wait_ge(lnv.h, lnv.n)
            act.activation(st8[:, 6:7], st8[:, 5:6], AF.Ln).then_inc(lna.inc(), 1)
            act.wait_ge(lna.h, lna.n)
            act.activation(st8[:, 6:7], st8[:, 6:7], AF.Exp, scale=-0.5
                           ).then_inc(lna.inc(), 1)
            dve.wait_ge(lna.h, lna.n)
            dve.scalar_tensor_tensor(out=st8[:, 7:8], in0=st8[:, 2:3], scalar=-1.0,
                                     in1=st8[:, 6:7], op0=ALU.mult, op1=ALU.mult
                                     ).then_inc(lnv.inc(), 1)
            act.wait_ge(lnv.h, lnv.n)
            if g >= 2:
                _, v = lnw.tgt[g - 2]
                act.wait_ge(lnw.tgt[g - 2][0], v)
            act.activation(xnt[s][:, :], xbuf[s][:, :], AF.Identity,
                           bias=st8[:, 7:8], scale=st8[:, 6:7]).then_inc(lna.inc(), 1)
            lna_copy.append(lna.n)
            gp.wait_ge(lna.h, lna.n)
            lnw.load(gp, xnsc.ap()[128 * g:128 * (g + 1), :], xnt[s][:, :])
    if upto == "ln":
        lnw.wait(gp, 15)
        lnw.wait(gp, 16)
        ini.wait(sy, 0)
        inim.wait(sy, 0)
        nc.all_engine_barrier()
        return nc

    units = []
    for kt in range(NH):
        for which in (0, 1):
            for dh in (0, 1):
                units.append(("w13", (kt, which, dh)))
    for dt in range(ND):
        for ci in range(len(W2CH)):
            units.append(("w2", (dt, ci)))
    NU = len(units)

    def uid_w(kt, which, dh):
        return kt * 4 + which * 2 + dh

    def uid_2(dt, ci):
        return NH * 4 + dt * len(W2CH) + ci

    def uparams(u):
        kind, idx = u
        if kind == "w13":
            kt, which, dh = idx
            return dict(src=(w1 if which == 0 else w3),
                        ssrc=(s1 if which == 0 else s3),
                        dst=(w1d if which == 0 else w3d),
                        pkoff=kt * 128 * 2048 + dh * 1024, pkw=1024,
                        ppitch=2048, mcnt=16,
                        soff=kt * 128 * 64 + 32 * dh, spp=64, scnt=32,
                        dstoff=kt * 128 * DM + dh * 2048, dpitch=DM, dw=2048)
        dt, ci = idx
        m0, mc = W2CH[ci]
        return dict(src=w2, ssrc=s2, dst=w2d,
                    pkoff=(dt * 128 * 86 + m0) * 64, pkw=mc * 64, ppitch=86 * 64,
                    mcnt=mc, soff=dt * 128 * 172 + 2 * m0, spp=172, scnt=2 * mc,
                    dstoff=dt * 128 * HID + m0 * 128, dpitch=HID, dw=mc * 128)

    def wtgt(u):
        return (dwr_v.sems[u % 2].h, 16 * (u // 2 + 1))

    pkb = [est.enter_context(sbt(f"pkb{i}", [128, 1024], I32)) for i in range(2)]
    scb = [est.enter_context(sbt(f"scqb{i}", [128, 32], F32)) for i in range(2)]
    lsbt = est.enter_context(sbt("lsbt", [128, 1024], I32))
    outb = [est.enter_context(sbt(f"dqo{i}", [128, 2048], BF16)) for i in range(2)]

    dqst = {"load": 0, "comp": 0}

    def dq_load1():
        j = dqst["load"]
        if j >= NU:
            return
        if j == 0:
            for gi_ in (15, 16):
                gp.wait_ge(lnw.tgt[gi_][0], lnw.tgt[gi_][1])
        u = uparams(units[j])
        s = j % 2
        if j >= 2:
            gp.wait_ge(dqv.h, 4 * (j - 1))
        pk_v.load(gp, bass.AP(pkb[s], 0, [[1024, 128], [1, u["pkw"]]]),
                  bass.AP(u["src"], u["pkoff"], [[u["ppitch"], 128], [1, u["pkw"]]]))
        pk_v.load(gp, bass.AP(scb[s], 0, [[32, 128], [1, u["scnt"]]]),
                  bass.AP(u["ssrc"], u["soff"], [[u["spp"], 128], [1, u["scnt"]]]))
        dqst["load"] = j + 1

    def dqfeed(n):
        for _ in range(n):
            j = dqst["comp"]
            if j >= NU:
                return
            while dqst["load"] < min(j + 2, NU):
                dq_load1()
            u = uparams(units[j])
            s = j % 2
            ob = outb[s]
            mc = u["mcnt"]
            if j == 0:
                for gi_ in (15, 16):
                    dve.wait_ge(lnw.tgt[gi_][0], lnw.tgt[gi_][1])
            pk_v.wait(dve, 2 * j)
            pk_v.wait(dve, 2 * j + 1)
            if j >= 1:
                dve.wait_ge(dqv.h, dqv.n)
            if j >= 2:
                dwr_v.wait(dve, j - 2)
            dve.tensor_single_scalar(out=lsbt[:, 0:u["pkw"]], in_=pkb[s][:, 0:u["pkw"]],
                                     scalar=4, op=ALU.arith_shift_right
                                     ).then_inc(dqv.inc(), 1)
            dve.wait_ge(dqv.h, dqv.n)
            dve.tensor_tensor(
                out=bass.AP(ob, 0, [[2048, 128], [128, mc], [1, 64]]),
                in0=bass.AP(lsbt, 0, [[1024, 128], [64, mc], [1, 64]]),
                in1=bass.AP(scb[s], 0, [[32, 128], [2, mc], [0, 64]]),
                op=ALU.mult).then_inc(dqv.inc(), 1)
            dve.wait_ge(dqv.h, dqv.n)
            dve.tensor_scalar(out=lsbt[:, 0:u["pkw"]], in0=pkb[s][:, 0:u["pkw"]],
                              scalar1=15, scalar2=8, op0=ALU.bitwise_and,
                              op1=ALU.bitwise_xor).then_inc(dqv.inc(), 1)
            dve.wait_ge(dqv.h, dqv.n)
            dve.scalar_tensor_tensor(
                out=bass.AP(ob, 64, [[2048, 128], [128, mc], [1, 64]]),
                in0=bass.AP(lsbt, 0, [[1024, 128], [64, mc], [1, 64]]),
                scalar=8.0,
                in1=bass.AP(scb[s], 1, [[32, 128], [2, mc], [0, 64]]),
                op0=ALU.subtract, op1=ALU.mult).then_inc(dqv.inc(), 1)
            sc.wait_ge(dqv.h, dqv.n)
            wl = dwr_v.load(sc, bass.AP(u["dst"], u["dstoff"],
                                        [[u["dpitch"], 128], [1, u["dw"]]]),
                            bass.AP(ob, 0, [[2048, 128], [1, u["dw"]]]))
            assert dwr_v.tgt[wl] == wtgt(j), (dwr_v.tgt[wl], wtgt(j))
            dqst["comp"] = j + 1
            while dqst["load"] < min(j + 3, NU):
                dq_load1()

    if upto == "dq":
        dqfeed(NU)
        dwr_v.wait(sc, len(dwr_v.tgt) - 1)
        dwr_v.wait(sc, len(dwr_v.tgt) - 2)
        nc.all_engine_barrier()
        return nc

    for half in range(2):
        t0 = half * NHALF
        with ExitStack() as hs:
            if half == 0:
                for gi_ in (15, 16):
                    act.wait_ge(lnw.tgt[gi_][0], lnw.tgt[gi_][1])
            with ExitStack() as atts:
                qhT = atts.enter_context(sbt("qhT", [128, NW * 1024], BF16))
                khT = atts.enter_context(sbt("khT", [128, NW * 1024], BF16))
                vh = atts.enter_context(sbt("vh", [128, 8 * 1024], BF16))

                with ExitStack() as cs:
                    qkvT = cs.enter_context(sbt("qkvT", [128, N3 * 1024], BF16))
                    cv = cs.enter_context(ExitStack())
                    xnT = cv.enter_context(sbt("xnTc", [128, ND * 1040], BF16))
                    cwb = [cv.enter_context(sbt(f"cwb{i}", [128, 32 * 128], BF16))
                           for i in range(2)]
                    cbsb = cv.enter_context(sbt("cbsb", [1, H3], BF16))
                    cps = cv.enter_context(pst("cps", [128, 4 * 512], F32))
                    glast = (t0 + 1040 - 1) // 128
                    for gi_ in (glast - 1, glast):
                        sy.wait_ge(lnw.tgt[gi_][0], lnw.tgt[gi_][1])
                    ini.wait(sy, 0)
                    inim.wait(sy, 0)
                    for k in range(ND):
                        xnb.load(sy, bass.AP(xnT, k * 1040, [[ND * 1040, 128], [1, 1040]]),
                                 xnsc.ap()[t0:t0 + 1040, 128 * k:128 * (k + 1)],
                                 transpose=True)
                    xnb_done = xnb.tgt[-1]
                    cbsl.load(gp, cbsb[:, :], convb.ap())
                    ccp_b = ccp.n
                    cce_b = cce.n

                    def cload(c):
                        b, k3 = divmod(c, 3)
                        if c >= 2:
                            gp.wait_ge(ccp.h, ccp_b + c - 1)
                        return cwsl.load(
                            gp,
                            bass.AP(cwb[(len(cwsl.tgt)) % 2], 0,
                                    [[32 * 128, 128], [128, 32], [1, 128]]),
                            bass.AP(convw, k3 * DM * H3 + 128 * b,
                                    [[H3, 128], [128 * H3, 32], [1, 128]]))

                    cw_base = len(cwsl.tgt)
                    cload(0)
                    cload(1)
                    te.wait_ge(xnb_done[0], xnb_done[1])
                    cbsl.wait(te, len(cbsl.tgt) - 1)
                    for b in range(N3):
                        bp = b % 2
                        if b >= 2:
                            te.wait_ge(cce.h, cce_b + b - 1)
                        for k3 in range(3):
                            c = b * 3 + k3
                            if c + 2 < N3 * 3:
                                cload(c + 2)
                            cwsl.wait(te, cw_base + c)
                            sbuf_w = cwb[(cw_base + c) % 2]
                            if k3 == 0:
                                for nh in range(2):
                                    te.matmul(cps[:, (2 * bp + nh) * 512:(2 * bp + nh + 1) * 512],
                                              cbsb[:, 128 * b:128 * (b + 1)],
                                              ones[:, 0:512], start=True, stop=False)
                            for a in range(ND):
                                for nh in range(2):
                                    mm = te.matmul(
                                        cps[:, (2 * bp + nh) * 512:(2 * bp + nh + 1) * 512],
                                        sbuf_w[:, a * 128:(a + 1) * 128],
                                        xnT[:, a * 1040 + k3 + nh * 512:
                                            a * 1040 + k3 + nh * 512 + 512],
                                        start=False, stop=(k3 == 2 and a == ND - 1))
                            mm.then_inc(ccp.inc(), 1)
                        act.wait_ge(ccp.h, ccp_b + 3 * (b + 1))
                        act.activation(qkvT[:, b * 1024:(b + 1) * 1024],
                                       cps[:, 2 * bp * 512:(2 * bp + 2) * 512],
                                       AF.Copy).then_inc(cce.inc(), 1)
                        dqfeed(2)
                    nc.all_engine_barrier()
                    cv.close()

                    with ExitStack() as qs:
                        qwb = [qs.enter_context(sbt(f"qwb{i}", [128, 8 * 1024], BF16))
                               for i in range(2)]
                        bql = qs.enter_context(sbt("bql", [1, 3 * WQK], BF16))
                        qps = qs.enter_context(pst("qps", [128, 4 * 512], F32))
                        cbsl.load(gp, bql[:, 0:WQK], bq.ap())
                        cbsl.load(gp, bql[:, WQK:2 * WQK], bk.ap())
                        cbsl.load(gp, bql[:, 2 * WQK:3 * WQK], bv.ap())
                        qkp_b = qkp.n
                        qke_b = qke.n
                        qw_base = len(qwsl.tgt)

                        def qwload(i, wt):
                            if i >= 2:
                                gp.wait_ge(qkp.h, qkp_b + 8 * (i - 1))
                            qwsl.load(gp,
                                      bass.AP(qwb[(qw_base + i) % 2], 0,
                                              [[8 * 1024, 128], [1024, 8], [1, 1024]]),
                                      bass.AP(wt, 0, [[WQK, 128], [128 * WQK, 8], [1, WQK]]))

                        qwload(0, wq)
                        qwload(1, wk)
                        cbsl.wait(te, len(cbsl.tgt) - 1)
                        for i, (srcoff, dstt, scl) in enumerate(
                                [(0, qhT, 0.125), (8, khT, 1.0), (16, vh, 1.0)]):
                            if i == 2:
                                qwload(2, wv)
                            qwsl.wait(te, qw_base + i)
                            wb = qwb[(qw_base + i) % 2]
                            for m in range(8):
                                bank = (m % 2) * 2
                                if i * 8 + m >= 2:
                                    te.wait_ge(qke.h, qke_b + i * 8 + m - 1)
                                for nh in range(2):
                                    pslice = qps[:, (bank + nh) * 512:(bank + nh + 1) * 512]
                                    if i < 2:
                                        te.matmul(pslice,
                                                  bql[:, i * WQK + 128 * m:i * WQK + 128 * (m + 1)],
                                                  ones[:, 0:512], start=True, stop=False)
                                        for a in range(8):
                                            mm = te.matmul(
                                                pslice,
                                                wb[:, a * 1024 + 128 * m:a * 1024 + 128 * (m + 1)],
                                                qkvT[:, (srcoff + a) * 1024 + nh * 512:
                                                     (srcoff + a) * 1024 + nh * 512 + 512],
                                                start=False, stop=(a == 7))
                                    else:
                                        te.matmul(pslice, ones[:, 0:128],
                                                  bql[:, 2 * WQK + nh * 512:2 * WQK + (nh + 1) * 512],
                                                  start=True, stop=False)
                                        for a in range(8):
                                            mm = te.matmul(
                                                pslice,
                                                qkvT[:, (16 + a) * 1024 + m * 128:
                                                     (16 + a) * 1024 + (m + 1) * 128],
                                                wb[:, a * 1024 + nh * 512:a * 1024 + (nh + 1) * 512],
                                                start=False, stop=(a == 7))
                                mm.then_inc(qkp.inc(), 1)
                                act.wait_ge(qkp.h, qkp.n)
                                act.activation(dstt[:, m * 1024:(m + 1) * 1024],
                                               qps[:, bank * 512:(bank + 2) * 512],
                                               AF.Copy, scale=scl).then_inc(qke.inc(), 1)
                                dqfeed(2)
                        nc.all_engine_barrier()

                with ExitStack() as bs:
                    oT = bs.enter_context(sbt("oT", [128, 8 * 1024], BF16))
                    aoT = bs.enter_context(sbt("aoT", [128, 8 * 1024], BF16))
                    smb = [bs.enter_context(sbt(f"smb{i}", [128, 512], F32)) for i in range(2)]
                    pbf = [bs.enter_context(sbt(f"pbf{i}", [128, 512], BF16)) for i in range(2)]
                    pnb = [bs.enter_context(sbt(f"pnb{i}", [128, 512], BF16)) for i in range(2)]
                    pTb = [bs.enter_context(sbt(f"pTb{i}", [128, 512], BF16)) for i in range(2)]
                    nmb = bs.enter_context(sbt("nmb", [128, 16], F32))
                    Sps = [bs.enter_context(pst(f"Sps{i}", [128, 512], F32)) for i in range(2)]
                    Tps = [bs.enter_context(pst(f"Tps{i}", [128, 512], BF16)) for i in range(2)]
                    Ops = [bs.enter_context(pst(f"Ops{i}", [128, 1024], F32)) for i in range(2)]
                    sA_b = sA.n; sB_b = sB.n; sC_b = sCx.n; sD_b = sD.n
                    sE_b = sE.n; sF_b = sF.n; sG_b = sG.n; sH_b = sH.n; sSf_b = sSf.n
                    for b in range(8):
                        for g in range(4):
                            bg = b * 4 + g
                            par = bg % 2
                            if bg >= 2:
                                te.wait_ge(sB.h, sB_b + 2 * (bg - 2) + 2)
                            for hh in range(4):
                                h = 4 * g + hh
                                i = h // 2
                                pb_ = 64 * (h % 2)
                                te.matmul(Sps[par][:, hh * 128:(hh + 1) * 128],
                                          ident[:, :],
                                          mask4[:, hh * 128:(hh + 1) * 128],
                                          start=True, stop=False)
                                mm = te.matmul(
                                    Sps[par][:, hh * 128:(hh + 1) * 128],
                                    qhT[pb_:pb_ + 64, i * 1024 + 128 * b:i * 1024 + 128 * (b + 1)],
                                    khT[pb_:pb_ + 64, i * 1024 + 128 * b:i * 1024 + 128 * (b + 1)],
                                    start=False, stop=True)
                            mm.then_inc(sA.inc(2), 2)
                            sa_grp = sA.n
                            dve.wait_ge(sA.h, sa_grp)
                            dve.reduce_max(bass.AP(nmb, 4 * par, [[16, 128], [1, 4]]),
                                           bass.AP(Sps[par], 0, [[512, 128], [128, 4], [1, 128]]),
                                           axis=AX, negate=True).then_inc(sB.inc(), 1)
                            dve.wait_ge(sB.h, sB.n)
                            if bg >= 2:
                                dve.wait_ge(sCx.h, sC_b + bg - 1)
                            dve.tensor_tensor(
                                out=smb[par][:, :], in0=Sps[par][:, :],
                                in1=bass.AP(nmb, 4 * par, [[16, 128], [1, 4], [0, 128]]),
                                op=ALU.add).then_inc(sB.inc(), 1)
                            act.wait_ge(sB.h, sB_b + 2 * bg + 2)
                            if bg >= 2:
                                act.wait_ge(sD.h, sD_b + 5 * (bg - 2) + 5)
                            act.activation(pbf[par][:, :], smb[par][:, :], AF.Exp
                                           ).then_inc(sCx.inc(), 1)
                            dve.wait_ge(sCx.h, sC_b + bg + 1)
                            dve.reduce_sum(bass.AP(nmb, 8 + 4 * par, [[16, 128], [1, 4]]),
                                           bass.AP(pbf[par], 0, [[512, 128], [128, 4], [1, 128]]),
                                           axis=AX).then_inc(sD.inc(), 1)
                            dve.wait_ge(sD.h, sD.n)
                            dve.reciprocal(bass.AP(nmb, 8 + 4 * par, [[16, 128], [1, 4]]),
                                           bass.AP(nmb, 8 + 4 * par, [[16, 128], [1, 4]])
                                           ).then_inc(sR.inc(), 1)
                            act.wait_ge(sR.h, sR.n)
                            if bg >= 2:
                                act.wait_ge(sE.h, sE_b + bg - 1)
                            for hh in range(4):
                                act.activation(
                                    pnb[par][:, hh * 128:(hh + 1) * 128],
                                    pbf[par][:, hh * 128:(hh + 1) * 128],
                                    AF.Identity,
                                    scale=bass.AP(nmb, 8 + 4 * par + hh, [[16, 128], [1, 1]]),
                                ).then_inc(sD.inc(), 1)
                            te.wait_ge(sD.h, sD_b + 5 * bg + 5)
                            if bg >= 2:
                                te.wait_ge(sF.h, sF_b + bg - 1)
                            for hh in range(4):
                                mm = te.transpose(Tps[par][:, hh * 128:(hh + 1) * 128],
                                                  pnb[par][:, hh * 128:(hh + 1) * 128],
                                                  ident[:, :])
                            mm.then_inc(sE.inc(), 1)
                            act.wait_ge(sE.h, sE_b + bg + 1)
                            if bg >= 2:
                                act.wait_ge(sG.h, sG_b + 2 * (bg - 2) + 2)
                            act.activation(pTb[par][:, :], Tps[par][:, :], AF.Copy
                                           ).then_inc(sF.inc(), 1)
                            te.wait_ge(sF.h, sF_b + bg + 1)
                            if g == 0 and b >= 2:
                                te.wait_ge(sH.h, sH_b + b - 1)
                            for idx, hh in enumerate((0, 2, 1, 3)):
                                h = 4 * g + hh
                                i = h // 2
                                pb_ = 64 * (h % 2)
                                mm = te.matmul(
                                    Ops[b % 2][pb_:pb_ + 64, i * 128:(i + 1) * 128],
                                    vh[:, b * 1024 + h * 64:b * 1024 + (h + 1) * 64],
                                    pTb[par][:, hh * 128:(hh + 1) * 128],
                                    start=True, stop=True)
                                if idx == 1:
                                    mm.then_inc(sG.inc(), 1)
                                    te.wait_ge(sG.h, sG.n)
                            mm.then_inc(sG.inc(), 1)
                            if bg % 2 == 0:
                                dqfeed(1)
                        dve.wait_ge(sG.h, sG_b + 8 * (b + 1))
                        dve.tensor_copy(
                            bass.AP(oT, b * 128, [[8 * 1024, 128], [1024, 8], [1, 128]]),
                            bass.AP(Ops[b % 2], 0, [[1024, 128], [128, 8], [1, 128]]),
                        ).then_inc(sH.inc(), 1)
                    nc.all_engine_barrier()

                    def wob(i):
                        return Ops[i // 2][:, (i % 2) * 512:(i % 2 + 1) * 512]

                    with ExitStack() as ws:
                        wwb = ws.enter_context(sbt("wwb", [128, 8 * 1024], BF16))
                        bolb = ws.enter_context(sbt("bolb", [1, WQK], BF16))
                        cbsl.load(gp, bolb[:, :], bo.ap())
                        wo_li = qwsl.load(
                            gp, bass.AP(wwb, 0, [[8 * 1024, 128], [1024, 8], [1, 1024]]),
                            bass.AP(wo, 0, [[WQK, 128], [128 * WQK, 8], [1, WQK]]))
                        qwsl.wait(te, wo_li)
                        cbsl.wait(te, len(cbsl.tgt) - 1)
                        wop_b = wop.n
                        woe_b = woe.n
                        for m in range(8):
                            bank = (m % 2) * 2
                            if m >= 2:
                                te.wait_ge(woe.h, woe_b + m - 1)
                            for nh in range(2):
                                pslice = wob(bank + nh)
                                te.matmul(pslice, bolb[:, 128 * m:128 * (m + 1)],
                                          ones[:, 0:512], start=True, stop=False)
                                for a in range(8):
                                    mm = te.matmul(
                                        pslice,
                                        wwb[:, a * 1024 + 128 * m:a * 1024 + 128 * (m + 1)],
                                        oT[:, a * 1024 + nh * 512:a * 1024 + (nh + 1) * 512],
                                        start=False, stop=(a == 7))
                            mm.then_inc(wop.inc(), 1)
                            act.wait_ge(wop.h, wop.n)
                            act.activation(aoT[:, m * 1024:(m + 1) * 1024],
                                           Ops[bank // 2][:, :],
                                           AF.Copy).then_inc(woe.inc(), 1)
                            dqfeed(3)
                        nc.all_engine_barrier()

                    with ExitStack() as ps_:
                        pwb = [ps_.enter_context(sbt(f"pwb{i}", [128, 8 * 128], BF16))
                               for i in range(2)]
                        pblb = ps_.enter_context(sbt("pblb", [1, DM], BF16))
                        fbx = [ps_.enter_context(sbt(f"fbx{i}", [128, 512], F32))
                               for i in range(4)]
                        cbsl.load(gp, pblb[:, :], pb.ap())
                        pjp_b = pjp.n
                        pje_b = pje.n
                        pw_base = len(qwsl.tgt)

                        def pwload(m):
                            if m >= 2:
                                gp.wait_ge(pjp.h, pjp_b + m - 1)
                            qwsl.load(gp,
                                      bass.AP(pwb[(pw_base + m) % 2], 0,
                                              [[8 * 128, 128], [128, 8], [1, 128]]),
                                      bass.AP(pw, 128 * m, [[DM, 128], [128 * DM, 8], [1, 128]]))

                        pwload(0)
                        pwload(1)
                        cbsl.wait(te, len(cbsl.tgt) - 1)
                        aw_base = len(aws.tgt)
                        for m in range(32):
                            bank = (m % 2) * 2
                            if m + 2 < 32:
                                pwload(m + 2)
                            qwsl.wait(te, pw_base + m)
                            if m >= 2:
                                te.wait_ge(pje.h, pje_b + 2 * (m - 2) + 2)
                            wbm = pwb[(pw_base + m) % 2]
                            for nh in range(2):
                                pslice = wob(bank + nh)
                                te.matmul(pslice, pblb[:, 128 * m:128 * (m + 1)],
                                          ones[:, 0:512], start=True, stop=False)
                                for a in range(8):
                                    mm = te.matmul(
                                        pslice, wbm[:, a * 128:(a + 1) * 128],
                                        aoT[:, a * 1024 + nh * 512:a * 1024 + (nh + 1) * 512],
                                        start=False, stop=(a == 7))
                            mm.then_inc(pjp.inc(), 1)
                            act.wait_ge(pjp.h, pjp.n)
                            for nh in range(2):
                                fi = 2 * (m % 2) + nh
                                if m >= 2:
                                    aws.wait(act, aw_base + 2 * (m - 2) + nh)
                                act.activation(fbx[fi][:, :], wob(bank + nh),
                                               AF.Copy).then_inc(pje.inc(), 1)
                                gp.wait_ge(pje.h, pje.n)
                                aws.load(gp,
                                         bass.AP(accs, 128 * m * TCORE + t0 + nh * 512,
                                                 [[TCORE, 128], [1, 512]]),
                                         fbx[fi][:, :])
                            dqfeed(1)
                        for i_ in range(4):
                            aws.wait(gp, len(aws.tgt) - 1 - i_)
                        nc.all_engine_barrier()
            if upto == "attn" and half == 0:
                return nc

            with ExitStack() as fhs:
                hT = fhs.enter_context(sbt("hT", [128, NH * 512], BF16))
                fps = fhs.enter_context(pst("fps", [128, 8 * 512], F32))

                with ExitStack() as f1s:
                    xnTq2 = [f1s.enter_context(sbt(f"xnTq{qq}", [128, ND * 528], BF16))
                             for qq in range(2)]
                    wsl = [f1s.enter_context(sbt(f"wsl{i}", [128, 512], BF16))
                           for i in range(8)]
                    hstmp = f1s.enter_context(sbt("hst", [128, 8 * 512], BF16))
                    hs1 = [f1s.enter_context(sbt(f"hs1_{i}", [128, 4 * 512], BF16))
                           for i in range(2)]

                    if half == 0:
                        for gi_ in (15, 16):
                            sy.wait_ge(lnw.tgt[gi_][0], lnw.tgt[gi_][1])
                            sc.wait_ge(lnw.tgt[gi_][0], lnw.tgt[gi_][1])
                    for lq in range(2):
                        q = half * 2 + lq
                        glast = (q * 512 + 528 - 1) // 128
                        for gi_ in (glast - 1, glast):
                            sy.wait_ge(lnw.tgt[gi_][0], lnw.tgt[gi_][1])
                        for k in range(ND):
                            xnb.load(sy,
                                     bass.AP(xnTq2[lq], k * 528, [[ND * 528, 128], [1, 528]]),
                                     xnsc.ap()[q * 512:q * 512 + 528, 128 * k:128 * (k + 1)],
                                     transpose=True)
                    xnbq_done = xnb.tgt[-1]

                    f1c_b = f1c.n
                    fw_base = len(fws.tgt)
                    hsw_base = len(hsw.tgt)
                    loads = [(mg, which, k) for mg in range(len(HG))
                             for which in (0, 1) for k in range(ND)]

                    def f1load(li):
                        mg, which, k = loads[li]
                        src = w1d if which == 0 else w3d
                        rows0 = 128 * HGOFF[mg]
                        nrows = 128 * HG[mg]
                        if li >= 8:
                            sy.wait_ge(f1c.h, f1c_b + li - 7)
                        if k % 16 == 0:
                            for kt in range(HGOFF[mg], HGOFF[mg] + HG[mg]):
                                h_, v_ = wtgt(uid_w(kt, which, k // 16))
                                sy.wait_ge(h_, v_)
                        fws.load(sy,
                                 bass.AP(wsl[(fw_base + li) % 8], 0, [[512, 128], [1, nrows]]),
                                 src.ap()[rows0:rows0 + nrows, 128 * k:128 * (k + 1)],
                                 transpose=True)

                    for li in range(8):
                        f1load(li)
                    te.wait_ge(xnbq_done[0], xnbq_done[1])
                    li = 0
                    silu_op = []
                    hmul_op = []
                    for mg in range(len(HG)):
                        ntile = HG[mg]
                        nb = 2 * ntile
                        for which in (0, 1):
                            for k in range(ND):
                                if li + 8 < len(loads):
                                    f1load(li + 8)
                                fws.wait(te, fw_base + li)
                                wtile = wsl[(fw_base + li) % 8]
                                for h_ in range(ntile):
                                    for qq in range(2):
                                        b_ = 2 * h_ + qq
                                        if k == 0:
                                            if which == 0 and mg >= 1:
                                                te.wait_ge(hmul.h,
                                                           hmul_op[8 * (mg - 1) + b_])
                                            if which == 1:
                                                te.wait_ge(sSil.h,
                                                           silu_op[8 * mg + b_])
                                        mm = te.matmul(
                                            fps[:, b_ * 512:(b_ + 1) * 512],
                                            wtile[:, h_ * 128:(h_ + 1) * 128],
                                            xnTq2[qq][:, k * 528 + 1:k * 528 + 513],
                                            start=(k == 0), stop=(k == ND - 1))
                                mm.then_inc(f1c.inc(), 1)
                                li += 1
                            dqfeed(7)
                            if which == 0:
                                act.wait_ge(f1c.h, f1c.n)
                                if mg >= 1:
                                    act.wait_ge(hmul.h, hmul_op[8 * (mg - 1) + 7])
                                for b_ in range(nb):
                                    act.activation(
                                        hstmp[:, b_ * 512:(b_ + 1) * 512],
                                        fps[:, b_ * 512:(b_ + 1) * 512],
                                        AF.Silu).then_inc(sSil.inc(), 1)
                                    silu_op.append(sSil.n)
                                for _pad in range(nb, 8):
                                    silu_op.append(sSil.n)
                        dve.wait_ge(f1c.h, f1c.n)
                        dve.wait_ge(sSil.h, silu_op[8 * mg + nb - 1])
                        if mg >= 2:
                            hsw.wait(dve, hsw_base + mg - 2)
                        for h_ in range(ntile):
                            for qq in range(2):
                                b_ = 2 * h_ + qq
                                if qq == 0:
                                    o_ = hT[:, (HGOFF[mg] + h_) * 512:
                                            (HGOFF[mg] + h_ + 1) * 512]
                                else:
                                    o_ = hs1[mg % 2][:, h_ * 512:(h_ + 1) * 512]
                                dve.tensor_tensor(
                                    out=o_,
                                    in0=hstmp[:, b_ * 512:(b_ + 1) * 512],
                                    in1=fps[:, b_ * 512:(b_ + 1) * 512],
                                    op=ALU.mult).then_inc(hmul.inc(), 1)
                                hmul_op.append(hmul.n)
                        for _pad in range(nb, 8):
                            hmul_op.append(hmul.n)
                        gp.wait_ge(hmul.h, hmul_op[8 * mg + nb - 1])
                        hsw.load(gp,
                                 bass.AP(hsc, HGOFF[mg] * 128 * 512,
                                         [[512, 128], [128 * 512, ntile], [1, 512]]),
                                 bass.AP(hs1[mg % 2], 0,
                                         [[2048, 128], [512, ntile], [1, 512]]))
                    hsw.wait(gp, len(hsw.tgt) - 1)
                    hsw.wait(gp, len(hsw.tgt) - 2)
                    nc.all_engine_barrier()

                for lq in range(2):
                    q = half * 2 + lq
                    hrl_base = len(hrl.tgt)
                    if lq == 1:
                        for c_ in range(8):
                            k0 = c_ * 11
                            nk = min(11, NH - k0)
                            hrl.load(gp,
                                     bass.AP(hT, k0 * 512,
                                             [[NH * 512, 128], [512, nk], [1, 512]]),
                                     bass.AP(hsc, k0 * 128 * 512,
                                             [[512, 128], [128 * 512, nk], [1, 512]]))
                    with ExitStack() as f2s:
                        wsl2 = [f2s.enter_context(sbt(f"wsl2_{i}", [128, 1024], BF16))
                                for i in range(8)]
                        accb = [f2s.enter_context(sbt(f"accb{i}", [128, 512], F32))
                                for i in range(4)]
                        asmb = [f2s.enter_context(sbt(f"asmb{i}", [128, 512], BF16))
                                for i in range(4)]

                        f2c_b = f2c.n
                        sF2_b = sF2d.n
                        f2_base = len(fws.tgt)
                        acc_base = len(accl.tgt)
                        asw_base = len(asws.tgt)

                        def f2load(li2):
                            dgp, k = divmod(li2, NH)
                            if li2 >= 8:
                                sy.wait_ge(f2c.h, f2c_b + li2 - 7)
                            ci = min(k // 16, len(W2CH) - 1)
                            if k == W2CH[ci][0]:
                                for dt in range(8 * dgp, 8 * dgp + 8):
                                    h_, v_ = wtgt(uid_2(dt, ci))
                                    sy.wait_ge(h_, v_)
                            fws.load(sy,
                                     bass.AP(wsl2[(f2_base + li2) % 8], 0,
                                             [[1024, 128], [1, 1024]]),
                                     w2d.ap()[1024 * dgp:1024 * (dgp + 1),
                                              128 * k:128 * (k + 1)],
                                     transpose=True)

                        def accload(m):
                            if m >= 4:
                                gp.wait_ge(sF2d.h, sF2_b + m - 3)
                            accl.load(gp,
                                      bass.AP(accb[(acc_base + m) % 4], 0,
                                              [[512, 128], [1, 512]]),
                                      bass.AP(accs, 128 * m * TCORE + q * 512,
                                              [[TCORE, 128], [1, 512]]))

                        for li2 in range(8):
                            f2load(li2)
                        for m in range(4):
                            accload(m)
                        li2 = 0
                        for dgp in range(4):
                            for k in range(NH):
                                if lq == 1 and dgp == 0 and k % 11 == 0:
                                    hrl.wait(te, hrl_base + k // 11)
                                if k % 3 == 0:
                                    dqfeed(1)
                                if li2 + 8 < 4 * NH:
                                    f2load(li2 + 8)
                                fws.wait(te, f2_base + li2)
                                wtile = wsl2[(f2_base + li2) % 8]
                                for j8 in range(8):
                                    if k == 0 and dgp >= 1:
                                        te.wait_ge(sF2d.h, sF2_b + 8 * (dgp - 1) + j8 + 1)
                                    mm = te.matmul(
                                        fps[:, j8 * 512:(j8 + 1) * 512],
                                        wtile[:, j8 * 128:(j8 + 1) * 128],
                                        hT[:, k * 512:(k + 1) * 512],
                                        start=(k == 0), stop=(k == NH - 1))
                                mm.then_inc(f2c.inc(), 1)
                                li2 += 1
                            for j8 in range(8):
                                m = 8 * dgp + j8
                                if m + 4 < 32:
                                    accload(m + 4)
                                dve.wait_ge(f2c.h, f2c_b + (dgp + 1) * NH)
                                accl.wait(dve, acc_base + m)
                                if m >= 4:
                                    asws.wait(dve, asw_base + m - 4)
                                dve.tensor_tensor(out=asmb[m % 4][:, :],
                                                  in0=fps[:, j8 * 512:(j8 + 1) * 512],
                                                  in1=accb[(acc_base + m) % 4][:, :],
                                                  op=ALU.add).then_inc(sF2d.inc(), 1)
                                gp.wait_ge(sF2d.h, sF2d.n)
                                asws.load(gp,
                                          bass.AP(asms, 128 * m * TCORE + q * 512,
                                                  [[TCORE, 128], [1, 512]]),
                                          asmb[m % 4][:, :])
                        for i_ in range(4):
                            asws.wait(gp, len(asws.tgt) - 1 - i_)
                        nc.all_engine_barrier()

                    with ExitStack() as asx:
                        abuf = [asx.enter_context(sbt(f"abuf{i}", [128, DM], BF16))
                                for i in range(2)]
                        xb2 = [asx.enter_context(sbt(f"xb2{i}", [128, DM], F32))
                               for i in range(2)]
                        ob = [asx.enter_context(sbt(f"ob{i}", [128, DM], F32))
                              for i in range(2)]
                        ab_base = len(abls.tgt)
                        ax_base = len(axls.tgt)
                        st_base = len(stws.tgt)
                        f3add = []
                        for j in range(4):
                            s = j % 2
                            if j >= 2:
                                sy.wait_ge(sF2d.h, f3add[j - 2])
                                sc.wait_ge(sF2d.h, f3add[j - 2])
                            abls.load(sy, abuf[s][:, :],
                                      asms.ap()[0:DM, q * 512 + 128 * j:
                                                q * 512 + 128 * (j + 1)],
                                      transpose=True)
                            axls.load(sc, xb2[s][:, :],
                                      x.ap()[1 + q * 512 + 128 * j:
                                             1 + q * 512 + 128 * (j + 1), :])
                            abls.wait(dve, ab_base + j)
                            axls.wait(dve, ax_base + j)
                            if j >= 2:
                                stws.wait(dve, st_base + j - 2)
                            dve.tensor_tensor(out=ob[s][:, :], in0=abuf[s][:, :],
                                              in1=xb2[s][:, :], op=ALU.add
                                              ).then_inc(sF2d.inc(), 1)
                            f3add.append(sF2d.n)
                            gp.wait_ge(sF2d.h, sF2d.n)
                            stws.load(gp, out.ap()[q * 512 + 128 * j:
                                                   q * 512 + 128 * (j + 1), :],
                                      ob[s][:, :])
                        stws.wait(gp, len(stws.tgt) - 1)
                        stws.wait(gp, len(stws.tgt) - 2)
                        nc.all_engine_barrier()
                    if upto == "ffn0" and q == 0:
                        return nc

    for i_ in range(2):
        stws.wait(gp, len(stws.tgt) - 1 - i_)
    est.close()
    return nc


_cache = {}


def kernel(L0_attn_inputs, L0_attn_conv_w, L0_attn_conv_b, L0_attn_wq, L0_attn_bq,
           L0_attn_wk, L0_attn_bk, L0_attn_wv, L0_attn_bv, L0_attn_wo, L0_attn_bo,
           L0_attn_proj_w, L0_attn_proj_b, L0_ffn_w1, L0_ffn_s1, L0_ffn_w2,
           L0_ffn_s2, L0_ffn_w3, L0_ffn_s3):
    xf = np.asarray(L0_attn_inputs, dtype=np.float32).reshape(16384, DM)
    zero = np.zeros((1, DM), np.float32)
    pad = np.zeros((ROWS - TCORE - 2, DM), np.float32)

    common = {
        "convw": np.ascontiguousarray(np.asarray(L0_attn_conv_w, np.float32)),
        "convb": np.asarray(L0_attn_conv_b, np.float32).reshape(1, H3),
        "wq": np.ascontiguousarray(np.asarray(L0_attn_wq, np.float32).reshape(WQK, WQK)),
        "bq": np.asarray(L0_attn_bq, np.float32).reshape(1, WQK),
        "wk": np.ascontiguousarray(np.asarray(L0_attn_wk, np.float32).reshape(WQK, WQK)),
        "bk": np.asarray(L0_attn_bk, np.float32).reshape(1, WQK),
        "wv": np.ascontiguousarray(np.asarray(L0_attn_wv, np.float32).reshape(WQK, WQK)),
        "bv": np.asarray(L0_attn_bv, np.float32).reshape(1, WQK),
        "wo": np.ascontiguousarray(np.asarray(L0_attn_wo, np.float32).reshape(WQK, WQK)),
        "bo": np.asarray(L0_attn_bo, np.float32).reshape(1, WQK),
        "pw": np.ascontiguousarray(np.asarray(L0_attn_proj_w, np.float32)),
        "pb": np.asarray(L0_attn_proj_b, np.float32).reshape(1, DM),
        "w1": np.ascontiguousarray(np.asarray(L0_ffn_w1, np.int32)),
        "s1": np.asarray(L0_ffn_s1, np.float32).reshape(1, 704512),
        "w2": np.ascontiguousarray(np.asarray(L0_ffn_w2, np.int32)),
        "s2": np.asarray(L0_ffn_s2, np.float32).reshape(1, 704512),
        "w3": np.ascontiguousarray(np.asarray(L0_ffn_w3, np.int32)),
        "s3": np.asarray(L0_ffn_s3, np.float32).reshape(1, 704512),
        "identin": np.eye(128, dtype=ml_dtypes.bfloat16),
        "maskin": np.tile(np.where(np.tril(np.ones((128, 128), bool)), 0.0,
                                   np.float32(-1e9)).astype(np.float32), (1, 4)),
    }

    in_maps = []
    for c in range(NCORE):
        lo = c * TCORE
        left = zero if lo % 8192 == 0 else xf[lo - 1:lo]
        hi = lo + TCORE
        right = zero if hi % 8192 == 0 else xf[hi:hi + 1]
        shard = np.ascontiguousarray(
            np.concatenate([left, xf[lo:hi], right, pad], axis=0))
        m = dict(common)
        m["x"] = shard
        in_maps.append(m)

    key = "prog"
    if key not in _cache:
        nc = bass.Bass("TRN2", target_bir_lowering=False)
        build(nc)
        _cache[key] = nc
    nc = _cache[key]

    res = run_bass_kernel_spmd(nc, in_maps, core_ids=list(range(NCORE)))
    outs = [res.results[c]["out"] for c in range(NCORE)]
    return np.concatenate(outs, axis=0).reshape(2, 8192, DM)




# version 10 (final)
# speedup vs baseline: 1.3037x; 1.1061x over previous
import numpy as np
import ml_dtypes
from contextlib import ExitStack

import concourse.bass as bass
import concourse.mybir as mybir
from concourse.bass_utils import run_bass_kernel_spmd

F32 = mybir.dt.float32
BF16 = mybir.dt.bfloat16
I32 = mybir.dt.int32
AF = mybir.ActivationFunctionType
ALU = mybir.AluOpType
AX = mybir.AxisListType.X

NCORE = 8
TCORE = 2048
ROWS = 2176
DM = 4096
ND = DM // 128
WQK = 1024
NW = 8
H3 = 3072
N3 = 24
HID = 11008
NH = 86
NHALF = 1024
NQ = 512

HG = [4] * 21 + [2]
HGOFF = [0]
for _s in HG:
    HGOFF.append(HGOFF[-1] + _s)
W2CH = [(0, 16), (16, 16), (32, 16), (48, 16), (64, 16), (80, 6)]

DBG = False
PH_LN = True
PH_DQ = True


class Ctr:

    def __init__(self, nc, name):
        self.h = nc.alloc_semaphore(name)
        self.n = 0

    def inc(self, k=1):
        self.n += k
        return self.h


class Slots:

    def __init__(self, nc, name, n):
        self.sems = [Ctr(nc, f"{name}{i}") for i in range(n)]
        self.n = n
        self.tgt = []

    def load(self, q, out_ap, in_ap, **kw):
        i = len(self.tgt)
        s = self.sems[i % self.n]
        q.dma_start(out_ap, in_ap, **kw).then_inc(s.inc(16), 16)
        self.tgt.append((s.h, s.n))
        return i

    def wait(self, eng, i):
        h, v = self.tgt[i]
        eng.wait_ge(h, v)


def build(nc, upto="all"):
    x = nc.dram_tensor("x", [ROWS, DM], F32, kind="ExternalInput")
    convw = nc.dram_tensor("convw", [3, DM, H3], F32, kind="ExternalInput")
    convb = nc.dram_tensor("convb", [1, H3], F32, kind="ExternalInput")
    wq = nc.dram_tensor("wq", [WQK, WQK], F32, kind="ExternalInput")
    wk = nc.dram_tensor("wk", [WQK, WQK], F32, kind="ExternalInput")
    wv = nc.dram_tensor("wv", [WQK, WQK], F32, kind="ExternalInput")
    bq = nc.dram_tensor("bq", [1, WQK], F32, kind="ExternalInput")
    bk = nc.dram_tensor("bk", [1, WQK], F32, kind="ExternalInput")
    bv = nc.dram_tensor("bv", [1, WQK], F32, kind="ExternalInput")
    wo = nc.dram_tensor("wo", [WQK, WQK], F32, kind="ExternalInput")
    bo = nc.dram_tensor("bo", [1, WQK], F32, kind="ExternalInput")
    pw = nc.dram_tensor("pw", [WQK, DM], F32, kind="ExternalInput")
    pb = nc.dram_tensor("pb", [1, DM], F32, kind="ExternalInput")
    w1 = nc.dram_tensor("w1", [352256, 64], I32, kind="ExternalInput")
    w2 = nc.dram_tensor("w2", [352256, 64], I32, kind="ExternalInput")
    w3 = nc.dram_tensor("w3", [352256, 64], I32, kind="ExternalInput")
    s1 = nc.dram_tensor("s1", [1, 704512], F32, kind="ExternalInput")
    s2 = nc.dram_tensor("s2", [1, 704512], F32, kind="ExternalInput")
    s3 = nc.dram_tensor("s3", [1, 704512], F32, kind="ExternalInput")
    identin = nc.dram_tensor("identin", [128, 128], BF16, kind="ExternalInput")
    maskin = nc.dram_tensor("maskin", [128, 512], F32, kind="ExternalInput")
    out = nc.dram_tensor("out", [TCORE, DM], F32, kind="ExternalOutput")

    scratch_kind = "ExternalOutput" if DBG else "Internal"
    xnsc = nc.dram_tensor("xnsc", [ROWS, DM], BF16, kind=scratch_kind)
    w1d = nc.dram_tensor("w1d", [HID, DM], BF16, kind=scratch_kind)
    w3d = nc.dram_tensor("w3d", [HID, DM], BF16, kind=scratch_kind)
    w2d = nc.dram_tensor("w2d", [DM, HID], BF16, kind=scratch_kind)
    accs = nc.dram_tensor("accs", [DM, TCORE], F32, kind=scratch_kind)
    asms = nc.dram_tensor("asms", [DM, TCORE], BF16, kind=scratch_kind)
    hsc = nc.dram_tensor("hsc", [HID, 512], BF16, kind=scratch_kind)

    sy = nc.sync
    sc = nc.scalar
    gp = nc.gpsimd
    dve = nc.vector
    act = nc.scalar
    pool = nc.gpsimd
    te = nc.tensor

    _uid = [0]

    def sbt(name, shape, dtype):
        _uid[0] += 1
        return nc.sbuf_tensor(f"{name}_{_uid[0]}", shape, dtype)

    def pst(name, shape, dtype):
        _uid[0] += 1
        return nc.psum_tensor(f"{name}_{_uid[0]}", shape, dtype)

    allsems = []

    def C(name):
        c = Ctr(nc, name)
        allsems.append(c)
        return c

    def SL(name, n):
        s = Slots(nc, name, n)
        allsems.extend(s.sems)
        return s

    ini = SL("ini", 1)
    xls = SL("xls", 2)
    lnv = C("lnv")
    lna = C("lna")
    lnq = C("lnq")
    lnw = SL("lnw", 2)
    pk_v = SL("pkv", 4)
    pk_p = SL("pkp", 4)
    dqv = C("dqv")
    dqp = C("dqp")
    dwr_v = SL("dwrv", 2)
    dwr_p = SL("dwrp", 2)
    xnb = SL("xnb", 1)
    cwsl = SL("cws", 2)
    cbsl = SL("cbs", 1)
    ccp = C("ccp")
    cce = C("cce")
    qwsl = SL("qws", 2)
    qkp = C("qkp")
    qke = C("qke")
    sSf = C("sSf")
    sA = C("sA"); sB = C("sB"); sCx = C("sCx"); sD = C("sD")
    sE = C("sE"); sF = C("sF"); sG = C("sG"); sH = C("sH")
    sR = C("sR")
    wop = C("wop"); woe = C("woe")
    pjp = C("pjp"); pje = C("pje")
    aws = SL("aws", 4)
    fws = SL("fws", 8)
    f1c = C("f1c")
    f2c = C("f2c")
    sSil = C("sSil"); hmul = C("hmul")
    sF2d = C("sF2d")
    accl = SL("accl", 4)
    asws = SL("asws", 4)
    hsw = SL("hsw", 2)
    hrl = SL("hrl", 8)
    abls = SL("abls", 2)
    axls = SL("axls", 2)
    stws = SL("stws", 2)

    est = ExitStack()
    ident = est.enter_context(sbt("ident", [128, 128], BF16))
    mask4 = est.enter_context(sbt("mask4", [128, 512], F32))
    ones = est.enter_context(sbt("ones", [1, 1024], BF16))

    for c in allsems:
        pool.sem_clear(c.h)
    nc.all_engine_barrier()
    nc.all_engine_barrier()

    ini.load(sy, ident[:, :], identin.ap())
    ini.load(sy, mask4[:, :], maskin.ap())
    dve.memset(ones[:, :], 1.0)

    with ExitStack() as ph:
        xbuf = [ph.enter_context(sbt(f"xbuf{i}", [128, DM], F32)) for i in range(2)]
        xsq = ph.enter_context(sbt("xsq", [128, DM], F32))
        xnt = [ph.enter_context(sbt(f"xnt{i}", [128, DM], BF16)) for i in range(2)]
        st8 = ph.enter_context(sbt("st8", [128, 8], F32))

        lna_copy = []
        lnv_sq = []
        for g in range(17):
            s = g % 2
            if g >= 2:
                sc.wait_ge(lna.h, lna_copy[g - 2])
            li = xls.load(sc, xbuf[s][:, :], x.ap()[128 * g:128 * (g + 1), :])
            xls.wait(dve, li)
            dve.reduce_sum(st8[:, 0:1], xbuf[s][:, :], axis=AX).then_inc(lnv.inc(), 1)
            dve.wait_ge(lnv.h, lnv.n)
            xls.wait(act, li)
            if g >= 1:
                act.wait_ge(lnv.h, lnv_sq[g - 1])
            act.activation(xsq[:, :], xbuf[s][:, :], AF.Square).then_inc(lnq.inc(), 1)
            dve.wait_ge(lnq.h, lnq.n)
            dve.reduce_sum(st8[:, 1:2], xsq[:, :], axis=AX).then_inc(lnv.inc(), 1)
            lnv_sq.append(lnv.n)
            dve.wait_ge(lnv.h, lnv.n)
            dve.tensor_single_scalar(out=st8[:, 2:3], in_=st8[:, 0:1],
                                     scalar=1.0 / DM, op=ALU.mult).then_inc(lnv.inc(), 1)
            dve.tensor_single_scalar(out=st8[:, 3:4], in_=st8[:, 1:2],
                                     scalar=1.0 / DM, op=ALU.mult).then_inc(lnv.inc(), 1)
            dve.wait_ge(lnv.h, lnv.n)
            dve.tensor_tensor(out=st8[:, 4:5], in0=st8[:, 2:3], in1=st8[:, 2:3],
                              op=ALU.mult).then_inc(lnv.inc(), 1)
            dve.wait_ge(lnv.h, lnv.n)
            dve.scalar_tensor_tensor(out=st8[:, 5:6], in0=st8[:, 4:5], scalar=-1.0,
                                     in1=st8[:, 3:4], op0=ALU.mult, op1=ALU.add
                                     ).then_inc(lnv.inc(), 1)
            dve.wait_ge(lnv.h, lnv.n)
            dve.tensor_single_scalar(out=st8[:, 5:6], in_=st8[:, 5:6],
                                     scalar=1e-5, op=ALU.add).then_inc(lnv.inc(), 1)
            act.wait_ge(lnv.h, lnv.n)
            act.activation(st8[:, 6:7], st8[:, 5:6], AF.Ln).then_inc(lna.inc(), 1)
            act.wait_ge(lna.h, lna.n)
            act.activation(st8[:, 6:7], st8[:, 6:7], AF.Exp, scale=-0.5
                           ).then_inc(lna.inc(), 1)
            dve.wait_ge(lna.h, lna.n)
            dve.scalar_tensor_tensor(out=st8[:, 7:8], in0=st8[:, 2:3], scalar=-1.0,
                                     in1=st8[:, 6:7], op0=ALU.mult, op1=ALU.mult
                                     ).then_inc(lnv.inc(), 1)
            act.wait_ge(lnv.h, lnv.n)
            if g >= 2:
                _, v = lnw.tgt[g - 2]
                act.wait_ge(lnw.tgt[g - 2][0], v)
            act.activation(xnt[s][:, :], xbuf[s][:, :], AF.Identity,
                           bias=st8[:, 7:8], scale=st8[:, 6:7]).then_inc(lna.inc(), 1)
            lna_copy.append(lna.n)
            gp.wait_ge(lna.h, lna.n)
            lnw.load(gp, xnsc.ap()[128 * g:128 * (g + 1), :], xnt[s][:, :])
    if upto == "ln":
        lnw.wait(gp, 15)
        lnw.wait(gp, 16)
        ini.wait(sy, 1)
        nc.all_engine_barrier()
        return nc

    units = []
    for kt in range(NH):
        for which in (0, 1):
            for dh in (0, 1):
                units.append(("w13", (kt, which, dh)))
    for dt in range(ND):
        for ci in range(len(W2CH)):
            units.append(("w2", (dt, ci)))
    NU = len(units)

    def uid_w(kt, which, dh):
        return kt * 4 + which * 2 + dh

    def uid_2(dt, ci):
        return NH * 4 + dt * len(W2CH) + ci

    def uparams(u):
        kind, idx = u
        if kind == "w13":
            kt, which, dh = idx
            return dict(src=(w1 if which == 0 else w3),
                        ssrc=(s1 if which == 0 else s3),
                        dst=(w1d if which == 0 else w3d),
                        pkoff=kt * 128 * 2048 + dh * 1024, pkw=1024,
                        ppitch=2048, mcnt=16,
                        soff=kt * 128 * 64 + 32 * dh, spp=64, scnt=32,
                        dstoff=kt * 128 * DM + dh * 2048, dpitch=DM, dw=2048)
        dt, ci = idx
        m0, mc = W2CH[ci]
        return dict(src=w2, ssrc=s2, dst=w2d,
                    pkoff=(dt * 128 * 86 + m0) * 64, pkw=mc * 64, ppitch=86 * 64,
                    mcnt=mc, soff=dt * 128 * 172 + 2 * m0, spp=172, scnt=2 * mc,
                    dstoff=dt * 128 * HID + m0 * 128, dpitch=HID, dw=mc * 128)

    def wtgt(u):
        return (dwr_v.sems[u % 2].h, 16 * (u // 2 + 1))

    pkb = [est.enter_context(sbt(f"pkb{i}", [128, 1024], I32)) for i in range(2)]
    scb = [est.enter_context(sbt(f"scqb{i}", [128, 32], F32)) for i in range(2)]
    lsbt = est.enter_context(sbt("lsbt", [128, 1024], I32))
    outb = [est.enter_context(sbt(f"dqo{i}", [128, 2048], BF16)) for i in range(2)]

    dqst = {"load": 0, "comp": 0}

    def dq_load1():
        j = dqst["load"]
        if j >= NU:
            return
        if j == 0:
            for gi_ in (15, 16):
                gp.wait_ge(lnw.tgt[gi_][0], lnw.tgt[gi_][1])
        u = uparams(units[j])
        s = j % 2
        if j >= 2:
            gp.wait_ge(dqv.h, 4 * (j - 1))
        pk_v.load(gp, bass.AP(pkb[s], 0, [[1024, 128], [1, u["pkw"]]]),
                  bass.AP(u["src"], u["pkoff"], [[u["ppitch"], 128], [1, u["pkw"]]]))
        pk_v.load(gp, bass.AP(scb[s], 0, [[32, 128], [1, u["scnt"]]]),
                  bass.AP(u["ssrc"], u["soff"], [[u["spp"], 128], [1, u["scnt"]]]))
        dqst["load"] = j + 1

    def dqfeed(n):
        for _ in range(n):
            j = dqst["comp"]
            if j >= NU:
                return
            while dqst["load"] < min(j + 2, NU):
                dq_load1()
            u = uparams(units[j])
            s = j % 2
            ob = outb[s]
            mc = u["mcnt"]
            if j == 0:
                for gi_ in (15, 16):
                    dve.wait_ge(lnw.tgt[gi_][0], lnw.tgt[gi_][1])
            pk_v.wait(dve, 2 * j)
            pk_v.wait(dve, 2 * j + 1)
            if j >= 1:
                dve.wait_ge(dqv.h, dqv.n)
            if j >= 2:
                dwr_v.wait(dve, j - 2)
            dve.tensor_single_scalar(out=lsbt[:, 0:u["pkw"]], in_=pkb[s][:, 0:u["pkw"]],
                                     scalar=4, op=ALU.arith_shift_right
                                     ).then_inc(dqv.inc(), 1)
            dve.wait_ge(dqv.h, dqv.n)
            dve.tensor_tensor(
                out=bass.AP(ob, 0, [[2048, 128], [128, mc], [1, 64]]),
                in0=bass.AP(lsbt, 0, [[1024, 128], [64, mc], [1, 64]]),
                in1=bass.AP(scb[s], 0, [[32, 128], [2, mc], [0, 64]]),
                op=ALU.mult).then_inc(dqv.inc(), 1)
            dve.wait_ge(dqv.h, dqv.n)
            dve.tensor_scalar(out=lsbt[:, 0:u["pkw"]], in0=pkb[s][:, 0:u["pkw"]],
                              scalar1=15, scalar2=8, op0=ALU.bitwise_and,
                              op1=ALU.bitwise_xor).then_inc(dqv.inc(), 1)
            dve.wait_ge(dqv.h, dqv.n)
            dve.scalar_tensor_tensor(
                out=bass.AP(ob, 64, [[2048, 128], [128, mc], [1, 64]]),
                in0=bass.AP(lsbt, 0, [[1024, 128], [64, mc], [1, 64]]),
                scalar=8.0,
                in1=bass.AP(scb[s], 1, [[32, 128], [2, mc], [0, 64]]),
                op0=ALU.subtract, op1=ALU.mult).then_inc(dqv.inc(), 1)
            sc.wait_ge(dqv.h, dqv.n)
            wl = dwr_v.load(sc, bass.AP(u["dst"], u["dstoff"],
                                        [[u["dpitch"], 128], [1, u["dw"]]]),
                            bass.AP(ob, 0, [[2048, 128], [1, u["dw"]]]))
            assert dwr_v.tgt[wl] == wtgt(j), (dwr_v.tgt[wl], wtgt(j))
            dqst["comp"] = j + 1
            while dqst["load"] < min(j + 3, NU):
                dq_load1()

    if upto == "dq":
        dqfeed(NU)
        dwr_v.wait(sc, len(dwr_v.tgt) - 1)
        dwr_v.wait(sc, len(dwr_v.tgt) - 2)
        nc.all_engine_barrier()
        return nc

    for half in range(2):
        t0 = half * NHALF
        with ExitStack() as hs:
            if half == 0:
                for gi_ in (15, 16):
                    act.wait_ge(lnw.tgt[gi_][0], lnw.tgt[gi_][1])
            with ExitStack() as atts:
                qhT = atts.enter_context(sbt("qhT", [128, NW * 1024], BF16))
                khT = atts.enter_context(sbt("khT", [128, NW * 1024], BF16))
                vh = atts.enter_context(sbt("vh", [128, 8 * 1024], BF16))

                with ExitStack() as cs:
                    qkvT = cs.enter_context(sbt("qkvT", [128, N3 * 1024], BF16))
                    cv = cs.enter_context(ExitStack())
                    xnT = cv.enter_context(sbt("xnTc", [128, ND * 1040], BF16))
                    cwb = [cv.enter_context(sbt(f"cwb{i}", [128, 32 * 128], BF16))
                           for i in range(2)]
                    cbsb = cv.enter_context(sbt("cbsb", [1, H3], BF16))
                    cps = cv.enter_context(pst("cps", [128, 4 * 512], F32))
                    glast = (t0 + 1040 - 1) // 128
                    for gi_ in (glast - 1, glast):
                        sy.wait_ge(lnw.tgt[gi_][0], lnw.tgt[gi_][1])
                    ini.wait(sy, 1)
                    for k in range(ND):
                        xnb.load(sy, bass.AP(xnT, k * 1040, [[ND * 1040, 128], [1, 1040]]),
                                 xnsc.ap()[t0:t0 + 1040, 128 * k:128 * (k + 1)],
                                 transpose=True)
                    xnb_done = xnb.tgt[-1]
                    cbsl.load(gp, cbsb[:, :], convb.ap())
                    ccp_b = ccp.n
                    cce_b = cce.n

                    def cload(c):
                        b, k3 = divmod(c, 3)
                        if c >= 2:
                            gp.wait_ge(ccp.h, ccp_b + c - 1)
                        return cwsl.load(
                            gp,
                            bass.AP(cwb[(len(cwsl.tgt)) % 2], 0,
                                    [[32 * 128, 128], [128, 32], [1, 128]]),
                            bass.AP(convw, k3 * DM * H3 + 128 * b,
                                    [[H3, 128], [128 * H3, 32], [1, 128]]))

                    cw_base = len(cwsl.tgt)
                    cload(0)
                    cload(1)
                    te.wait_ge(xnb_done[0], xnb_done[1])
                    cbsl.wait(te, len(cbsl.tgt) - 1)
                    for b in range(N3):
                        bp = b % 2
                        if b >= 2:
                            te.wait_ge(cce.h, cce_b + b - 1)
                        for k3 in range(3):
                            c = b * 3 + k3
                            if c + 2 < N3 * 3:
                                cload(c + 2)
                            cwsl.wait(te, cw_base + c)
                            sbuf_w = cwb[(cw_base + c) % 2]
                            if k3 == 0:
                                for nh in range(2):
                                    te.matmul(cps[:, (2 * bp + nh) * 512:(2 * bp + nh + 1) * 512],
                                              cbsb[:, 128 * b:128 * (b + 1)],
                                              ones[:, 0:512], start=True, stop=False)
                            for a in range(ND):
                                for nh in range(2):
                                    mm = te.matmul(
                                        cps[:, (2 * bp + nh) * 512:(2 * bp + nh + 1) * 512],
                                        sbuf_w[:, a * 128:(a + 1) * 128],
                                        xnT[:, a * 1040 + k3 + nh * 512:
                                            a * 1040 + k3 + nh * 512 + 512],
                                        start=False, stop=(k3 == 2 and a == ND - 1))
                            mm.then_inc(ccp.inc(), 1)
                        act.wait_ge(ccp.h, ccp_b + 3 * (b + 1))
                        act.activation(qkvT[:, b * 1024:(b + 1) * 1024],
                                       cps[:, 2 * bp * 512:(2 * bp + 2) * 512],
                                       AF.Copy).then_inc(cce.inc(), 1)
                        dqfeed(2)
                    nc.all_engine_barrier()
                    cv.close()

                    with ExitStack() as qs:
                        qwb = [qs.enter_context(sbt(f"qwb{i}", [128, 8 * 1024], BF16))
                               for i in range(2)]
                        bql = qs.enter_context(sbt("bql", [1, 3 * WQK], BF16))
                        qps = qs.enter_context(pst("qps", [128, 4 * 512], F32))
                        cbsl.load(gp, bql[:, 0:WQK], bq.ap())
                        cbsl.load(gp, bql[:, WQK:2 * WQK], bk.ap())
                        cbsl.load(gp, bql[:, 2 * WQK:3 * WQK], bv.ap())
                        qkp_b = qkp.n
                        qke_b = qke.n
                        qw_base = len(qwsl.tgt)

                        def qwload(i, wt):
                            if i >= 2:
                                gp.wait_ge(qkp.h, qkp_b + 8 * (i - 1))
                            qwsl.load(gp,
                                      bass.AP(qwb[(qw_base + i) % 2], 0,
                                              [[8 * 1024, 128], [1024, 8], [1, 1024]]),
                                      bass.AP(wt, 0, [[WQK, 128], [128 * WQK, 8], [1, WQK]]))

                        qwload(0, wq)
                        qwload(1, wk)
                        cbsl.wait(te, len(cbsl.tgt) - 1)
                        for i, (srcoff, dstt, scl) in enumerate(
                                [(0, qhT, 0.125), (8, khT, 1.0), (16, vh, 1.0)]):
                            if i == 2:
                                qwload(2, wv)
                            qwsl.wait(te, qw_base + i)
                            wb = qwb[(qw_base + i) % 2]
                            for m in range(8):
                                bank = (m % 2) * 2
                                if i * 8 + m >= 2:
                                    te.wait_ge(qke.h, qke_b + i * 8 + m - 1)
                                for nh in range(2):
                                    pslice = qps[:, (bank + nh) * 512:(bank + nh + 1) * 512]
                                    if i < 2:
                                        te.matmul(pslice,
                                                  bql[:, i * WQK + 128 * m:i * WQK + 128 * (m + 1)],
                                                  ones[:, 0:512], start=True, stop=False)
                                        for a in range(8):
                                            mm = te.matmul(
                                                pslice,
                                                wb[:, a * 1024 + 128 * m:a * 1024 + 128 * (m + 1)],
                                                qkvT[:, (srcoff + a) * 1024 + nh * 512:
                                                     (srcoff + a) * 1024 + nh * 512 + 512],
                                                start=False, stop=(a == 7))
                                    else:
                                        te.matmul(pslice, ones[:, 0:128],
                                                  bql[:, 2 * WQK + nh * 512:2 * WQK + (nh + 1) * 512],
                                                  start=True, stop=False)
                                        for a in range(8):
                                            mm = te.matmul(
                                                pslice,
                                                qkvT[:, (16 + a) * 1024 + m * 128:
                                                     (16 + a) * 1024 + (m + 1) * 128],
                                                wb[:, a * 1024 + nh * 512:a * 1024 + (nh + 1) * 512],
                                                start=False, stop=(a == 7))
                                mm.then_inc(qkp.inc(), 1)
                                act.wait_ge(qkp.h, qkp.n)
                                act.activation(dstt[:, m * 1024:(m + 1) * 1024],
                                               qps[:, bank * 512:(bank + 2) * 512],
                                               AF.Copy, scale=scl).then_inc(qke.inc(), 1)
                                dqfeed(2)
                        nc.all_engine_barrier()

                with ExitStack() as bs:
                    oT = bs.enter_context(sbt("oT", [128, 8 * 1024], BF16))
                    aoT = bs.enter_context(sbt("aoT", [128, 8 * 1024], BF16))
                    smb = [bs.enter_context(sbt(f"smb{i}", [128, 512], F32)) for i in range(2)]
                    pbf = [bs.enter_context(sbt(f"pbf{i}", [128, 512], BF16)) for i in range(2)]
                    pnb = [bs.enter_context(sbt(f"pnb{i}", [128, 512], BF16)) for i in range(2)]
                    pTb = [bs.enter_context(sbt(f"pTb{i}", [128, 512], BF16)) for i in range(2)]
                    nmb = bs.enter_context(sbt("nmb", [128, 16], F32))
                    Sps = [bs.enter_context(pst(f"Sps{i}", [128, 512], F32)) for i in range(2)]
                    Tps = [bs.enter_context(pst(f"Tps{i}", [128, 512], BF16)) for i in range(2)]
                    Ops = [bs.enter_context(pst(f"Ops{i}", [128, 1024], F32)) for i in range(2)]
                    sA_b = sA.n; sB_b = sB.n; sC_b = sCx.n; sD_b = sD.n
                    sE_b = sE.n; sF_b = sF.n; sG_b = sG.n; sH_b = sH.n; sSf_b = sSf.n
                    for b in range(8):
                        for g in range(4):
                            bg = b * 4 + g
                            par = bg % 2
                            if bg >= 2:
                                te.wait_ge(sSf.h, sSf_b + bg - 1)
                            sa_tgt = [0, 0]
                            for idx, hh in enumerate((0, 2, 1, 3)):
                                h = 4 * g + hh
                                i = h // 2
                                pb_ = 64 * (h % 2)
                                mm = te.matmul(
                                    Sps[par][:, hh * 128:(hh + 1) * 128],
                                    qhT[pb_:pb_ + 64, i * 1024 + 128 * b:i * 1024 + 128 * (b + 1)],
                                    khT[pb_:pb_ + 64, i * 1024 + 128 * b:i * 1024 + 128 * (b + 1)],
                                    start=True, stop=True)
                                if idx == 1:
                                    mm.then_inc(sA.inc(), 1)
                                    te.wait_ge(sA.h, sA.n)
                            mm.then_inc(sA.inc(), 1)
                            sa_grp = sA.n
                            dve.wait_ge(sA.h, sa_grp)
                            if bg >= 2:
                                dve.wait_ge(sCx.h, sC_b + bg - 1)
                            dve.tensor_tensor(out=smb[par][:, :], in0=Sps[par][:, :],
                                              in1=mask4[:, :], op=ALU.add
                                              ).then_inc(sSf.inc(), 1)
                            dve.wait_ge(sSf.h, sSf.n)
                            dve.reduce_max(bass.AP(nmb, 4 * par, [[16, 128], [1, 4]]),
                                           bass.AP(smb[par], 0, [[512, 128], [128, 4], [1, 128]]),
                                           axis=AX, negate=True).then_inc(sB.inc(), 1)
                            dve.wait_ge(sB.h, sB.n)
                            dve.tensor_tensor(
                                out=smb[par][:, :], in0=smb[par][:, :],
                                in1=bass.AP(nmb, 4 * par, [[16, 128], [1, 4], [0, 128]]),
                                op=ALU.add).then_inc(sB.inc(), 1)
                            act.wait_ge(sB.h, sB_b + 2 * bg + 2)
                            if bg >= 2:
                                act.wait_ge(sD.h, sD_b + 2 * (bg - 2) + 2)
                            act.activation(pbf[par][:, :], smb[par][:, :], AF.Exp
                                           ).then_inc(sCx.inc(), 1)
                            dve.wait_ge(sCx.h, sC_b + bg + 1)
                            dve.reduce_sum(bass.AP(nmb, 8 + 4 * par, [[16, 128], [1, 4]]),
                                           bass.AP(pbf[par], 0, [[512, 128], [128, 4], [1, 128]]),
                                           axis=AX).then_inc(sD.inc(), 1)
                            dve.wait_ge(sD.h, sD.n)
                            dve.reciprocal(bass.AP(nmb, 8 + 4 * par, [[16, 128], [1, 4]]),
                                           bass.AP(nmb, 8 + 4 * par, [[16, 128], [1, 4]])
                                           ).then_inc(sR.inc(), 1)
                            dve.wait_ge(sR.h, sR.n)
                            if bg >= 2:
                                dve.wait_ge(sE.h, sE_b + bg - 1)
                            dve.tensor_tensor(
                                out=pnb[par][:, :], in0=pbf[par][:, :],
                                in1=bass.AP(nmb, 8 + 4 * par, [[16, 128], [1, 4], [0, 128]]),
                                op=ALU.mult).then_inc(sD.inc(), 1)
                            te.wait_ge(sD.h, sD_b + 2 * bg + 2)
                            if bg >= 2:
                                te.wait_ge(sF.h, sF_b + bg - 1)
                            for hh in range(4):
                                mm = te.transpose(Tps[par][:, hh * 128:(hh + 1) * 128],
                                                  pnb[par][:, hh * 128:(hh + 1) * 128],
                                                  ident[:, :])
                            mm.then_inc(sE.inc(), 1)
                            dve.wait_ge(sE.h, sE_b + bg + 1)
                            if bg >= 2:
                                dve.wait_ge(sG.h, sG_b + 2 * (bg - 2) + 2)
                            dve.tensor_copy(pTb[par][:, :], Tps[par][:, :]).then_inc(sF.inc(), 1)
                            te.wait_ge(sF.h, sF_b + bg + 1)
                            if g == 0 and b >= 2:
                                te.wait_ge(sH.h, sH_b + b - 1)
                            for idx, hh in enumerate((0, 2, 1, 3)):
                                h = 4 * g + hh
                                i = h // 2
                                pb_ = 64 * (h % 2)
                                mm = te.matmul(
                                    Ops[b % 2][pb_:pb_ + 64, i * 128:(i + 1) * 128],
                                    vh[:, b * 1024 + h * 64:b * 1024 + (h + 1) * 64],
                                    pTb[par][:, hh * 128:(hh + 1) * 128],
                                    start=True, stop=True)
                                if idx == 1:
                                    mm.then_inc(sG.inc(), 1)
                                    te.wait_ge(sG.h, sG.n)
                            mm.then_inc(sG.inc(), 1)
                            dqfeed(1)
                        dve.wait_ge(sG.h, sG_b + 8 * (b + 1))
                        dve.tensor_copy(
                            bass.AP(oT, b * 128, [[8 * 1024, 128], [1024, 8], [1, 128]]),
                            bass.AP(Ops[b % 2], 0, [[1024, 128], [128, 8], [1, 128]]),
                        ).then_inc(sH.inc(), 1)
                    nc.all_engine_barrier()

                    def wob(i):
                        return Ops[i // 2][:, (i % 2) * 512:(i % 2 + 1) * 512]

                    with ExitStack() as ws:
                        wwb = ws.enter_context(sbt("wwb", [128, 8 * 1024], BF16))
                        bolb = ws.enter_context(sbt("bolb", [1, WQK], BF16))
                        cbsl.load(gp, bolb[:, :], bo.ap())
                        wo_li = qwsl.load(
                            gp, bass.AP(wwb, 0, [[8 * 1024, 128], [1024, 8], [1, 1024]]),
                            bass.AP(wo, 0, [[WQK, 128], [128 * WQK, 8], [1, WQK]]))
                        qwsl.wait(te, wo_li)
                        cbsl.wait(te, len(cbsl.tgt) - 1)
                        wop_b = wop.n
                        woe_b = woe.n
                        for m in range(8):
                            bank = (m % 2) * 2
                            if m >= 2:
                                te.wait_ge(woe.h, woe_b + m - 1)
                            for nh in range(2):
                                pslice = wob(bank + nh)
                                te.matmul(pslice, bolb[:, 128 * m:128 * (m + 1)],
                                          ones[:, 0:512], start=True, stop=False)
                                for a in range(8):
                                    mm = te.matmul(
                                        pslice,
                                        wwb[:, a * 1024 + 128 * m:a * 1024 + 128 * (m + 1)],
                                        oT[:, a * 1024 + nh * 512:a * 1024 + (nh + 1) * 512],
                                        start=False, stop=(a == 7))
                            mm.then_inc(wop.inc(), 1)
                            act.wait_ge(wop.h, wop.n)
                            act.activation(aoT[:, m * 1024:(m + 1) * 1024],
                                           Ops[bank // 2][:, :],
                                           AF.Copy).then_inc(woe.inc(), 1)
                            dqfeed(3)
                        nc.all_engine_barrier()

                    with ExitStack() as ps_:
                        pwb = [ps_.enter_context(sbt(f"pwb{i}", [128, 8 * 128], BF16))
                               for i in range(2)]
                        pblb = ps_.enter_context(sbt("pblb", [1, DM], BF16))
                        fbx = [ps_.enter_context(sbt(f"fbx{i}", [128, 512], F32))
                               for i in range(4)]
                        cbsl.load(gp, pblb[:, :], pb.ap())
                        pjp_b = pjp.n
                        pje_b = pje.n
                        pw_base = len(qwsl.tgt)

                        def pwload(m):
                            if m >= 2:
                                gp.wait_ge(pjp.h, pjp_b + m - 1)
                            qwsl.load(gp,
                                      bass.AP(pwb[(pw_base + m) % 2], 0,
                                              [[8 * 128, 128], [128, 8], [1, 128]]),
                                      bass.AP(pw, 128 * m, [[DM, 128], [128 * DM, 8], [1, 128]]))

                        pwload(0)
                        pwload(1)
                        cbsl.wait(te, len(cbsl.tgt) - 1)
                        aw_base = len(aws.tgt)
                        for m in range(32):
                            bank = (m % 2) * 2
                            if m + 2 < 32:
                                pwload(m + 2)
                            qwsl.wait(te, pw_base + m)
                            if m >= 2:
                                te.wait_ge(pje.h, pje_b + 2 * (m - 2) + 2)
                            wbm = pwb[(pw_base + m) % 2]
                            for nh in range(2):
                                pslice = wob(bank + nh)
                                te.matmul(pslice, pblb[:, 128 * m:128 * (m + 1)],
                                          ones[:, 0:512], start=True, stop=False)
                                for a in range(8):
                                    mm = te.matmul(
                                        pslice, wbm[:, a * 128:(a + 1) * 128],
                                        aoT[:, a * 1024 + nh * 512:a * 1024 + (nh + 1) * 512],
                                        start=False, stop=(a == 7))
                            mm.then_inc(pjp.inc(), 1)
                            act.wait_ge(pjp.h, pjp.n)
                            for nh in range(2):
                                fi = 2 * (m % 2) + nh
                                if m >= 2:
                                    aws.wait(act, aw_base + 2 * (m - 2) + nh)
                                act.activation(fbx[fi][:, :], wob(bank + nh),
                                               AF.Copy).then_inc(pje.inc(), 1)
                                gp.wait_ge(pje.h, pje.n)
                                aws.load(gp,
                                         bass.AP(accs, 128 * m * TCORE + t0 + nh * 512,
                                                 [[TCORE, 128], [1, 512]]),
                                         fbx[fi][:, :])
                            dqfeed(1)
                        for i_ in range(4):
                            aws.wait(gp, len(aws.tgt) - 1 - i_)
                        nc.all_engine_barrier()
            if upto == "attn" and half == 0:
                return nc

            with ExitStack() as fhs:
                hT = fhs.enter_context(sbt("hT", [128, NH * 512], BF16))
                fps = fhs.enter_context(pst("fps", [128, 8 * 512], F32))

                with ExitStack() as f1s:
                    xnTq2 = [f1s.enter_context(sbt(f"xnTq{qq}", [128, ND * 528], BF16))
                             for qq in range(2)]
                    wsl = [f1s.enter_context(sbt(f"wsl{i}", [128, 512], BF16))
                           for i in range(8)]
                    hstmp = f1s.enter_context(sbt("hst", [128, 8 * 512], BF16))
                    hs1 = [f1s.enter_context(sbt(f"hs1_{i}", [128, 4 * 512], BF16))
                           for i in range(2)]

                    if half == 0:
                        for gi_ in (15, 16):
                            sy.wait_ge(lnw.tgt[gi_][0], lnw.tgt[gi_][1])
                            sc.wait_ge(lnw.tgt[gi_][0], lnw.tgt[gi_][1])
                    for lq in range(2):
                        q = half * 2 + lq
                        glast = (q * 512 + 528 - 1) // 128
                        for gi_ in (glast - 1, glast):
                            sy.wait_ge(lnw.tgt[gi_][0], lnw.tgt[gi_][1])
                        for k in range(ND):
                            xnb.load(sy,
                                     bass.AP(xnTq2[lq], k * 528, [[ND * 528, 128], [1, 528]]),
                                     xnsc.ap()[q * 512:q * 512 + 528, 128 * k:128 * (k + 1)],
                                     transpose=True)
                    xnbq_done = xnb.tgt[-1]

                    f1c_b = f1c.n
                    fw_base = len(fws.tgt)
                    hsw_base = len(hsw.tgt)
                    loads = [(mg, which, k) for mg in range(len(HG))
                             for which in (0, 1) for k in range(ND)]

                    def f1load(li):
                        mg, which, k = loads[li]
                        src = w1d if which == 0 else w3d
                        rows0 = 128 * HGOFF[mg]
                        nrows = 128 * HG[mg]
                        if li >= 8:
                            sy.wait_ge(f1c.h, f1c_b + li - 7)
                        if k % 16 == 0:
                            for kt in range(HGOFF[mg], HGOFF[mg] + HG[mg]):
                                h_, v_ = wtgt(uid_w(kt, which, k // 16))
                                sy.wait_ge(h_, v_)
                        fws.load(sy,
                                 bass.AP(wsl[(fw_base + li) % 8], 0, [[512, 128], [1, nrows]]),
                                 src.ap()[rows0:rows0 + nrows, 128 * k:128 * (k + 1)],
                                 transpose=True)

                    for li in range(8):
                        f1load(li)
                    te.wait_ge(xnbq_done[0], xnbq_done[1])
                    li = 0
                    silu_op = []
                    hmul_op = []
                    for mg in range(len(HG)):
                        ntile = HG[mg]
                        nb = 2 * ntile
                        for which in (0, 1):
                            for k in range(ND):
                                if li + 8 < len(loads):
                                    f1load(li + 8)
                                fws.wait(te, fw_base + li)
                                wtile = wsl[(fw_base + li) % 8]
                                for h_ in range(ntile):
                                    for qq in range(2):
                                        b_ = 2 * h_ + qq
                                        if k == 0:
                                            if which == 0 and mg >= 1:
                                                te.wait_ge(hmul.h,
                                                           hmul_op[8 * (mg - 1) + b_])
                                            if which == 1:
                                                te.wait_ge(sSil.h,
                                                           silu_op[8 * mg + b_])
                                        mm = te.matmul(
                                            fps[:, b_ * 512:(b_ + 1) * 512],
                                            wtile[:, h_ * 128:(h_ + 1) * 128],
                                            xnTq2[qq][:, k * 528 + 1:k * 528 + 513],
                                            start=(k == 0), stop=(k == ND - 1))
                                mm.then_inc(f1c.inc(), 1)
                                li += 1
                            dqfeed(6)
                            if which == 0:
                                act.wait_ge(f1c.h, f1c.n)
                                if mg >= 1:
                                    act.wait_ge(hmul.h, hmul_op[8 * (mg - 1) + 7])
                                for b_ in range(nb):
                                    act.activation(
                                        hstmp[:, b_ * 512:(b_ + 1) * 512],
                                        fps[:, b_ * 512:(b_ + 1) * 512],
                                        AF.Silu).then_inc(sSil.inc(), 1)
                                    silu_op.append(sSil.n)
                                for _pad in range(nb, 8):
                                    silu_op.append(sSil.n)
                        dve.wait_ge(f1c.h, f1c.n)
                        dve.wait_ge(sSil.h, silu_op[8 * mg + nb - 1])
                        if mg >= 2:
                            hsw.wait(dve, hsw_base + mg - 2)
                        for h_ in range(ntile):
                            for qq in range(2):
                                b_ = 2 * h_ + qq
                                if qq == 0:
                                    o_ = hT[:, (HGOFF[mg] + h_) * 512:
                                            (HGOFF[mg] + h_ + 1) * 512]
                                else:
                                    o_ = hs1[mg % 2][:, h_ * 512:(h_ + 1) * 512]
                                dve.tensor_tensor(
                                    out=o_,
                                    in0=hstmp[:, b_ * 512:(b_ + 1) * 512],
                                    in1=fps[:, b_ * 512:(b_ + 1) * 512],
                                    op=ALU.mult).then_inc(hmul.inc(), 1)
                                hmul_op.append(hmul.n)
                        for _pad in range(nb, 8):
                            hmul_op.append(hmul.n)
                        gp.wait_ge(hmul.h, hmul_op[8 * mg + nb - 1])
                        hsw.load(gp,
                                 bass.AP(hsc, HGOFF[mg] * 128 * 512,
                                         [[512, 128], [128 * 512, ntile], [1, 512]]),
                                 bass.AP(hs1[mg % 2], 0,
                                         [[2048, 128], [512, ntile], [1, 512]]))
                    hsw.wait(gp, len(hsw.tgt) - 1)
                    hsw.wait(gp, len(hsw.tgt) - 2)
                    nc.all_engine_barrier()

                with ExitStack() as f23:
                    wsl2 = [f23.enter_context(sbt(f"wsl2_{i}", [128, 1024], BF16))
                            for i in range(8)]
                    accb = [f23.enter_context(sbt(f"accb{i}", [128, 512], F32))
                            for i in range(4)]
                    asmb = [f23.enter_context(sbt(f"asmb{i}", [128, 512], BF16))
                            for i in range(4)]
                    abuf = [f23.enter_context(sbt(f"abuf{i}", [128, DM // 2], BF16))
                            for i in range(2)]
                    xb2 = [f23.enter_context(sbt(f"xb2{i}", [128, DM // 2], F32))
                           for i in range(2)]
                    ob = [f23.enter_context(sbt(f"ob{i}", [128, DM // 2], F32))
                          for i in range(2)]
                    f3prev = None
                    for lq in range(2):
                        q = half * 2 + lq
                        hrl_base = len(hrl.tgt)
                        if lq == 1:
                            gp.wait_ge(f2c.h, f2c.n)
                            for c_ in range(8):
                                k0 = c_ * 11
                                nk = min(11, NH - k0)
                                hrl.load(gp,
                                         bass.AP(hT, k0 * 512,
                                                 [[NH * 512, 128], [512, nk], [1, 512]]),
                                         bass.AP(hsc, k0 * 128 * 512,
                                                 [[512, 128], [128 * 512, nk], [1, 512]]))

                        f2c_b = f2c.n
                        sF2_b = sF2d.n
                        f2_base = len(fws.tgt)
                        acc_base = len(accl.tgt)
                        asw_base = len(asws.tgt)

                        def f2load(li2):
                            dgp, k = divmod(li2, NH)
                            if f2c_b + li2 - 7 > 0:
                                sy.wait_ge(f2c.h, f2c_b + li2 - 7)
                            ci = min(k // 16, len(W2CH) - 1)
                            if k == W2CH[ci][0]:
                                for dt in range(8 * dgp, 8 * dgp + 8):
                                    h_, v_ = wtgt(uid_2(dt, ci))
                                    sy.wait_ge(h_, v_)
                            fws.load(sy,
                                     bass.AP(wsl2[(f2_base + li2) % 8], 0,
                                             [[1024, 128], [1, 1024]]),
                                     w2d.ap()[1024 * dgp:1024 * (dgp + 1),
                                              128 * k:128 * (k + 1)],
                                     transpose=True)

                        def accload(m):
                            v_ = sF2_b + (m - 3 if m >= 4 else m - 7)
                            if v_ > 0:
                                gp.wait_ge(sF2d.h, v_)
                            accl.load(gp,
                                      bass.AP(accb[(acc_base + m) % 4], 0,
                                              [[512, 128], [1, 512]]),
                                      bass.AP(accs, 128 * m * TCORE + q * 512,
                                              [[TCORE, 128], [1, 512]]))

                        for li2 in range(8):
                            f2load(li2)
                        for m in range(4):
                            accload(m)
                        li2 = 0
                        for dgp in range(4):
                            for k in range(NH):
                                if lq == 1 and dgp == 0 and k % 11 == 0:
                                    hrl.wait(te, hrl_base + k // 11)
                                if k % 4 == 0:
                                    dqfeed(1)
                                if li2 + 8 < 4 * NH:
                                    f2load(li2 + 8)
                                fws.wait(te, f2_base + li2)
                                wtile = wsl2[(f2_base + li2) % 8]
                                for j8 in range(8):
                                    if k == 0:
                                        v_ = sF2_b + (8 * (dgp - 1) + j8 + 1
                                                      if dgp >= 1 else j8 - 11)
                                        if v_ > 0:
                                            te.wait_ge(sF2d.h, v_)
                                    mm = te.matmul(
                                        fps[:, j8 * 512:(j8 + 1) * 512],
                                        wtile[:, j8 * 128:(j8 + 1) * 128],
                                        hT[:, k * 512:(k + 1) * 512],
                                        start=(k == 0), stop=(k == NH - 1))
                                mm.then_inc(f2c.inc(), 1)
                                li2 += 1
                            for j8 in range(8):
                                m = 8 * dgp + j8
                                if m + 4 < 32:
                                    accload(m + 4)
                                dve.wait_ge(f2c.h, f2c_b + (dgp + 1) * NH)
                                accl.wait(dve, acc_base + m)
                                if asw_base + m - 4 >= 0:
                                    asws.wait(dve, asw_base + m - 4)
                                dve.tensor_tensor(out=asmb[m % 4][:, :],
                                                  in0=fps[:, j8 * 512:(j8 + 1) * 512],
                                                  in1=accb[(acc_base + m) % 4][:, :],
                                                  op=ALU.add).then_inc(sF2d.inc(), 1)
                                gp.wait_ge(sF2d.h, sF2d.n)
                                asws.load(gp,
                                          bass.AP(asms, 128 * m * TCORE + q * 512,
                                                  [[TCORE, 128], [1, 512]]),
                                          asmb[m % 4][:, :])
                        for i_ in range(4):
                            asws.wait(gp, len(asws.tgt) - 1 - i_)

                        for i_ in range(4):
                            asws.wait(sy, len(asws.tgt) - 1 - i_)
                        ab_base = len(abls.tgt)
                        ax_base = len(axls.tgt)
                        st_base = len(stws.tgt)
                        f3add = []
                        for j in range(8):
                            s = j % 2
                            tb = j // 2
                            ch = (j % 2) * (DM // 2)
                            if j >= 2:
                                sy.wait_ge(sF2d.h, f3add[j - 2])
                                sc.wait_ge(sF2d.h, f3add[j - 2])
                            elif f3prev is not None:
                                sy.wait_ge(sF2d.h, f3prev[j + 6])
                                sc.wait_ge(sF2d.h, f3prev[j + 6])
                            abls.load(sy, abuf[s][:, :],
                                      asms.ap()[ch:ch + DM // 2,
                                                q * 512 + 128 * tb:
                                                q * 512 + 128 * (tb + 1)],
                                      transpose=True)
                            axls.load(sc, xb2[s][:, :],
                                      x.ap()[1 + q * 512 + 128 * tb:
                                             2 + q * 512 + 128 * tb + 127,
                                             ch:ch + DM // 2])
                            abls.wait(dve, ab_base + j)
                            axls.wait(dve, ax_base + j)
                            if st_base + j - 2 >= 0:
                                stws.wait(dve, st_base + j - 2)
                            dve.tensor_tensor(out=ob[s][:, :], in0=abuf[s][:, :],
                                              in1=xb2[s][:, :], op=ALU.add
                                              ).then_inc(sF2d.inc(), 1)
                            f3add.append(sF2d.n)
                            gp.wait_ge(sF2d.h, sF2d.n)
                            stws.load(gp, out.ap()[q * 512 + 128 * tb:
                                                   q * 512 + 128 * (tb + 1),
                                                   ch:ch + DM // 2],
                                      ob[s][:, :])
                        f3prev = f3add
                        if lq == 1:
                            stws.wait(gp, len(stws.tgt) - 1)
                            stws.wait(gp, len(stws.tgt) - 2)
                    nc.all_engine_barrier()
                if upto == "ffn0":
                    return nc

    for i_ in range(2):
        stws.wait(gp, len(stws.tgt) - 1 - i_)
    est.close()
    return nc


_cache = {}


def kernel(L0_attn_inputs, L0_attn_conv_w, L0_attn_conv_b, L0_attn_wq, L0_attn_bq,
           L0_attn_wk, L0_attn_bk, L0_attn_wv, L0_attn_bv, L0_attn_wo, L0_attn_bo,
           L0_attn_proj_w, L0_attn_proj_b, L0_ffn_w1, L0_ffn_s1, L0_ffn_w2,
           L0_ffn_s2, L0_ffn_w3, L0_ffn_s3):
    xf = np.asarray(L0_attn_inputs, dtype=np.float32).reshape(16384, DM)
    zero = np.zeros((1, DM), np.float32)
    pad = np.zeros((ROWS - TCORE - 2, DM), np.float32)

    common = {
        "convw": np.ascontiguousarray(np.asarray(L0_attn_conv_w, np.float32)),
        "convb": np.asarray(L0_attn_conv_b, np.float32).reshape(1, H3),
        "wq": np.ascontiguousarray(np.asarray(L0_attn_wq, np.float32).reshape(WQK, WQK)),
        "bq": np.asarray(L0_attn_bq, np.float32).reshape(1, WQK),
        "wk": np.ascontiguousarray(np.asarray(L0_attn_wk, np.float32).reshape(WQK, WQK)),
        "bk": np.asarray(L0_attn_bk, np.float32).reshape(1, WQK),
        "wv": np.ascontiguousarray(np.asarray(L0_attn_wv, np.float32).reshape(WQK, WQK)),
        "bv": np.asarray(L0_attn_bv, np.float32).reshape(1, WQK),
        "wo": np.ascontiguousarray(np.asarray(L0_attn_wo, np.float32).reshape(WQK, WQK)),
        "bo": np.asarray(L0_attn_bo, np.float32).reshape(1, WQK),
        "pw": np.ascontiguousarray(np.asarray(L0_attn_proj_w, np.float32)),
        "pb": np.asarray(L0_attn_proj_b, np.float32).reshape(1, DM),
        "w1": np.ascontiguousarray(np.asarray(L0_ffn_w1, np.int32)),
        "s1": np.asarray(L0_ffn_s1, np.float32).reshape(1, 704512),
        "w2": np.ascontiguousarray(np.asarray(L0_ffn_w2, np.int32)),
        "s2": np.asarray(L0_ffn_s2, np.float32).reshape(1, 704512),
        "w3": np.ascontiguousarray(np.asarray(L0_ffn_w3, np.int32)),
        "s3": np.asarray(L0_ffn_s3, np.float32).reshape(1, 704512),
        "identin": np.eye(128, dtype=ml_dtypes.bfloat16),
        "maskin": np.tile(np.where(np.tril(np.ones((128, 128), bool)), 0.0,
                                   np.float32(-1e9)).astype(np.float32), (1, 4)),
    }

    in_maps = []
    for c in range(NCORE):
        lo = c * TCORE
        left = zero if lo % 8192 == 0 else xf[lo - 1:lo]
        hi = lo + TCORE
        right = zero if hi % 8192 == 0 else xf[hi:hi + 1]
        shard = np.ascontiguousarray(
            np.concatenate([left, xf[lo:hi], right, pad], axis=0))
        m = dict(common)
        m["x"] = shard
        in_maps.append(m)

    key = "prog"
    if key not in _cache:
        nc = bass.Bass("TRN2", target_bir_lowering=False)
        build(nc)
        _cache[key] = nc
    nc = _cache[key]

    res = run_bass_kernel_spmd(nc, in_maps, core_ids=list(range(NCORE)))
    outs = [res.results[c]["out"] for c in range(NCORE)]
    return np.concatenate(outs, axis=0).reshape(2, 8192, DM)


